# version 15
# baseline (speedup 1.0000x reference)
import numpy as np
from contextlib import ExitStack

import concourse.bass as bass
import concourse.mybir as mybir
from concourse.bass_utils import run_bass_kernel_spmd

F32 = mybir.dt.float32
AF = mybir.ActivationFunctionType
ALU = mybir.AluOpType
AXX = mybir.AxisListType.X

NCORES = 8
DM = 2048
NH = 32
HD = 64
NHP = NH // 2
CONVD = 6144
PROJ = 8224
FFI = 8192
CH = 128
TT = 512
CPT = TT // CH
NKD = DM // 128

ENGS = ("pe", "dve", "act", "pool", "sp")


class Buf:
    __slots__ = ("name", "w", "rs", "pending")

    def __init__(self, name):
        self.name = name
        self.w = None
        self.rs = []
        self.pending = set()


NRING = 8


class Op:
    __slots__ = ("eng", "fn", "deps", "dma", "need_inc", "cum", "qj", "sidx")

    def __init__(self, eng, fn, deps, dma):
        self.eng = eng
        self.fn = fn
        self.deps = deps
        self.dma = dma
        self.need_inc = False
        self.cum = 0
        self.qj = 0
        self.sidx = 0


class Sched:
    def __init__(self):
        self.ops = {e: [] for e in ENGS}
        self.all = []
        self.ring = {}
        self.qcnt = {}

    def op(self, eng, fn, r=(), w=(), deps=(), dma=False):
        ds = set()
        for b in r:
            if b.w is not None:
                ds.add(b.w)
            if b.pending:
                ds.update(b.pending)
                b.pending = set()
        for b in w:
            if b.w is not None:
                ds.add(b.w)
            ds.update(b.rs)
            if b.pending:
                ds.update(b.pending)
                b.pending = set()
        for dd in deps:
            if dd is not None:
                ds.add(dd)
        best = {}
        for dd in ds:
            key = (dd.eng, dd.qj, dd.dma)
            cur = best.get(key)
            if cur is None or dd.sidx > cur.sidx:
                best[key] = dd
        ds = set(best.values())
        o = Op(eng, fn, ds, dma)
        o.sidx = len(self.ops[eng])
        if dma:
            idx = self.qcnt.get(eng, 0)
            self.qcnt[eng] = idx + 1
            j = idx % NRING
            prev = self.ring.get((eng, j))
            if prev is not None:
                ds.add(prev)
            self.ring[(eng, j)] = o
            o.qj = j
            o.cum = 16 * (idx // NRING + 1)
            o.need_inc = True
        self.ops[eng].append(o)
        self.all.append(o)
        for b in r:
            b.rs.append(o)
        for b in w:
            b.w = o
            b.rs = []
        return o

    @staticmethod
    def retag(olds, news):
        ds = set()
        for b in olds:
            if b.w is not None:
                ds.add(b.w)
            ds.update(b.rs)
            ds.update(b.pending)
        for b in news:
            b.w = None
            b.rs = []
            b.pending = set(ds)

    def finalize(self, blk, sems, qsems, tail_waits):
        for o in self.all:
            for dd in o.deps:
                dd.need_inc = True
        for e in ENGS:
            c_cum = 0
            for o in self.ops[e]:
                if o.need_inc and not o.dma:
                    c_cum += 1
                    o.cum = c_cum
            assert c_cum < 60000, (e, c_cum)
        for (e, j), o in self.ring.items():
            assert o.cum < 60000

        def key_sem(dd):
            if dd.dma:
                return ("q", dd.eng, dd.qj), qsems[dd.eng][dd.qj]
            return dd.eng, sems[dd.eng]

        def emit_stream(e, eng):
            hw = {}
            for o in self.ops[e]:
                tgt = {}
                for dd in o.deps:
                    key, sem = key_sem(dd)
                    if tgt.get(key, (0, None))[0] < dd.cum:
                        tgt[key] = (dd.cum, sem)
                needed = [(v, sem) for key, (v, sem) in tgt.items()
                          if hw.get(key, 0) < v]
                for key, (v, sem) in tgt.items():
                    if hw.get(key, 0) < v:
                        hw[key] = v
                for v, sem in needed[:-1]:
                    eng.wait_ge(sem, v)
                inst = o.fn(eng)
                if needed:
                    inst._wait_ge(needed[-1][1], needed[-1][0])
                if o.need_inc:
                    if o.dma:
                        inst.then_inc(qsems[o.eng][o.qj], 16)
                    else:
                        inst.then_inc(sems[o.eng], 1)
            if e == "pool":
                for dd in tail_waits:
                    key, sem = key_sem(dd)
                    if hw.get(key, 0) < dd.cum:
                        hw[key] = dd.cum
                        eng.wait_ge(sem, dd.cum)

        blk.tensor(lambda eng: emit_stream("pe", eng))
        blk.vector(lambda eng: emit_stream("dve", eng))
        blk.scalar(lambda eng: emit_stream("act", eng))
        blk.gpsimd(lambda eng: emit_stream("pool", eng))
        blk.sync(lambda eng: emit_stream("sp", eng))


def build_program(n_tiles=4):
    TOK = n_tiles * TT
    nc = bass.Bass("TRN2", target_bir_lowering=False,
                   dynamic_dma_scratch_size=2048)

    d = {}
    for nm, shp in [("u_own", [TOK, DM]), ("u_halo", [256, DM]),
                    ("flag", [128, 1]), ("w_int", [DM, PROJ]),
                    ("w_outt", [DM, DM]), ("w_gate", [DM, FFI]),
                    ("w_up", [DM, FFI]), ("w_down", [FFI, DM]),
                    ("conv_w", [128, 192]), ("conv_b", [128, 48]),
                    ("z_bias", [DM]), ("rms_w", [DM]), ("dvec", [NH]),
                    ("ident", [128, 128]), ("negtriu", [128, 128]),
                    ("triladd", [128, 128]), ("sel127", [128, 1]),
                    ("ones_row", [1, 128])]:
        d[nm] = nc.dram_tensor(nm, shp, F32, kind="ExternalInput")
    d["out"] = nc.dram_tensor("out", [TOK, DM], F32, kind="ExternalOutput")

    es = ExitStack()
    sb = lambda name, shape: es.enter_context(nc.sbuf_tensor(name, shape, F32))

    xnf = sb("xnf", [128, NKD * TT])
    ytok = sb("ytok", [128, CPT * DM])
    gacc = sb("gacc", [128, CPT * DM])
    slabA = sb("slabA", [128, NKD * 512])
    slabB = sb("slabB", [128, NKD * 512])
    wbuf = sb("wbuf", [128, 2 * 2048])
    hsx = sb("hsx", [128, 2048])
    tmp1 = sb("tmp1", [128, DM])
    tmp2 = sb("tmp2", [128, DM])
    tmp3 = sb("tmp3", [128, DM])
    ident = sb("identc", [128, 128])
    negtriu = sb("negtriuc", [128, 128])
    triladd = sb("triladdc", [128, 128])
    sel127 = sb("sel127c", [128, 1])
    ones = sb("onesc", [1, 128])
    dbc = sb("dbcc", [128, NH])
    convw = sb("convwc", [128, 48 * 4])
    convb = sb("convbc", [128, 48])
    flagv = sb("flagc", [128, 1])
    wa = sb("wac", [128, NKD * NH])
    tails = sb("tailsc", [128, 48 * 3])
    S = sb("Sc", [128, NHP * HD])
    stats = sb("statsc", [128, 8])
    cbias = sb("cbiasc", [128, 2])
    acum = slabB[:, 6168:6424]
    acumf = slabB[0:32, 6424:6936]
    alast = slabB[0:1, 6936:7064]
    dsb = slabB[:, 7064:7192]
    dcb = slabB[:, 7192:7320]
    expa = slabB[:, 7320:7448]
    arow = slabB[0:1, 7448:7576]

    ps = [es.enter_context(nc.psum_tensor(f"ps{i}", [128, 512], F32))
          for i in range(8)]
    sem = {e: es.enter_context(nc.semaphore(f"sem_{e}")) for e in ENGS}
    qsem = {e: [es.enter_context(nc.semaphore(f"qsem_{e}{j}"))
                for j in range(8)]
            for e in ("sp", "act", "pool")}

    S_ = Sched()

    b_const = Buf("const")
    b_wa = Buf("wa")
    b_tails = [Buf(f"tl{i}") for i in range(48)]
    b_S = [Buf(f"S{h}") for h in range(NH)]
    b_acum = [Buf(f"ac{c}") for c in range(CPT)]
    b_acumf = [Buf(f"af{c}") for c in range(CPT)]
    b_alast = [Buf(f"al{c}") for c in range(CPT)]
    b_ds = [Buf(f"ds{c}") for c in range(CPT)]
    b_dcb = [Buf(f"dc{c}") for c in range(CPT)]
    b_expa = [Buf(f"ea{c}") for c in range(CPT)]
    b_arow = Buf("arow")
    b_stats = Buf("stats")
    b_xn = [Buf(f"xn{c}") for c in range(CPT)]
    b_xnh = Buf("xnh")
    b_lnus = [Buf("lnu0"), Buf("lnu1")]
    b_lnx = Buf("lnx")
    b_ynf = [Buf(f"ynf{c}") for c in range(CPT)]
    b_y = [Buf(f"y{c}") for c in range(CPT)]
    b_stk = [Buf(f"stk{c}") for c in range(CPT)]
    b_g = [Buf(f"g{c}") for c in range(CPT)]
    b_slA = Buf("slA")
    b_slA2 = [Buf("slAh0"), Buf("slAh1")]
    b_slB = Buf("slB")
    b_pb = [[Buf(f"pb{s}{i}") for i in range(3)] for s in range(2)]
    b_xbc = [[Buf(f"xb{s}{i}") for i in range(3)] for s in range(2)]
    b_zf = [Buf(f"zf{c}") for c in range(CPT)]
    b_wb = [Buf("wb0"), Buf("wb1")]
    b_hst = Buf("hst")
    b_hs = [Buf("hs0"), Buf("hs1")]
    b_sst = Buf("sst")
    b_zb = Buf("zb")
    b_rmw = Buf("rmw")
    b_t1 = Buf("t1")
    b_t2 = Buf("t2")
    b_t3 = Buf("t3")
    b_ps = [Buf(f"p{i}") for i in range(8)]

    def dma(q, dst_ap, src_ap, r=(), w=(), deps=()):
        return S_.op(q, lambda e, a=dst_ap, b2=src_ap: e.dma_start(a, b2),
                     r=r, w=w, deps=deps, dma=True)

    XT2 = hsx[:, 0:128]
    BT2 = hsx[:, 128:256]
    MSK = hsx[:, 256:384]
    MTm = hsx[:, 384:512]
    BTS = hsx[:, 512:640]
    O1 = hsx[:, 640:704]
    E1 = hsx[:, 704:768]
    SPA = hsx[:, 768:800]
    SPB = hsx[:, 800:832]

    dma("sp", ident[:, :], d["ident"].ap(), w=[b_const])
    dma("sp", negtriu[:, :], d["negtriu"].ap(), w=[b_const])
    dma("sp", triladd[:, :], d["triladd"].ap(), w=[b_const])
    dma("sp", sel127[:, :], d["sel127"].ap(), w=[b_const])
    dma("sp", ones[:, :], d["ones_row"].ap(), w=[b_const])
    dma("sp", flagv[:, :], d["flag"].ap(), w=[b_const])
    dma("sp", convw[:, :], d["conv_w"].ap(), w=[b_const])
    dma("sp", convb[:, :], d["conv_b"].ap(), w=[b_const])
    dma("sp", wa.ap().rearrange("p (k f) -> p k f", k=NKD),
        d["w_int"].ap().rearrange("(k p) e -> p k e", p=128)[:, :, 8192:8224],
        w=[b_wa])
    dma("sp", tmp2[0:1, 0:NH],
        d["dvec"].ap().rearrange("(o n) -> o n", o=1), w=[b_t2])
    S_.op("dve", lambda e: e.memset(cbias[:, 0:1], 1e-5), w=[b_const])
    S_.op("dve", lambda e: e.memset(cbias[:, 1:2], 1.0), w=[b_const])
    S_.op("pe", lambda e: e.matmul(ps[2][0:128, 0:NH], ones[0:1, :],
                                   tmp2[0:1, 0:NH], start=True, stop=True),
          r=[b_const, b_t2], w=[b_ps[2]])
    S_.op("dve", lambda e: e.tensor_copy(dbc[:, :], ps[2][0:128, 0:NH]),
          r=[b_ps[2]], w=[b_const])

    lnus = (slabB[:, 0:2048], slabB[:, 2048:4096])
    lnx = hsx[:, 0:2048]

    def ln_load(c, u_src_ap):
        lnu = lnus[c % 2]
        lnub = b_lnus[c % 2]
        dma("act", lnu, u_src_ap, w=[lnub])

    def ln_chunk(c, u_src_ap, xn_out_fn, preloaded=False):
        lnu = lnus[c % 2]
        lnub = b_lnus[c % 2]
        if not preloaded:
            dma("act", lnu, u_src_ap, w=[lnub])
        S_.op("dve", lambda e, lnu=lnu: e.tensor_reduce(
            stats[:, 0:1], lnu, AXX, ALU.add), r=[lnub], w=[b_stats])
        S_.op("act", lambda e, lnu=lnu: e.activation(
            lnx, lnu, AF.Square, accum_out=stats[:, 1:2]),
            r=[lnub], w=[b_lnx, b_stats])
        S_.op("act", lambda e: e.activation(
            stats[:, 2:3], stats[:, 0:1], AF.Copy, scale=1.0 / DM),
            r=[b_stats], w=[b_stats])
        S_.op("act", lambda e: e.activation(
            stats[:, 3:4], stats[:, 2:3], AF.Square), r=[b_stats], w=[b_stats])
        S_.op("dve", lambda e: e.scalar_tensor_tensor(
            stats[:, 4:5], stats[:, 1:2], 1.0 / DM, stats[:, 3:4],
            ALU.mult, ALU.subtract), r=[b_stats], w=[b_stats])
        S_.op("dve", lambda e: e.tensor_scalar(
            stats[:, 7:8], stats[:, 4:5], 1e-5, None, ALU.add),
            r=[b_stats], w=[b_stats])
        S_.op("act", lambda e: e.activation(
            stats[:, 5:6], stats[:, 7:8], AF.Sqrt),
            r=[b_stats], w=[b_stats])
        S_.op("dve", lambda e: e.reciprocal(stats[:, 6:7], stats[:, 5:6]),
              r=[b_stats], w=[b_stats])
        for k in range(NKD):
            pk = 6 + (k % 2)
            sc = lnx[:, 128 * (k % 2):128 * (k % 2) + 128]
            S_.op("dve", lambda e, k=k, sc=sc, lnu=lnu, lnub=lnub:
                  e.tensor_scalar(
                      sc, lnu[:, 128 * k:128 * (k + 1)], stats[:, 2:3],
                      stats[:, 6:7], ALU.subtract, ALU.mult),
                  r=[lnub, b_stats], w=[b_lnx])
            S_.op("pe", lambda e, sc=sc, pk=pk: e.transpose(
                ps[pk][0:128, 0:128], sc, ident[:, :]),
                r=[b_lnx, b_const, b_ps[pk]], w=[b_ps[pk]])
            dst, wb = xn_out_fn(k)
            if k % 2 == 0:
                S_.op("act", lambda e, dst=dst, pk=pk: e.copy(
                    dst, ps[pk][0:128, 0:128]), r=[b_ps[pk]], w=[wb])
            else:
                S_.op("dve", lambda e, dst=dst, pk=pk: e.tensor_copy(
                    dst, ps[pk][0:128, 0:128]), r=[b_ps[pk]], w=[wb])

    def a_prep(c, xn_tile_fn, xn_deps):
        for k in range(NKD):
            S_.op("pe", lambda e, k=k: e.matmul(
                ps[5][0:128, 0:NH], xn_tile_fn(k), wa[:, 32 * k:32 * (k + 1)],
                start=(k == 0), stop=(k == NKD - 1)),
                r=list(xn_deps) + [b_wa],
                w=[b_ps[5]] if k in (0, NKD - 1) else ())
        S_.op("act", lambda e: e.activation(SPA, ps[5][0:128, 0:NH], AF.Exp),
              r=[b_ps[5]], w=[b_sst])
        S_.op("act", lambda e: e.activation(SPB, SPA, AF.Ln,
                                            bias=cbias[:, 1:2]),
              r=[b_sst, b_const], w=[b_sst])
        S_.op("pe", lambda e: e.matmul(
            ps[5][0:128, 0:NH], negtriu[:, :], SPB, start=True, stop=True),
            r=[b_const, b_sst, b_ps[5]], w=[b_ps[5]])
        S_.op("dve", lambda e, c=c: e.tensor_copy(
            acum[:, 64 * c + 32:64 * c + 64], ps[5][0:128, 0:NH]),
            r=[b_ps[5]], w=[b_acum[c]])
        S_.op("pe", lambda e, c=c: e.transpose(
            ps[6][0:64, 0:128], acum[:, 64 * c:64 * c + 64], ident[:, :]),
            r=[b_acum[c], b_const, b_ps[6]], w=[b_ps[6]])
        S_.op("dve", lambda e, c=c: e.tensor_copy(
            acumf[0:32, 128 * c:128 * (c + 1)], ps[6][32:64, 0:128]),
            r=[b_ps[6]], w=[b_acumf[c]])
        S_.op("pe", lambda e, c=c: e.matmul(
            ps[7][0:1, 0:NH], sel127[:, :], acum[:, 64 * c + 32:64 * c + 64],
            start=True, stop=True),
            r=[b_const, b_acum[c], b_ps[7]], w=[b_ps[7]])
        S_.op("dve", lambda e, c=c: e.tensor_copy(
            alast[0:1, 32 * c:32 * (c + 1)], ps[7][0:1, 0:NH]),
            r=[b_ps[7]], w=[b_alast[c]])
        S_.op("pe", lambda e, c=c: e.matmul(
            ps[7][0:128, 0:NH], ones[0:1, :], alast[0:1, 32 * c:32 * (c + 1)],
            start=True, stop=True),
            r=[b_const, b_alast[c], b_ps[7]], w=[b_ps[7]])
        S_.op("dve", lambda e, c=c: e.tensor_tensor(
            dsb[:, 32 * c:32 * (c + 1)], ps[7][0:128, 0:NH],
            acum[:, 64 * c + 32:64 * c + 64], ALU.subtract),
            r=[b_ps[7], b_acum[c]], w=[b_ds[c]])
        S_.op("act", lambda e, c=c: e.activation(
            dsb[:, 32 * c:32 * (c + 1)], dsb[:, 32 * c:32 * (c + 1)], AF.Exp),
            r=[b_ds[c]], w=[b_ds[c]])
        S_.op("act", lambda e, c=c: e.activation(
            dcb[:, 32 * c:32 * (c + 1)], ps[7][0:128, 0:NH], AF.Exp),
            r=[b_ps[7]], w=[b_dcb[c]])
        S_.op("act", lambda e, c=c: e.activation(
            expa[:, 32 * c:32 * (c + 1)], acum[:, 64 * c + 32:64 * c + 64],
            AF.Exp), r=[b_acum[c]], w=[b_expa[c]])

    wbv = wbuf.ap().rearrange("p (b q) -> p b q", b=2)

    def load_wblk(bi, e0, width=128):
        src = d["w_int"].ap().rearrange("(k p) e -> p k e", p=128)
        dst = wbuf.ap().rearrange("p (b k f) -> p b k f", b=2, k=NKD)
        return dma("sp", dst[:, bi, :, 0:width], src[:, :, e0:e0 + width],
                   w=[b_wb[bi]])

    def inproj_block(bi, pbank, toks_n, xn_tile_fn, xn_deps):
        wv = wbuf.ap().rearrange("p (b k f) -> p b k f", b=2, k=NKD)
        for k in range(NKD):
            S_.op("pe", lambda e, k=k: e.matmul(
                ps[pbank][0:128, 0:toks_n], wv[:, bi, k, 0:128],
                xn_tile_fn(k), start=(k == 0), stop=(k == NKD - 1)),
                r=list(xn_deps) + [b_wb[bi]],
                w=[b_ps[pbank]] if k in (0, NKD - 1) else ())

    def PBO(st, which):
        return 3084 * st + 516 * which

    def XBO(st, which):
        return 3084 * st + 1548 + 512 * which

    def conv_chain(st, which, gb, n_tok, off_in=0, off_out=0):
        pbs = slabB[:, PBO(st, which):PBO(st, which) + 516]
        xo = slabB[:, XBO(st, which):XBO(st, which) + 512]
        S_.op("dve", lambda e: e.tensor_scalar(
            xo[:, off_out:off_out + n_tok], pbs[:, off_in:off_in + n_tok],
            convw[:, 4 * gb:4 * gb + 1], None, ALU.mult),
            r=[b_pb[st][which], b_const], w=[b_xbc[st][which]])
        for wt in (1, 2, 3):
            S_.op("dve", lambda e, wt=wt: e.scalar_tensor_tensor(
                xo[:, off_out:off_out + n_tok],
                pbs[:, off_in + wt:off_in + wt + n_tok],
                convw[:, 4 * gb + wt:4 * gb + wt + 1],
                xo[:, off_out:off_out + n_tok], ALU.mult, ALU.add),
                r=[b_pb[st][which], b_const, b_xbc[st][which]],
                w=[b_xbc[st][which]])
        S_.op("act", lambda e: e.activation(
            xo[:, off_out:off_out + n_tok], xo[:, off_out:off_out + n_tok],
            AF.Silu, bias=convb[:, gb:gb + 1]),
            r=[b_xbc[st][which], b_const], w=[b_xbc[st][which]])

    def mask_stages(hp, c, mset):
        h0, h1 = 2 * hp, 2 * hp + 1
        msks = (MSK, MSK2) if mset == 0 else (MSKB, MSK2B)
        stages = []
        for hl, h in ((0, h0), (1, h1)):
            arw = arow if hl == 0 else arow2
            abf = b_arow if hl == 0 else b_arow2
            msk = msks[hl]

            def sA(h=h, c=c, arw=arw, abf=abf):
                S_.op("pe", lambda e, h=h, c=c: e.matmul(
                    ps[7][0:1, 0:128], ident[0:32, h:h + 1],
                    acumf[0:32, 128 * c:128 * (c + 1)], start=True, stop=True),
                    r=[b_const, b_acumf[c], b_ps[7]], w=[b_ps[7]])
                S_.op("dve", lambda e, arw=arw: e.tensor_copy(
                    arw[0:1, :], ps[7][0:1, 0:128]), r=[b_ps[7]], w=[abf])

            def sB(h=h, c=c, arw=arw, abf=abf, msk=msk):
                S_.op("pe", lambda e, arw=arw: e.matmul(
                    ps[7][0:128, 0:128], ones[0:1, :], arw[0:1, :],
                    start=True, stop=True),
                    r=[b_const, abf, b_ps[7]], w=[b_ps[7]])
                S_.op("dve", lambda e, h=h, c=c, msk=msk: e.scalar_tensor_tensor(
                    msk, ps[7][0:128, 0:128],
                    acum[:, 64 * c + 32 + h:64 * c + 32 + h + 1],
                    triladd[:, :], ALU.subtract, ALU.add),
                    r=[b_ps[7], b_acum[c], b_const], w=[b_sst])
                S_.op("act", lambda e, msk=msk: e.activation(msk, msk, AF.Exp),
                      r=[b_sst], w=[b_sst])

            stages += [sA, sB]
        return stages

    def ssd_chunk(hp, c, x_off, st=0, mset=0, mops=()):
        mops = list(mops)

        def mpop():
            if mops:
                mops.pop(0)()
        h0, h1 = 2 * hp, 2 * hp + 1
        msks = (MSK, MSK2) if mset == 0 else (MSKB, MSK2B)
        xa = slabB[:, XBO(st, 0) + x_off:XBO(st, 0) + x_off + 128]
        ba = slabB[:, XBO(st, 1) + x_off:XBO(st, 1) + x_off + 128]
        ca = slabB[:, XBO(st, 2) + x_off:XBO(st, 2) + x_off + 128]
        S_.op("pe", lambda e: e.transpose(ps[2][0:128, 0:128], xa, ident[:, :]),
              r=[b_xbc[st][0], b_const, b_ps[2]], w=[b_ps[2]])
        S_.op("dve", lambda e: e.tensor_copy(XT2, ps[2][0:128, 0:128]),
              r=[b_ps[2]], w=[b_sst])
        S_.op("pe", lambda e: e.transpose(ps[6][0:128, 0:128], ba, ident[:, :]),
              r=[b_xbc[st][1], b_const, b_ps[6]], w=[b_ps[6]])
        S_.op("dve", lambda e: e.tensor_copy(BT2, ps[6][0:128, 0:128]),
              r=[b_ps[6]], w=[b_sst])
        mpop()
        for hl, h in ((0, h0), (1, h1)):
            hh = 64 * hl
            scol = S[hh:hh + 64, HD * hp:HD * hp + 64]
            msk = msks[hl]
            S_.op("pe", lambda e, hh=hh: e.matmul(
                ps[3][0:128, 0:128], ba[hh:hh + 64, :], ca[hh:hh + 64, :],
                start=True, stop=True),
                r=[b_xbc[st][1], b_xbc[st][2], b_ps[3]], w=[b_ps[3]])
            mpop()
            S_.op("dve", lambda e, msk=msk: e.tensor_tensor(
                MTm, ps[3][0:128, 0:128], msk, ALU.mult),
                r=[b_ps[3], b_sst], w=[b_sst])
            S_.op("pe", lambda e, hh=hh, scol=scol, h=h: e.matmul(
                ps[4][0:128, 0:64], ca[hh:hh + 64, :], scol,
                start=True, stop=True),
                r=[b_xbc[st][2], b_S[h], b_ps[4]], w=[b_ps[4]])
            if hl == 0:
                mpop()
            S_.op("dve", lambda e, h=h, c=c: e.tensor_scalar(
                O1, ps[4][0:128, 0:64],
                expa[:, 32 * c + h:32 * c + h + 1], None, ALU.mult),
                r=[b_ps[4], b_expa[c]], w=[b_sst])
            S_.op("dve", lambda e, h=h, hh=hh: e.scalar_tensor_tensor(
                E1, XT2[:, hh:hh + 64], dbc[:, h:h + 1], O1,
                ALU.mult, ALU.add), r=[b_sst, b_const], w=[b_sst])
            S_.op("pe", lambda e, hh=hh: e.matmul(
                ps[5][0:128, 0:64], MTm, XT2[:, hh:hh + 64],
                start=True, stop=True), r=[b_sst, b_ps[5]], w=[b_ps[5]])
            yslice = ytok[:, DM * c + 64 * h:DM * c + 64 * h + 64]
            S_.op("dve", lambda e, yslice=yslice: e.tensor_tensor(
                yslice, ps[5][0:128, 0:64], E1, ALU.add),
                r=[b_ps[5], b_sst], w=[b_y[c]])
        while mops:
            mpop()
        for hl, h in ((0, h0), (1, h1)):
            hh = 64 * hl
            S_.op("dve", lambda e, hh=hh, h=h, c=c: e.tensor_scalar(
                BTS[:, hh:hh + 64], BT2[:, hh:hh + 64],
                dsb[:, 32 * c + h:32 * c + h + 1], None, ALU.mult),
                r=[b_sst, b_ds[c]], w=[b_sst])
        S_.op("pe", lambda e: e.matmul(
            ps[6][0:128, 0:128], BTS, XT2, start=True, stop=True),
            r=[b_sst, b_ps[6]], w=[b_ps[6]])
        for hl, h in ((0, h0), (1, h1)):
            hh = 64 * hl
            scol = S[hh:hh + 64, HD * hp:HD * hp + 64]
            pview = ps[6][hh:hh + 64, hh:hh + 64]
            S_.op("dve", lambda e, scol=scol, hh=hh, h=h, c=c:
                  e.tensor_scalar(
                      ssd_tmp[hh:hh + 64, 0:64], scol,
                      dcb[hh:hh + 64, 32 * c + h:32 * c + h + 1],
                      None, ALU.mult),
                  r=[b_S[h], b_dcb[c]], w=[b_sst2])
            S_.op("dve", lambda e, pview=pview, scol=scol, hh=hh:
                  e.tensor_tensor(scol, pview, ssd_tmp[hh:hh + 64, 0:64],
                                  ALU.add),
                  r=[b_ps[6], b_sst2], w=[b_S[h]])

    ssd_tmp = hsx[:, 832:896]
    MSK2 = hsx[:, 896:1024]
    MSKB = hsx[:, 1024:1152]
    MSK2B = hsx[:, 1152:1280]
    arow2 = slabB[0:1, 7576:7704]
    b_sst2 = Buf("sst2")
    b_arow2 = Buf("arow2")
    hpb = (tmp1[:, 1048:1568], tmp2[:, 1048:1568])
    hxb = (hsx[:, 1280:1408], hsx[:, 1408:1536])
    hsc = (hsx[:, 1024:1152], hsx[:, 1152:1280])
    hXT = hsx[:, 1536:1664]
    hBT = hsx[:, 1664:1792]
    hBTS = hsx[:, 1792:1920]
    hacum = hsx[:, 1920:1952]
    hds = hsx[:, 1952:1984]
    halast = hsx[0:1, 1984:2016]
    hspa = tmp3[:, 0:32]
    hspb = tmp3[:, 32:64]
    b_hxn = Buf("hxn")
    b_hsc = Buf("hsc")
    b_hpb = [Buf("hpb0"), Buf("hpb1")]
    b_hxb = [Buf("hxb0"), Buf("hxb1")]
    b_hssd = Buf("hssd")
    b_hac = Buf("hac")
    b_hds = Buf("hds")
    b_hal = Buf("hal")

    def hxn_tile(k, lo, hi):
        base = tmp2 if k < 8 else tmp1
        kk = k % 8
        return base[:, 131 * kk + (lo - 125):131 * kk + (hi - 125)]

    def halo_ln_aprep():
        sink = slabB[:, 0:2048]
        sinkw = [b_pb[0][0], b_pb[0][1], b_pb[0][2], b_xbc[0][0]]
        for hc in range(2):
            lo = 125 if hc == 0 else 128
            dma("act", tmp3[:, :],
                d["u_halo"].ap()[128 * hc:128 * (hc + 1), :], w=[b_t3])
            S_.op("dve", lambda e: e.tensor_reduce(
                stats[:, 0:1], tmp3[:, :], AXX, ALU.add),
                r=[b_t3], w=[b_stats])
            S_.op("act", lambda e: e.activation(
                sink, tmp3[:, :], AF.Square, accum_out=stats[:, 1:2]),
                r=[b_t3], w=sinkw + [b_stats])
            S_.op("act", lambda e: e.activation(
                stats[:, 2:3], stats[:, 0:1], AF.Copy, scale=1.0 / DM),
                r=[b_stats], w=[b_stats])
            S_.op("act", lambda e: e.activation(
                stats[:, 3:4], stats[:, 2:3], AF.Square),
                r=[b_stats], w=[b_stats])
            S_.op("dve", lambda e: e.scalar_tensor_tensor(
                stats[:, 4:5], stats[:, 1:2], 1.0 / DM, stats[:, 3:4],
                ALU.mult, ALU.subtract), r=[b_stats], w=[b_stats])
            S_.op("dve", lambda e: e.tensor_scalar(
                stats[:, 7:8], stats[:, 4:5], 1e-5, None, ALU.add),
                r=[b_stats], w=[b_stats])
            S_.op("act", lambda e: e.activation(
                stats[:, 5:6], stats[:, 7:8], AF.Sqrt),
                r=[b_stats], w=[b_stats])
            S_.op("dve", lambda e: e.reciprocal(stats[:, 6:7], stats[:, 5:6]),
                  r=[b_stats], w=[b_stats])
            for k in range(NKD):
                sc = hsc[k % 2]
                pk = 6 + (k % 2)
                S_.op("dve", lambda e, k=k, sc=sc: e.tensor_scalar(
                    sc, tmp3[:, 128 * k:128 * (k + 1)], stats[:, 2:3],
                    stats[:, 6:7], ALU.subtract, ALU.mult),
                    r=[b_t3, b_stats], w=[b_hsc])
                S_.op("pe", lambda e, sc=sc, pk=pk: e.transpose(
                    ps[pk][0:128, 0:128], sc, ident[:, :]),
                    r=[b_hsc, b_const, b_ps[pk]], w=[b_ps[pk]])
                dst = hxn_tile(k, lo, 128 * (hc + 1))
                srcc = ps[pk][0:128, (lo - 128 * hc):128]
                if k % 2 == 0:
                    S_.op("act", lambda e, dst=dst, srcc=srcc: e.copy(
                        dst, srcc), r=[b_ps[pk]], w=[b_hxn])
                else:
                    S_.op("dve", lambda e, dst=dst, srcc=srcc: e.tensor_copy(
                        dst, srcc), r=[b_ps[pk]], w=[b_hxn])
        for k in range(NKD):
            S_.op("pe", lambda e, k=k: e.matmul(
                ps[5][0:128, 0:NH], hxn_tile(k, 128, 256),
                wa[:, 32 * k:32 * (k + 1)],
                start=(k == 0), stop=(k == NKD - 1)),
                r=[b_hxn, b_wa],
                w=[b_ps[5]] if k in (0, NKD - 1) else ())
        S_.op("act", lambda e: e.activation(hspa, ps[5][0:128, 0:NH], AF.Exp),
              r=[b_ps[5], b_t3], w=[b_t3])
        S_.op("act", lambda e: e.activation(hspb, hspa, AF.Ln,
                                            bias=cbias[:, 1:2]),
              r=[b_t3, b_const], w=[b_t3])
        S_.op("pe", lambda e: e.matmul(
            ps[5][0:128, 0:NH], negtriu[:, :], hspb, start=True, stop=True),
            r=[b_const, b_t3, b_ps[5]], w=[b_ps[5]])
        S_.op("dve", lambda e: e.tensor_copy(hacum, ps[5][0:128, 0:NH]),
              r=[b_ps[5]], w=[b_hac])
        S_.op("pe", lambda e: e.matmul(
            ps[7][0:1, 0:NH], sel127[:, :], hacum, start=True, stop=True),
            r=[b_const, b_hac, b_ps[7]], w=[b_ps[7]])
        S_.op("dve", lambda e: e.tensor_copy(halast, ps[7][0:1, 0:NH]),
              r=[b_ps[7]], w=[b_hal])
        S_.op("pe", lambda e: e.matmul(
            ps[7][0:128, 0:NH], ones[0:1, :], halast, start=True, stop=True),
            r=[b_const, b_hal, b_ps[7]], w=[b_ps[7]])
        S_.op("dve", lambda e: e.tensor_tensor(
            hds, ps[7][0:128, 0:NH], hacum, ALU.subtract),
            r=[b_ps[7], b_hac], w=[b_hds])
        S_.op("act", lambda e: e.activation(hds, hds, AF.Exp),
              r=[b_hds], w=[b_hds])

    def halo_proj(hp, which, bi):
        wv = wbuf.ap().rearrange("p (b k f) -> p b k f", b=2, k=NKD)
        gb = 16 * which + hp
        if which < 2:
            for k in range(NKD):
                S_.op("pe", lambda e, k=k: e.matmul(
                    ps[5][0:128, 0:131], wv[:, bi, k, 0:128],
                    hxn_tile(k, 125, 256),
                    start=(k == 0), stop=(k == NKD - 1)),
                    r=[b_hxn, b_wb[bi]],
                    w=[b_ps[5]] if k in (0, NKD - 1) else ())
            pbs = hpb[which]
            xo = hxb[which]
            S_.op("act", lambda e, pbs=pbs: e.copy(
                pbs[:, 0:131], ps[5][0:128, 0:131]),
                r=[b_ps[5]], w=[b_hpb[which]])
            S_.op("dve", lambda e, pbs=pbs, xo=xo, gb=gb: e.tensor_scalar(
                xo[:, 0:128], pbs[:, 0:128], convw[:, 4 * gb:4 * gb + 1],
                None, ALU.mult), r=[b_hpb[which], b_const], w=[b_hxb[which]])
            for wt in (1, 2, 3):
                S_.op("dve", lambda e, pbs=pbs, xo=xo, gb=gb, wt=wt:
                      e.scalar_tensor_tensor(
                          xo[:, 0:128], pbs[:, wt:wt + 128],
                          convw[:, 4 * gb + wt:4 * gb + wt + 1], xo[:, 0:128],
                          ALU.mult, ALU.add),
                      r=[b_hpb[which], b_const, b_hxb[which]],
                      w=[b_hxb[which]])
            S_.op("act", lambda e, xo=xo, gb=gb: e.activation(
                xo[:, 0:128], xo[:, 0:128], AF.Silu,
                bias=convb[:, gb:gb + 1]),
                r=[b_hxb[which], b_const], w=[b_hxb[which]])
            S_.op("dve", lambda e, pbs=pbs, gb=gb: e.tensor_copy(
                tails[:, 3 * gb:3 * gb + 3], pbs[:, 128:131]),
                r=[b_hpb[which]], w=[b_tails[gb]])
        else:
            for k in range(NKD):
                S_.op("pe", lambda e, k=k: e.matmul(
                    ps[5][0:128, 0:3], wv[:, bi, k, 0:128],
                    hxn_tile(k, 253, 256),
                    start=(k == 0), stop=(k == NKD - 1)),
                    r=[b_hxn, b_wb[bi]],
                    w=[b_ps[5]] if k in (0, NKD - 1) else ())
            S_.op("dve", lambda e, gb=gb: e.tensor_copy(
                tails[:, 3 * gb:3 * gb + 3], ps[5][0:128, 0:3]),
                r=[b_ps[5]], w=[b_tails[gb]])

    def halo_state(hp):
        h0, h1 = 2 * hp, 2 * hp + 1
        S_.op("pe", lambda e: e.transpose(
            ps[5][0:128, 0:128], hxb[0], ident[:, :]),
            r=[b_hxb[0], b_const, b_ps[5]], w=[b_ps[5]])
        S_.op("dve", lambda e: e.tensor_copy(hXT, ps[5][0:128, 0:128]),
              r=[b_ps[5]], w=[b_hssd])
        S_.op("pe", lambda e: e.transpose(
            ps[5][0:128, 0:128], hxb[1], ident[:, :]),
            r=[b_hxb[1], b_const, b_ps[5]], w=[b_ps[5]])
        S_.op("dve", lambda e: e.tensor_copy(hBT, ps[5][0:128, 0:128]),
              r=[b_ps[5]], w=[b_hssd])
        for hl, h in ((0, h0), (1, h1)):
            hh = 64 * hl
            S_.op("dve", lambda e, hh=hh, h=h: e.tensor_scalar(
                hBTS[:, hh:hh + 64], hBT[:, hh:hh + 64],
                hds[:, h:h + 1], None, ALU.mult),
                r=[b_hssd, b_hds], w=[b_hssd])
        S_.op("pe", lambda e: e.matmul(
            ps[5][0:128, 0:128], hBTS, hXT, start=True, stop=True),
            r=[b_hssd, b_ps[5]], w=[b_ps[5]])
        for hl, h in ((0, h0), (1, h1)):
            hh = 64 * hl
            scol = S[hh:hh + 64, HD * hp:HD * hp + 64]
            S_.op("dve", lambda e, hh=hh, scol=scol: e.tensor_scalar(
                scol, ps[5][hh:hh + 64, hh:hh + 64], flagv[hh:hh + 64, 0:1],
                None, ALU.mult), r=[b_ps[5], b_const], w=[b_S[h]])

    out_dmas = []

    def emit_ln_chunk(t, c, preloaded=False):
        S_.retag([b_ynf[c]], [b_xn[c]])
        ln_chunk(c,
                 d["u_own"].ap()[TT * t + 128 * c:TT * t + 128 * (c + 1), :],
                 lambda k, c=c: (
                     xnf[:, TT * k + 128 * c:TT * k + 128 * c + 128],
                     b_xn[c]), preloaded=preloaded)

    def emit_ln(t):
        for c in range(CPT):
            emit_ln_chunk(t, c)

    S_.retag([b_pb[0][i] for i in range(3)] + [b_xbc[0][i] for i in range(3)],
             b_lnus)
    S_.retag([b_sst, b_sst2], [b_lnx])
    emit_ln(0)
    for t in range(n_tiles):

        S_.retag([b_zf[c] for c in range(CPT)] + b_lnus
                 + [b_pb[s][i] for s in range(2) for i in range(3)]
                 + [b_xbc[s][i] for s in range(2) for i in range(3)]
                 + b_acum + b_acumf + b_alast
                 + b_ds + b_dcb + b_expa + [b_arow, b_arow2], [b_slB])
        S_.retag([b_sst, b_sst2, b_zb, b_rmw, b_lnx], [b_hst])
        S_.retag([b_slA2[0], b_slA2[1]], [b_slA])
        S_.retag([b_wb[0], b_wb[1]], [b_wb[0], b_wb[1]])
        slBv = slabB.ap().rearrange("p (k f) -> p k f", k=NKD)
        slAv = slabA.ap().rearrange("p (k f) -> p k f", k=NKD)

        def emit_hs_transpose(th, hs, hsbuf):
            for ic in range(4):
                S_.op("pe", lambda e, ic=ic, hs=hs: e.matmul(
                    ps[2][0:128, 128 * ic:128 * (ic + 1)],
                    hs[:, 128 * ic:128 * (ic + 1)], ident[:, :],
                    start=(ic == 0), stop=(ic == 3), is_transpose=True),
                    r=[hsbuf, b_const],
                    w=[b_ps[2]] if ic in (0, 3) else ())
            S_.op("act", lambda e, th=th: e.copy(
                hsx[:, 512 * th:512 * (th + 1)], ps[2][0:128, 0:512]),
                r=[b_ps[2]], w=[b_hst])

        for isl in range(16):
            pend_tr = []
            dma("sp", slAv[:, :, :],
                d["w_gate"].ap().rearrange("(k p) e -> p k e", p=128)
                [:, :, 512 * isl:512 * (isl + 1)], w=[b_slA])
            dma("sp", slBv[:, :, :],
                d["w_up"].ap().rearrange("(k p) e -> p k e", p=128)
                [:, :, 512 * isl:512 * (isl + 1)], w=[b_slB])
            for th in range(CPT):
                for k in range(NKD):
                    S_.op("pe", lambda e, k=k, th=th: e.matmul(
                        ps[0][0:128, 0:512],
                        xnf[:, TT * k + 128 * th:TT * k + 128 * (th + 1)],
                        slabA[:, 512 * k:512 * (k + 1)],
                        start=(k == 0), stop=(k == NKD - 1)),
                        r=[b_xn[th], b_slA],
                        w=[b_ps[0]] if k in (0, NKD - 1) else ())
                S_.op("act", lambda e: e.activation(
                    tmp1[:, 0:512], ps[0][0:128, 0:512], AF.Silu),
                    r=[b_ps[0]], w=[b_t1])
                for k in range(NKD):
                    S_.op("pe", lambda e, k=k, th=th: e.matmul(
                        ps[1][0:128, 0:512],
                        xnf[:, TT * k + 128 * th:TT * k + 128 * (th + 1)],
                        slabB[:, 512 * k:512 * (k + 1)],
                        start=(k == 0), stop=(k == NKD - 1)),
                        r=[b_xn[th], b_slB],
                        w=[b_ps[1]] if k in (0, NKD - 1) else ())
                S_.op("dve", lambda e: e.tensor_tensor(
                    tmp1[:, 512:1024], ps[1][0:128, 0:512], tmp1[:, 0:512],
                    ALU.mult), r=[b_ps[1], b_t1], w=[b_t1])
                hv = tmp1[:, 512:1024]
                habs = tmp1[:, 1536:2048]
                hs = (tmp1[:, 1024:1536] if th % 2 == 0
                      else tmp2[:, 1536:2048])
                S_.op("act", lambda e, hv=hv, habs=habs: e.activation(
                    habs, hv, AF.Abs), r=[b_t1], w=[b_t1])
                S_.op("dve", lambda e, habs=habs: e.tensor_tensor(
                    tmp2[:, 0:128], habs[:, 0::4], habs[:, 1::4], ALU.max),
                    r=[b_t1], w=[b_t2])
                S_.op("dve", lambda e, habs=habs: e.tensor_tensor(
                    tmp2[:, 128:256], habs[:, 0::4], habs[:, 1::4], ALU.min),
                    r=[b_t1], w=[b_t2])
                S_.op("dve", lambda e, habs=habs: e.tensor_tensor(
                    tmp2[:, 256:384], habs[:, 2::4], habs[:, 3::4], ALU.max),
                    r=[b_t1], w=[b_t2])
                S_.op("dve", lambda e, habs=habs: e.tensor_tensor(
                    tmp2[:, 384:512], habs[:, 2::4], habs[:, 3::4], ALU.min),
                    r=[b_t1], w=[b_t2])
                S_.op("dve", lambda e: e.tensor_tensor(
                    tmp2[:, 512:640], tmp2[:, 0:128], tmp2[:, 256:384],
                    ALU.min), r=[b_t2], w=[b_t2])
                S_.op("dve", lambda e: e.tensor_tensor(
                    tmp2[:, 640:768], tmp2[:, 128:256], tmp2[:, 384:512],
                    ALU.max), r=[b_t2], w=[b_t2])
                S_.op("dve", lambda e: e.tensor_tensor(
                    tmp2[:, 768:896], tmp2[:, 512:640], tmp2[:, 640:768],
                    ALU.max), r=[b_t2], w=[b_t2])
                for kk in range(4):
                    S_.op("dve", lambda e, kk=kk, habs=habs: e.tensor_tensor(
                        tmp2[:, 1024:1536][:, kk::4],
                        habs[:, kk::4], tmp2[:, 768:896], ALU.is_ge),
                        r=[b_t1, b_t2], w=[b_t2])
                hsbuf = b_hs[th % 2]
                S_.op("dve", lambda e, hv=hv, hs=hs: e.tensor_tensor(
                    hs, hv, tmp2[:, 1024:1536], ALU.mult),
                    r=[b_t1, b_t2], w=[hsbuf])
                pend_tr.append((th, hs, hsbuf))
                if th > 0:
                    emit_hs_transpose(*pend_tr.pop(0))
            for hd in range(4):
                wi = hd % 2
                wdv = wbuf.ap().rearrange("p (b k f) -> p b k f", b=2, k=4)
                dma("sp", wdv[:, wi, :, :],
                    d["w_down"].ap().rearrange("(k p) e -> p k e", p=128)
                    [:, 4 * isl:4 * isl + 4, 512 * hd:512 * (hd + 1)],
                    w=[b_wb[wi]])
                for th in range(CPT):
                    if hd == 0 and th == 1 and pend_tr:
                        emit_hs_transpose(*pend_tr.pop(0))
                    pbk = 3 + (th % 2)
                    for ic in range(4):
                        S_.op("pe", lambda e, ic=ic, th=th, wi=wi, pbk=pbk,
                              wdv=wdv: e.matmul(
                                  ps[pbk][0:128, 0:512],
                                  hsx[:, 512 * th + 128 * ic:
                                      512 * th + 128 * (ic + 1)],
                                  wdv[:, wi, ic, :],
                                  start=(ic == 0), stop=(ic == 3)),
                              r=[b_hst, b_wb[wi]],
                              w=[b_ps[pbk]] if ic in (0, 3) else ())
                    gsl = gacc[:, DM * th + 512 * hd:DM * th + 512 * (hd + 1)]
                    if isl == 0:
                        S_.op("dve", lambda e, gsl=gsl, pbk=pbk: e.tensor_copy(
                            gsl, ps[pbk][0:128, 0:512]),
                            r=[b_ps[pbk]], w=[b_g[th]])
                    else:
                        S_.op("dve", lambda e, gsl=gsl, pbk=pbk:
                              e.tensor_tensor(gsl, ps[pbk][0:128, 0:512],
                                              gsl, ALU.add),
                              r=[b_ps[pbk], b_g[th]], w=[b_g[th]])

        S_.retag([b_hs[0], b_hs[1]], [b_t1, b_t2])
        S_.retag([b_slB], [b_pb[s][i] for s in range(2) for i in range(3)]
                 + [b_xbc[s][i] for s in range(2) for i in range(3)]
                 + b_acum + b_acumf + b_alast + b_ds + b_dcb + b_expa
                 + [b_arow, b_arow2])
        S_.retag([b_hst], [b_sst, b_sst2])
        if t == 0:
            halo_ln_aprep()
            S_.retag([b_hsc], [b_sst])
        for c in range(CPT):
            a_prep(c, lambda k, c=c:
                   xnf[:, TT * k + 128 * c:TT * k + 128 * (c + 1)], [b_xn[c]])
        def mamba_inproj(hp):
            st = hp % 2
            for which, e0 in ((0, 128 * hp), (1, 2048 * which if False else
                              2048 + 128 * hp), (2, 4096 + 128 * hp)):
                bi = (3 * hp + which) % 2
                load_wblk(bi, e0)
                pbank = which % 2
                if t == 0:
                    halo_proj(hp, which, bi)
                inproj_block(bi, pbank, 512,
                             lambda k: xnf[:, TT * k:TT * k + 512],
                             list(b_xn))
                gb = 16 * which + hp
                pbs = slabB[:, PBO(st, which):PBO(st, which) + 516]
                S_.op("dve", lambda e, pbs=pbs, gb=gb: e.tensor_copy(
                    pbs[:, 0:3], tails[:, 3 * gb:3 * gb + 3]),
                    r=[b_tails[gb]], w=[b_pb[st][which]])
                S_.op("act", lambda e, pbs=pbs, pbank=pbank: e.copy(
                    pbs[:, 3:515], ps[pbank][0:128, 0:512]),
                    r=[b_ps[pbank]], w=[b_pb[st][which]])
                S_.op("dve", lambda e, pbs=pbs, gb=gb: e.tensor_copy(
                    tails[:, 3 * gb:3 * gb + 3], pbs[:, 512:515]),
                    r=[b_pb[st][which]], w=[b_tails[gb]])
                conv_chain(st, which, gb, 512, off_in=0, off_out=0)

        def ssd_block(hp, next_first=False):
            st = hp % 2
            if hp == 0:
                for f in mask_stages(0, 0, 0):
                    f()
            for c in range(CPT):
                if c + 1 < CPT:
                    mops = mask_stages(hp, c + 1, (c + 1) % 2)
                elif next_first:
                    mops = mask_stages(hp + 1, 0, 0)
                else:
                    mops = ()
                ssd_chunk(hp, c, 128 * c, st=st, mset=c % 2, mops=mops)

        for hp in range(NHP):
            mamba_inproj(hp)
            if t == 0:
                halo_state(hp)
            if hp > 0:
                ssd_block(hp - 1, next_first=True)
        ssd_block(NHP - 1)

        S_.retag([b_pb[s][i] for s in range(2) for i in range(3)]
                 + [b_xbc[s][i] for s in range(2) for i in range(3)]
                 + b_acum + b_acumf + b_alast + b_ds + b_dcb + b_expa
                 + [b_arow, b_arow2],
                 [b_zf[0], b_zf[1], b_zf[2], b_zf[3]])
        S_.retag([b_sst, b_sst2, b_hsc, b_hxb[0], b_hxb[1], b_hssd,
                  b_hac, b_hds, b_hal], [b_zb])
        dma("sp", tmp3[0:1, 0:DM],
            d["z_bias"].ap().rearrange("(o n) -> o n", o=1), w=[b_t3])
        for j in range(4):
            pj = 6 + (j % 2)
            S_.op("pe", lambda e, j=j, pj=pj: e.matmul(
                ps[pj][0:128, 0:512], ones[0:1, :],
                tmp3[0:1, 512 * j:512 * (j + 1)], start=True, stop=True),
                r=[b_const, b_t3, b_ps[pj]], w=[b_ps[pj]])
            S_.op("dve", lambda e, j=j, pj=pj: e.tensor_copy(
                hsx[:, 512 * j:512 * (j + 1)], ps[pj][0:128, 0:512]),
                r=[b_ps[pj]], w=[b_zb])
        S_.retag([b_slA], [b_slA2[0], b_slA2[1]])
        for zs in range(4):
            for hh2 in range(2):
                dma("sp", slAv[:, 8 * hh2:8 * hh2 + 8, :],
                    d["w_int"].ap().rearrange("(k p) e -> p k e", p=128)
                    [:, 8 * hh2:8 * hh2 + 8,
                     6144 + 512 * zs:6144 + 512 * (zs + 1)], w=[b_slA2[hh2]])
            for c in range(CPT):
                pbk = c % 2
                for k in range(NKD):
                    S_.op("pe", lambda e, k=k, c=c, pbk=pbk: e.matmul(
                        ps[pbk][0:128, 0:512],
                        xnf[:, TT * k + 128 * c:TT * k + 128 * (c + 1)],
                        slabA[:, 512 * k:512 * (k + 1)],
                        start=(k == 0), stop=(k == NKD - 1)),
                        r=[b_xn[c], b_slA2[k // 8]],
                        w=[b_ps[pbk]] if k in (0, NKD - 1) else ())
                S_.op("dve", lambda e, zs=zs, c=c, pbk=pbk: e.tensor_tensor(
                    tmp3[:, 512 * (c % 4):512 * (c % 4) + 512],
                    ps[pbk][0:128, 0:512],
                    hsx[:, 512 * zs:512 * (zs + 1)], ALU.add),
                    r=[b_ps[pbk], b_zb], w=[b_t3])
                S_.op("act", lambda e, zs=zs, c=c: e.activation(
                    slabB[:, DM * c + 512 * zs:DM * c + 512 * (zs + 1)],
                    tmp3[:, 512 * (c % 4):512 * (c % 4) + 512], AF.Silu),
                    r=[b_t3], w=[b_zf[c]])

        S_.retag([b_zb], [b_rmw])
        dma("sp", tmp3[0:1, 0:DM],
            d["rms_w"].ap().rearrange("(o n) -> o n", o=1), w=[b_t3])
        for j in range(4):
            pj = 6 + (j % 2)
            S_.op("pe", lambda e, j=j, pj=pj: e.matmul(
                ps[pj][0:128, 0:512], ones[0:1, :],
                tmp3[0:1, 512 * j:512 * (j + 1)], start=True, stop=True),
                r=[b_const, b_t3, b_ps[pj]], w=[b_ps[pj]])
            S_.op("dve", lambda e, j=j, pj=pj: e.tensor_copy(
                hsx[:, 512 * j:512 * (j + 1)], ps[pj][0:128, 0:512]),
                r=[b_ps[pj]], w=[b_rmw])

        S_.retag(list(b_xn), list(b_ynf))

        def gating_tr(c):
            for k in range(NKD):
                pk = 6 + (k % 2)
                S_.op("pe", lambda e, k=k, pk=pk: e.transpose(
                    ps[pk][0:128, 0:128], tmp3[:, 128 * k:128 * (k + 1)],
                    ident[:, :]), r=[b_t3, b_const, b_ps[pk]], w=[b_ps[pk]])
                dst = xnf[:, TT * k + 128 * c:TT * k + 128 * (c + 1)]
                if k % 2 == 0:
                    S_.op("act", lambda e, dst=dst, pk=pk: e.copy(
                        dst, ps[pk][0:128, 0:128]), r=[b_ps[pk]], w=[b_ynf[c]])
                else:
                    S_.op("dve", lambda e, dst=dst, pk=pk: e.tensor_copy(
                        dst, ps[pk][0:128, 0:128]), r=[b_ps[pk]], w=[b_ynf[c]])

        def gating(c):
            S_.op("dve", lambda e, c=c: e.tensor_tensor(
                ytok[:, DM * c:DM * (c + 1)], slabB[:, DM * c:DM * (c + 1)],
                ytok[:, DM * c:DM * (c + 1)], ALU.mult),
                r=[b_zf[c], b_y[c]], w=[b_y[c]])
            S_.op("act", lambda e, c=c: e.activation(
                tmp3[:, :], ytok[:, DM * c:DM * (c + 1)], AF.Square,
                accum_out=stats[:, 1:2]), r=[b_y[c]], w=[b_t3, b_stats])
            S_.op("dve", lambda e: e.tensor_scalar(
                stats[:, 7:8], stats[:, 1:2], 1.0 / DM, 1e-5,
                ALU.mult, ALU.add), r=[b_stats], w=[b_stats])
            S_.op("act", lambda e: e.activation(
                stats[:, 5:6], stats[:, 7:8], AF.Sqrt),
                r=[b_stats], w=[b_stats])
            S_.op("dve", lambda e: e.reciprocal(stats[:, 6:7], stats[:, 5:6]),
                  r=[b_stats], w=[b_stats])
            S_.op("dve", lambda e, c=c: e.scalar_tensor_tensor(
                tmp3[:, :], ytok[:, DM * c:DM * (c + 1)], stats[:, 6:7],
                hsx[:, 0:DM], ALU.mult, ALU.mult),
                r=[b_y[c], b_stats, b_rmw], w=[b_t3])

        gating(0)
        gating_tr(0)
        S_.retag([b_y[c] for c in range(CPT)], [b_stk[c] for c in range(CPT)])
        for dsl in range(4):
            for hh2 in range(2):
                dma("sp", slAv[:, 8 * hh2:8 * hh2 + 8, :],
                    d["w_outt"].ap().rearrange("(k p) e -> p k e", p=128)
                    [:, 8 * hh2:8 * hh2 + 8, 512 * dsl:512 * (dsl + 1)],
                    w=[b_slA2[hh2]])
            for c in range(CPT):
                pbk = c % 2
                if dsl == 0 and c + 1 < CPT:
                    gating(c + 1)
                for k in range(NKD):
                    S_.op("pe", lambda e, k=k, c=c, pbk=pbk: e.matmul(
                        ps[pbk][0:128, 0:512],
                        xnf[:, TT * k + 128 * c:TT * k + 128 * (c + 1)],
                        slabA[:, 512 * k:512 * (k + 1)],
                        start=(k == 0), stop=(k == NKD - 1)),
                        r=[b_ynf[c], b_slA2[k // 8]],
                        w=[b_ps[pbk]] if k in (0, NKD - 1) else ())
                S_.op("act", lambda e, c=c, dsl=dsl, pbk=pbk: e.copy(
                    ytok[:, DM * c + 512 * dsl:DM * c + 512 * (dsl + 1)],
                    ps[pbk][0:128, 0:512]), r=[b_ps[pbk]], w=[b_stk[c]])
                if dsl == 0 and c + 1 < CPT:
                    gating_tr(c + 1)
                if dsl == 2 and t + 1 < n_tiles:
                    if c == 0:
                        S_.retag([b_zf[cc] for cc in range(CPT)], b_lnus)
                        S_.retag([b_rmw], [b_lnx])
                    if c >= 2:
                        cc = c - 2
                        ln_load(cc, d["u_own"].ap()
                                [TT * (t + 1) + 128 * cc:
                                 TT * (t + 1) + 128 * (cc + 1), :])
                if dsl == 3 and t + 1 < n_tiles:
                    emit_ln_chunk(t + 1, c, preloaded=True)
                    if c + 2 < CPT:
                        cc = c + 2
                        ln_load(cc, d["u_own"].ap()
                                [TT * (t + 1) + 128 * cc:
                                 TT * (t + 1) + 128 * (cc + 1), :])

        S_.retag([b_hxn, b_hpb[0], b_hpb[1], b_t1, b_t2], [b_t1, b_t2])
        for c in range(CPT):
            dma("act", tmp1[:, :],
                d["u_own"].ap()[TT * t + 128 * c:TT * t + 128 * (c + 1), :],
                w=[b_t1])
            S_.op("act", lambda e, c=c: e.activation(
                tmp3[:, :], gacc[:, DM * c:DM * (c + 1)], AF.Silu),
                r=[b_g[c]], w=[b_t3])
            S_.op("dve", lambda e, c=c: e.tensor_tensor(
                tmp2[:, :], tmp3[:, :], ytok[:, DM * c:DM * (c + 1)],
                ALU.mult), r=[b_t3, b_stk[c]], w=[b_t2])
            S_.op("dve", lambda e: e.tensor_tensor(
                tmp2[:, :], tmp1[:, :], tmp2[:, :], ALU.add),
                r=[b_t1, b_t2], w=[b_t2])
            od = dma("act",
                     d["out"].ap()[TT * t + 128 * c:TT * t + 128 * (c + 1), :],
                     tmp2[:, :], r=[b_t2])
            out_dmas.append(od)

    blk = es.enter_context(nc.Block())
    S_.finalize(blk, sem, qsem, out_dmas)
    es.close()
    return nc


_CACHE = {}


def _consts():
    c = {}
    c["ident"] = np.eye(128, dtype=np.float32)
    c["negtriu"] = -np.triu(np.ones((128, 128), dtype=np.float32))
    c["triladd"] = np.where(np.triu(np.ones((128, 128), dtype=bool)),
                            np.float32(0.0),
                            np.float32(-1e30)).astype(np.float32)
    s127 = np.zeros((128, 1), dtype=np.float32)
    s127[127, 0] = 1.0
    c["sel127"] = s127
    c["ones_row"] = np.ones((1, 128), dtype=np.float32)
    return c


def make_in_maps(u, w_int, w_outt, w_gate, w_up, w_down, conv_w, conv_b,
                 z_bias, rms_w, dvec, n_cores, tok):
    cns = _consts()
    in_maps = []
    for i in range(n_cores):
        m = {
            "u_own": np.ascontiguousarray(u[tok * i:tok * (i + 1)]),
            "u_halo": (np.zeros((256, DM), dtype=np.float32) if i == 0
                       else np.ascontiguousarray(u[tok * i - 256:tok * i])),
            "flag": (np.zeros((128, 1), dtype=np.float32) if i == 0
                     else np.ones((128, 1), dtype=np.float32)),
            "w_int": w_int, "w_outt": w_outt, "w_gate": w_gate,
            "w_up": w_up, "w_down": w_down,
            "conv_w": np.ascontiguousarray(
                conv_w.reshape(48, 128, 4).transpose(1, 0, 2).reshape(128, 192)),
            "conv_b": np.ascontiguousarray(conv_b.reshape(48, 128).T),
            "z_bias": z_bias, "rms_w": rms_w, "dvec": dvec,
        }
        m.update(cns)
        in_maps.append(m)
    return in_maps


def kernel(L0_attn_u, L0_attn_W_in, L0_attn_z_bias, L0_attn_conv_w,
           L0_attn_conv_b, L0_attn_D, L0_attn_rms_w, L0_attn_W_out,
           L0_ffn_w_gate, L0_ffn_w_up, L0_ffn_w_down):
    u = np.ascontiguousarray(np.asarray(L0_attn_u, dtype=np.float32)[0])
    w_int = np.ascontiguousarray(np.asarray(L0_attn_W_in, dtype=np.float32).T)
    w_outt = np.ascontiguousarray(np.asarray(L0_attn_W_out,
                                             dtype=np.float32).T)
    args = (u, w_int, w_outt,
            np.ascontiguousarray(np.asarray(L0_ffn_w_gate, dtype=np.float32)),
            np.ascontiguousarray(np.asarray(L0_ffn_w_up, dtype=np.float32)),
            np.ascontiguousarray(np.asarray(L0_ffn_w_down, dtype=np.float32)),
            np.ascontiguousarray(np.asarray(L0_attn_conv_w, dtype=np.float32)),
            np.ascontiguousarray(np.asarray(L0_attn_conv_b, dtype=np.float32)),
            np.ascontiguousarray(np.asarray(L0_attn_z_bias, dtype=np.float32)),
            np.ascontiguousarray(np.asarray(L0_attn_rms_w, dtype=np.float32)),
            np.ascontiguousarray(np.asarray(L0_attn_D, dtype=np.float32)))
    if "nc" not in _CACHE:
        _CACHE["nc"] = build_program(n_tiles=4)
    nc = _CACHE["nc"]
    TOK = 16384 // NCORES
    in_maps = make_in_maps(*args, NCORES, TOK)
    res = run_bass_kernel_spmd(nc, in_maps, core_ids=list(range(NCORES)))
    out = np.concatenate([np.asarray(r["out"]) for r in res.results], axis=0)
    return np.ascontiguousarray(out[None].astype(np.float32))


# version 18
# speedup vs baseline: 1.0076x; 1.0076x over previous
import numpy as np
from contextlib import ExitStack

import concourse.bass as bass
import concourse.mybir as mybir
from concourse.bass_utils import run_bass_kernel_spmd

F32 = mybir.dt.float32
AF = mybir.ActivationFunctionType
ALU = mybir.AluOpType
AXX = mybir.AxisListType.X

NCORES = 8
DM = 2048
NH = 32
HD = 64
NHP = NH // 2
CONVD = 6144
PROJ = 8224
FFI = 8192
CH = 128
TT = 512
CPT = TT // CH
NKD = DM // 128

ENGS = ("pe", "dve", "act", "pool", "sp")


class Buf:
    __slots__ = ("name", "w", "rs", "pending")

    def __init__(self, name):
        self.name = name
        self.w = None
        self.rs = []
        self.pending = set()


NRING = 8


class Op:
    __slots__ = ("eng", "fn", "deps", "dma", "need_inc", "cum", "qj", "sidx")

    def __init__(self, eng, fn, deps, dma):
        self.eng = eng
        self.fn = fn
        self.deps = deps
        self.dma = dma
        self.need_inc = False
        self.cum = 0
        self.qj = 0
        self.sidx = 0


class Sched:
    def __init__(self):
        self.ops = {e: [] for e in ENGS}
        self.all = []
        self.ring = {}
        self.qcnt = {}

    def op(self, eng, fn, r=(), w=(), deps=(), dma=False):
        ds = set()
        for b in r:
            if b.w is not None:
                ds.add(b.w)
            if b.pending:
                ds.update(b.pending)
                b.pending = set()
        for b in w:
            if b.w is not None:
                ds.add(b.w)
            ds.update(b.rs)
            if b.pending:
                ds.update(b.pending)
                b.pending = set()
        for dd in deps:
            if dd is not None:
                ds.add(dd)
        best = {}
        for dd in ds:
            key = (dd.eng, dd.qj, dd.dma)
            cur = best.get(key)
            if cur is None or dd.sidx > cur.sidx:
                best[key] = dd
        ds = set(best.values())
        o = Op(eng, fn, ds, dma)
        o.sidx = len(self.ops[eng])
        if dma:
            idx = self.qcnt.get(eng, 0)
            self.qcnt[eng] = idx + 1
            j = idx % NRING
            prev = self.ring.get((eng, j))
            if prev is not None:
                ds.add(prev)
            self.ring[(eng, j)] = o
            o.qj = j
            o.cum = 16 * (idx // NRING + 1)
            o.need_inc = True
        self.ops[eng].append(o)
        self.all.append(o)
        for b in r:
            b.rs.append(o)
        for b in w:
            b.w = o
            b.rs = []
        return o

    @staticmethod
    def retag(olds, news):
        ds = set()
        for b in olds:
            if b.w is not None:
                ds.add(b.w)
            ds.update(b.rs)
            ds.update(b.pending)
        for b in news:
            b.w = None
            b.rs = []
            b.pending = set(ds)

    def finalize(self, blk, sems, qsems, tail_waits):
        for o in self.all:
            for dd in o.deps:
                dd.need_inc = True
        for e in ENGS:
            c_cum = 0
            for o in self.ops[e]:
                if o.need_inc and not o.dma:
                    c_cum += 1
                    o.cum = c_cum
            assert c_cum < 60000, (e, c_cum)
        for (e, j), o in self.ring.items():
            assert o.cum < 60000

        def key_sem(dd):
            if dd.dma:
                return ("q", dd.eng, dd.qj), qsems[dd.eng][dd.qj]
            return dd.eng, sems[dd.eng]

        def emit_stream(e, eng):
            hw = {}
            for o in self.ops[e]:
                tgt = {}
                for dd in o.deps:
                    key, sem = key_sem(dd)
                    if tgt.get(key, (0, None))[0] < dd.cum:
                        tgt[key] = (dd.cum, sem)
                needed = [(v, sem) for key, (v, sem) in tgt.items()
                          if hw.get(key, 0) < v]
                for key, (v, sem) in tgt.items():
                    if hw.get(key, 0) < v:
                        hw[key] = v
                for v, sem in needed[:-1]:
                    eng.wait_ge(sem, v)
                inst = o.fn(eng)
                if needed:
                    inst._wait_ge(needed[-1][1], needed[-1][0])
                if o.need_inc:
                    if o.dma:
                        inst.then_inc(qsems[o.eng][o.qj], 16)
                    else:
                        inst.then_inc(sems[o.eng], 1)
            if e == "pool":
                for dd in tail_waits:
                    key, sem = key_sem(dd)
                    if hw.get(key, 0) < dd.cum:
                        hw[key] = dd.cum
                        eng.wait_ge(sem, dd.cum)

        blk.tensor(lambda eng: emit_stream("pe", eng))
        blk.vector(lambda eng: emit_stream("dve", eng))
        blk.scalar(lambda eng: emit_stream("act", eng))
        blk.gpsimd(lambda eng: emit_stream("pool", eng))
        blk.sync(lambda eng: emit_stream("sp", eng))


def build_program(n_tiles=4):
    TOK = n_tiles * TT
    nc = bass.Bass("TRN2", target_bir_lowering=False,
                   dynamic_dma_scratch_size=2048)

    d = {}
    for nm, shp in [("u_own", [TOK, DM]), ("u_halo", [256, DM]),
                    ("flag", [128, 1]), ("w_int", [DM, PROJ]),
                    ("w_outt", [DM, DM]), ("w_gate", [DM, FFI]),
                    ("w_up", [DM, FFI]), ("w_down", [FFI, DM]),
                    ("conv_w", [128, 192]), ("conv_b", [128, 48]),
                    ("z_bias", [DM]), ("rms_w", [DM]), ("dvec", [NH]),
                    ("ident", [128, 128]), ("negtriu", [128, 128]),
                    ("triladd", [128, 128]), ("sel127", [128, 1]),
                    ("ones_row", [1, 128])]:
        d[nm] = nc.dram_tensor(nm, shp, F32, kind="ExternalInput")
    d["out"] = nc.dram_tensor("out", [TOK, DM], F32, kind="ExternalOutput")

    es = ExitStack()
    sb = lambda name, shape: es.enter_context(nc.sbuf_tensor(name, shape, F32))

    xnf = sb("xnf", [128, NKD * TT])
    ytok = sb("ytok", [128, CPT * DM])
    gacc = sb("gacc", [128, CPT * DM])
    slabA = sb("slabA", [128, NKD * 512])
    slabB = sb("slabB", [128, NKD * 512])
    wbuf = sb("wbuf", [128, 2 * 2048])
    hsx = sb("hsx", [128, 2048])
    tmp1 = sb("tmp1", [128, DM])
    tmp2 = sb("tmp2", [128, DM])
    tmp3 = sb("tmp3", [128, DM])
    ident = sb("identc", [128, 128])
    negtriu = sb("negtriuc", [128, 128])
    triladd = sb("triladdc", [128, 128])
    sel127 = sb("sel127c", [128, 1])
    ones = sb("onesc", [1, 128])
    dbc = sb("dbcc", [128, NH])
    convw = sb("convwc", [128, 48 * 4])
    convb = sb("convbc", [128, 48])
    flagv = sb("flagc", [128, 1])
    wa = sb("wac", [128, NKD * NH])
    tails = sb("tailsc", [128, 48 * 3])
    S = sb("Sc", [128, NHP * HD])
    stats = sb("statsc", [128, 8])
    cbias = sb("cbiasc", [128, 2])
    acum = slabB[:, 6168:6424]
    acumf = slabB[0:32, 6424:6936]
    alast = slabB[0:1, 6936:7064]
    dsb = slabB[:, 7064:7192]
    dcb = slabB[:, 7192:7320]
    expa = slabB[:, 7320:7448]
    arow = slabB[0:1, 7448:7576]
    aw = (slabB[0:1, 7448:7704], slabB[0:1, 7704:7960])

    ps = [es.enter_context(nc.psum_tensor(f"ps{i}", [128, 512], F32))
          for i in range(8)]
    sem = {e: es.enter_context(nc.semaphore(f"sem_{e}")) for e in ENGS}
    qsem = {e: [es.enter_context(nc.semaphore(f"qsem_{e}{j}"))
                for j in range(8)]
            for e in ("sp", "act", "pool")}

    S_ = Sched()

    b_const = Buf("const")
    b_wa = Buf("wa")
    b_tails = [Buf(f"tl{i}") for i in range(48)]
    b_S = [Buf(f"S{h}") for h in range(NH)]
    b_acum = [Buf(f"ac{c}") for c in range(CPT)]
    b_acumf = [Buf(f"af{c}") for c in range(CPT)]
    b_alast = [Buf(f"al{c}") for c in range(CPT)]
    b_ds = [Buf(f"ds{c}") for c in range(CPT)]
    b_dcb = [Buf(f"dc{c}") for c in range(CPT)]
    b_expa = [Buf(f"ea{c}") for c in range(CPT)]
    b_arow = Buf("arow")
    b_aw = (Buf("aw0"), Buf("aw1"))
    b_ps7a = Buf("ps7a")
    b_ps7b = b_ps7a
    b_ps7c = b_ps7a
    b_stats = Buf("stats")
    b_xn = [Buf(f"xn{c}") for c in range(CPT)]
    b_xnh = Buf("xnh")
    b_lnus = [Buf("lnu0"), Buf("lnu1")]
    b_lnx = Buf("lnx")
    b_ynf = [Buf(f"ynf{c}") for c in range(CPT)]
    b_y = [Buf(f"y{c}") for c in range(CPT)]
    b_stk = [Buf(f"stk{c}") for c in range(CPT)]
    b_g = [Buf(f"g{c}") for c in range(CPT)]
    b_slA = Buf("slA")
    b_slA2 = [Buf("slAh0"), Buf("slAh1")]
    b_slB = Buf("slB")
    b_pb = [[Buf(f"pb{s}{i}") for i in range(3)] for s in range(2)]
    b_xbc = [[Buf(f"xb{s}{i}") for i in range(3)] for s in range(2)]
    b_zf = [Buf(f"zf{c}") for c in range(CPT)]
    b_wb = [Buf("wb0"), Buf("wb1")]
    b_hst = Buf("hst")
    b_hs = [Buf("hs0"), Buf("hs1")]
    b_sst = Buf("sst")
    b_zb = Buf("zb")
    b_rmw = Buf("rmw")
    b_t1 = Buf("t1")
    b_t2 = Buf("t2")
    b_t3 = Buf("t3")
    b_ps = [Buf(f"p{i}") for i in range(8)]

    def dma(q, dst_ap, src_ap, r=(), w=(), deps=()):
        return S_.op(q, lambda e, a=dst_ap, b2=src_ap: e.dma_start(a, b2),
                     r=r, w=w, deps=deps, dma=True)

    XT2 = hsx[:, 0:128]
    BT2 = hsx[:, 128:256]
    MSK = hsx[:, 256:384]
    MTm = hsx[:, 384:512]
    BTS = hsx[:, 512:640]
    O1 = hsx[:, 640:704]
    E1 = hsx[:, 704:768]
    SPA = hsx[:, 768:800]
    SPB = hsx[:, 800:832]

    dma("sp", ident[:, :], d["ident"].ap(), w=[b_const])
    dma("sp", negtriu[:, :], d["negtriu"].ap(), w=[b_const])
    dma("sp", triladd[:, :], d["triladd"].ap(), w=[b_const])
    dma("sp", sel127[:, :], d["sel127"].ap(), w=[b_const])
    dma("sp", ones[:, :], d["ones_row"].ap(), w=[b_const])
    dma("sp", flagv[:, :], d["flag"].ap(), w=[b_const])
    dma("sp", convw[:, :], d["conv_w"].ap(), w=[b_const])
    dma("sp", convb[:, :], d["conv_b"].ap(), w=[b_const])
    dma("sp", wa.ap().rearrange("p (k f) -> p k f", k=NKD),
        d["w_int"].ap().rearrange("(k p) e -> p k e", p=128)[:, :, 8192:8224],
        w=[b_wa])
    dma("sp", tmp2[0:1, 0:NH],
        d["dvec"].ap().rearrange("(o n) -> o n", o=1), w=[b_t2])
    S_.op("dve", lambda e: e.memset(cbias[:, 0:1], 1e-5), w=[b_const])
    S_.op("dve", lambda e: e.memset(cbias[:, 1:2], 1.0), w=[b_const])
    S_.op("pe", lambda e: e.matmul(ps[2][0:128, 0:NH], ones[0:1, :],
                                   tmp2[0:1, 0:NH], start=True, stop=True),
          r=[b_const, b_t2], w=[b_ps[2]])
    S_.op("dve", lambda e: e.tensor_copy(dbc[:, :], ps[2][0:128, 0:NH]),
          r=[b_ps[2]], w=[b_const])

    lnus = (slabB[:, 0:2048], slabB[:, 2048:4096])
    lnx = hsx[:, 0:2048]

    def ln_load(c, u_src_ap):
        lnu = lnus[c % 2]
        lnub = b_lnus[c % 2]
        dma("act", lnu, u_src_ap, w=[lnub])

    def ln_chunk(c, u_src_ap, xn_out_fn, preloaded=False):
        lnu = lnus[c % 2]
        lnub = b_lnus[c % 2]
        if not preloaded:
            dma("act", lnu, u_src_ap, w=[lnub])
        S_.op("dve", lambda e, lnu=lnu: e.tensor_reduce(
            stats[:, 0:1], lnu, AXX, ALU.add), r=[lnub], w=[b_stats])
        S_.op("act", lambda e, lnu=lnu: e.activation(
            lnx, lnu, AF.Square, accum_out=stats[:, 1:2]),
            r=[lnub], w=[b_lnx, b_stats])
        S_.op("act", lambda e: e.activation(
            stats[:, 2:3], stats[:, 0:1], AF.Copy, scale=1.0 / DM),
            r=[b_stats], w=[b_stats])
        S_.op("act", lambda e: e.activation(
            stats[:, 3:4], stats[:, 2:3], AF.Square), r=[b_stats], w=[b_stats])
        S_.op("dve", lambda e: e.scalar_tensor_tensor(
            stats[:, 4:5], stats[:, 1:2], 1.0 / DM, stats[:, 3:4],
            ALU.mult, ALU.subtract), r=[b_stats], w=[b_stats])
        S_.op("dve", lambda e: e.tensor_scalar(
            stats[:, 7:8], stats[:, 4:5], 1e-5, None, ALU.add),
            r=[b_stats], w=[b_stats])
        S_.op("act", lambda e: e.activation(
            stats[:, 5:6], stats[:, 7:8], AF.Sqrt),
            r=[b_stats], w=[b_stats])
        S_.op("dve", lambda e: e.reciprocal(stats[:, 6:7], stats[:, 5:6]),
              r=[b_stats], w=[b_stats])
        for k in range(NKD):
            pk = 6 + (k % 2)
            sc = lnx[:, 128 * (k % 2):128 * (k % 2) + 128]
            S_.op("dve", lambda e, k=k, sc=sc, lnu=lnu, lnub=lnub:
                  e.tensor_scalar(
                      sc, lnu[:, 128 * k:128 * (k + 1)], stats[:, 2:3],
                      stats[:, 6:7], ALU.subtract, ALU.mult),
                  r=[lnub, b_stats], w=[b_lnx])
            S_.op("pe", lambda e, sc=sc, pk=pk: e.transpose(
                ps[pk][0:128, 0:128], sc, ident[:, :]),
                r=[b_lnx, b_const, b_ps[pk]], w=[b_ps[pk]])
            dst, wb = xn_out_fn(k)
            if k % 2 == 0:
                S_.op("act", lambda e, dst=dst, pk=pk: e.copy(
                    dst, ps[pk][0:128, 0:128]), r=[b_ps[pk]], w=[wb])
            else:
                S_.op("dve", lambda e, dst=dst, pk=pk: e.tensor_copy(
                    dst, ps[pk][0:128, 0:128]), r=[b_ps[pk]], w=[wb])

    def a_prep(c, xn_tile_fn, xn_deps):
        for k in range(NKD):
            S_.op("pe", lambda e, k=k: e.matmul(
                ps[5][0:128, 0:NH], xn_tile_fn(k), wa[:, 32 * k:32 * (k + 1)],
                start=(k == 0), stop=(k == NKD - 1)),
                r=list(xn_deps) + [b_wa],
                w=[b_ps[5]] if k in (0, NKD - 1) else ())
        S_.op("act", lambda e: e.activation(SPA, ps[5][0:128, 0:NH], AF.Exp),
              r=[b_ps[5]], w=[b_sst])
        S_.op("act", lambda e: e.activation(SPB, SPA, AF.Ln,
                                            bias=cbias[:, 1:2]),
              r=[b_sst, b_const], w=[b_sst])
        S_.op("pe", lambda e: e.matmul(
            ps[5][0:128, 0:NH], negtriu[:, :], SPB, start=True, stop=True),
            r=[b_const, b_sst, b_ps[5]], w=[b_ps[5]])
        S_.op("dve", lambda e, c=c: e.tensor_copy(
            acum[:, 64 * c + 32:64 * c + 64], ps[5][0:128, 0:NH]),
            r=[b_ps[5]], w=[b_acum[c]])
        S_.op("pe", lambda e, c=c: e.transpose(
            ps[6][0:64, 0:128], acum[:, 64 * c:64 * c + 64], ident[:, :]),
            r=[b_acum[c], b_const, b_ps[6]], w=[b_ps[6]])
        S_.op("dve", lambda e, c=c: e.tensor_copy(
            acumf[0:32, 128 * c:128 * (c + 1)], ps[6][32:64, 0:128]),
            r=[b_ps[6]], w=[b_acumf[c]])
        S_.op("pe", lambda e, c=c: e.matmul(
            ps[7][0:1, 0:NH], sel127[:, :], acum[:, 64 * c + 32:64 * c + 64],
            start=True, stop=True),
            r=[b_const, b_acum[c], b_ps7a], w=[b_ps7a])
        S_.op("dve", lambda e, c=c: e.tensor_copy(
            alast[0:1, 32 * c:32 * (c + 1)], ps[7][0:1, 0:NH]),
            r=[b_ps7a], w=[b_alast[c]])
        S_.op("pe", lambda e, c=c: e.matmul(
            ps[7][0:128, 0:NH], ones[0:1, :], alast[0:1, 32 * c:32 * (c + 1)],
            start=True, stop=True),
            r=[b_const, b_alast[c], b_ps7a], w=[b_ps7a])
        S_.op("dve", lambda e, c=c: e.tensor_tensor(
            dsb[:, 32 * c:32 * (c + 1)], ps[7][0:128, 0:NH],
            acum[:, 64 * c + 32:64 * c + 64], ALU.subtract),
            r=[b_ps7a, b_acum[c]], w=[b_ds[c]])
        S_.op("act", lambda e, c=c: e.activation(
            dsb[:, 32 * c:32 * (c + 1)], dsb[:, 32 * c:32 * (c + 1)], AF.Exp),
            r=[b_ds[c]], w=[b_ds[c]])
        S_.op("act", lambda e, c=c: e.activation(
            dcb[:, 32 * c:32 * (c + 1)], ps[7][0:128, 0:NH], AF.Exp),
            r=[b_ps7a], w=[b_dcb[c]])
        S_.op("act", lambda e, c=c: e.activation(
            expa[:, 32 * c:32 * (c + 1)], acum[:, 64 * c + 32:64 * c + 64],
            AF.Exp), r=[b_acum[c]], w=[b_expa[c]])

    wbv = wbuf.ap().rearrange("p (b q) -> p b q", b=2)

    def load_wblk(bi, e0, width=128):
        src = d["w_int"].ap().rearrange("(k p) e -> p k e", p=128)
        dst = wbuf.ap().rearrange("p (b k f) -> p b k f", b=2, k=NKD)
        return dma("sp", dst[:, bi, :, 0:width], src[:, :, e0:e0 + width],
                   w=[b_wb[bi]])

    def inproj_block(bi, pbank, toks_n, xn_tile_fn, xn_deps):
        wv = wbuf.ap().rearrange("p (b k f) -> p b k f", b=2, k=NKD)
        for k in range(NKD):
            S_.op("pe", lambda e, k=k: e.matmul(
                ps[pbank][0:128, 0:toks_n], wv[:, bi, k, 0:128],
                xn_tile_fn(k), start=(k == 0), stop=(k == NKD - 1)),
                r=list(xn_deps) + [b_wb[bi]],
                w=[b_ps[pbank]] if k in (0, NKD - 1) else ())

    def PBO(st, which):
        return 3084 * st + 516 * which

    def XBO(st, which):
        return 3084 * st + 1548 + 512 * which

    def conv_chain(st, which, gb, n_tok, off_in=0, off_out=0):
        pbs = slabB[:, PBO(st, which):PBO(st, which) + 516]
        xo = slabB[:, XBO(st, which):XBO(st, which) + 512]
        S_.op("dve", lambda e: e.tensor_scalar(
            xo[:, off_out:off_out + n_tok], pbs[:, off_in:off_in + n_tok],
            convw[:, 4 * gb:4 * gb + 1], None, ALU.mult),
            r=[b_pb[st][which], b_const], w=[b_xbc[st][which]])
        for wt in (1, 2, 3):
            S_.op("dve", lambda e, wt=wt: e.scalar_tensor_tensor(
                xo[:, off_out:off_out + n_tok],
                pbs[:, off_in + wt:off_in + wt + n_tok],
                convw[:, 4 * gb + wt:4 * gb + wt + 1],
                xo[:, off_out:off_out + n_tok], ALU.mult, ALU.add),
                r=[b_pb[st][which], b_const, b_xbc[st][which]],
                w=[b_xbc[st][which]])
        S_.op("act", lambda e: e.activation(
            xo[:, off_out:off_out + n_tok], xo[:, off_out:off_out + n_tok],
            AF.Silu, bias=convb[:, gb:gb + 1]),
            r=[b_xbc[st][which], b_const], w=[b_xbc[st][which]])

    def x_stage(hp, hl, cpair):
        h = 2 * hp + hl

        def f(h=h, hl=hl, cpair=cpair):
            S_.op("pe", lambda e: e.matmul(
                ps[7][0:1, 256:512], ident[0:32, h:h + 1],
                acumf[0:32, 256 * cpair:256 * cpair + 256],
                start=True, stop=True),
                r=[b_const, b_acumf[2 * cpair], b_acumf[2 * cpair + 1],
                   b_ps7c], w=[b_ps7c])
            S_.op("act", lambda e: e.activation(
                aw[hl], ps[7][0:1, 256:512], AF.Copy),
                r=[b_ps7c], w=[b_aw[hl]])
        return f

    def souter(hp, hl, c):
        h = 2 * hp + hl
        msks = (MSK, MSK2) if c % 2 == 0 else (MSKB, MSK2B)
        msk = msks[hl]
        po = 128 * hl
        pbf = b_ps7a if hl == 0 else b_ps7b

        def f(h=h, hl=hl, c=c, msk=msk, po=po, pbf=pbf):
            S_.op("pe", lambda e: e.matmul(
                ps[7][0:128, po:po + 128], ones[0:1, :],
                aw[hl][0:1, 128 * (c % 2):128 * (c % 2) + 128],
                start=True, stop=True),
                r=[b_const, b_aw[hl], pbf], w=[pbf])
            S_.op("dve", lambda e: e.scalar_tensor_tensor(
                msk, ps[7][0:128, po:po + 128],
                acum[:, 64 * c + 32 + h:64 * c + 32 + h + 1],
                triladd[:, :], ALU.subtract, ALU.add),
                r=[pbf, b_acum[c], b_const], w=[b_sst])
            S_.op("act", lambda e: e.activation(msk, msk, AF.Exp),
                  r=[b_sst], w=[b_sst])
        return f

    def pre_stages(hp):
        return [x_stage(hp, 0, 0), x_stage(hp, 1, 0),
                souter(hp, 0, 0), souter(hp, 1, 0)]

    def ssd_chunk(hp, c, x_off, st=0, mset=0, mops=()):
        mops = list(mops)

        def mpop():
            if mops:
                mops.pop(0)()
        h0, h1 = 2 * hp, 2 * hp + 1
        msks = (MSK, MSK2) if mset == 0 else (MSKB, MSK2B)
        xa = slabB[:, XBO(st, 0) + x_off:XBO(st, 0) + x_off + 128]
        ba = slabB[:, XBO(st, 1) + x_off:XBO(st, 1) + x_off + 128]
        ca = slabB[:, XBO(st, 2) + x_off:XBO(st, 2) + x_off + 128]
        S_.op("pe", lambda e: e.transpose(ps[2][0:128, 0:128], xa, ident[:, :]),
              r=[b_xbc[st][0], b_const, b_ps[2]], w=[b_ps[2]])
        S_.op("act", lambda e: e.activation(XT2, ps[2][0:128, 0:128], AF.Copy),
              r=[b_ps[2]], w=[b_sst])
        S_.op("pe", lambda e: e.transpose(ps[6][0:128, 0:128], ba, ident[:, :]),
              r=[b_xbc[st][1], b_const, b_ps[6]], w=[b_ps[6]])
        S_.op("act", lambda e: e.activation(BT2, ps[6][0:128, 0:128], AF.Copy),
              r=[b_ps[6]], w=[b_sst])
        mpop()
        for hl, h in ((0, h0), (1, h1)):
            hh = 64 * hl
            scol = S[hh:hh + 64, HD * hp:HD * hp + 64]
            msk = msks[hl]
            S_.op("pe", lambda e, hh=hh: e.matmul(
                ps[3][0:128, 0:128], ba[hh:hh + 64, :], ca[hh:hh + 64, :],
                start=True, stop=True),
                r=[b_xbc[st][1], b_xbc[st][2], b_ps[3]], w=[b_ps[3]])
            mpop()
            S_.op("dve", lambda e, msk=msk: e.tensor_tensor(
                MTm, ps[3][0:128, 0:128], msk, ALU.mult),
                r=[b_ps[3], b_sst], w=[b_sst])
            S_.op("pe", lambda e, hh=hh, scol=scol, h=h: e.matmul(
                ps[4][0:128, 0:64], ca[hh:hh + 64, :], scol,
                start=True, stop=True),
                r=[b_xbc[st][2], b_S[h], b_ps[4]], w=[b_ps[4]])
            if hl == 0:
                mpop()
            S_.op("dve", lambda e, h=h, c=c: e.tensor_scalar(
                O1, ps[4][0:128, 0:64],
                expa[:, 32 * c + h:32 * c + h + 1], None, ALU.mult),
                r=[b_ps[4], b_expa[c]], w=[b_sst])
            S_.op("dve", lambda e, h=h, hh=hh: e.scalar_tensor_tensor(
                E1, XT2[:, hh:hh + 64], dbc[:, h:h + 1], O1,
                ALU.mult, ALU.add), r=[b_sst, b_const], w=[b_sst])
            S_.op("pe", lambda e, hh=hh: e.matmul(
                ps[5][0:128, 0:64], MTm, XT2[:, hh:hh + 64],
                start=True, stop=True), r=[b_sst, b_ps[5]], w=[b_ps[5]])
            yslice = ytok[:, DM * c + 64 * h:DM * c + 64 * h + 64]
            S_.op("dve", lambda e, yslice=yslice: e.tensor_tensor(
                yslice, ps[5][0:128, 0:64], E1, ALU.add),
                r=[b_ps[5], b_sst], w=[b_y[c]])
        while mops:
            mpop()
        for hl, h in ((0, h0), (1, h1)):
            hh = 64 * hl
            S_.op("dve", lambda e, hh=hh, h=h, c=c: e.tensor_scalar(
                BTS[:, hh:hh + 64], BT2[:, hh:hh + 64],
                dsb[:, 32 * c + h:32 * c + h + 1], None, ALU.mult),
                r=[b_sst, b_ds[c]], w=[b_sst])
        S_.op("pe", lambda e: e.matmul(
            ps[6][0:128, 0:128], BTS, XT2, start=True, stop=True),
            r=[b_sst, b_ps[6]], w=[b_ps[6]])
        for hl, h in ((0, h0), (1, h1)):
            hh = 64 * hl
            scol = S[hh:hh + 64, HD * hp:HD * hp + 64]
            pview = ps[6][hh:hh + 64, hh:hh + 64]
            S_.op("dve", lambda e, scol=scol, hh=hh, h=h, c=c:
                  e.tensor_scalar(
                      ssd_tmp[hh:hh + 64, 0:64], scol,
                      dcb[hh:hh + 64, 32 * c + h:32 * c + h + 1],
                      None, ALU.mult),
                  r=[b_S[h], b_dcb[c]], w=[b_sst2])
            S_.op("dve", lambda e, pview=pview, scol=scol, hh=hh:
                  e.tensor_tensor(scol, pview, ssd_tmp[hh:hh + 64, 0:64],
                                  ALU.add),
                  r=[b_ps[6], b_sst2], w=[b_S[h]])

    ssd_tmp = hsx[:, 832:896]
    MSK2 = hsx[:, 896:1024]
    MSKB = hsx[:, 1024:1152]
    MSK2B = hsx[:, 1152:1280]
    arow2 = slabB[0:1, 7576:7704]
    b_sst2 = Buf("sst2")
    b_arow2 = Buf("arow2")
    hpb = (tmp1[:, 1048:1568], tmp2[:, 1048:1568])
    hxb = (hsx[:, 1280:1408], hsx[:, 1408:1536])
    hsc = (hsx[:, 1024:1152], hsx[:, 1152:1280])
    hXT = hsx[:, 1536:1664]
    hBT = hsx[:, 1664:1792]
    hBTS = hsx[:, 1792:1920]
    hacum = hsx[:, 1920:1952]
    hds = hsx[:, 1952:1984]
    halast = hsx[0:1, 1984:2016]
    hspa = tmp3[:, 0:32]
    hspb = tmp3[:, 32:64]
    b_hxn = Buf("hxn")
    b_hsc = Buf("hsc")
    b_hpb = [Buf("hpb0"), Buf("hpb1")]
    b_hxb = [Buf("hxb0"), Buf("hxb1")]
    b_hssd = Buf("hssd")
    b_hac = Buf("hac")
    b_hds = Buf("hds")
    b_hal = Buf("hal")

    def hxn_tile(k, lo, hi):
        base = tmp2 if k < 8 else tmp1
        kk = k % 8
        return base[:, 131 * kk + (lo - 125):131 * kk + (hi - 125)]

    def halo_ln_aprep():
        sink = slabB[:, 0:2048]
        sinkw = [b_pb[0][0], b_pb[0][1], b_pb[0][2], b_xbc[0][0]]
        for hc in range(2):
            lo = 125 if hc == 0 else 128
            dma("act", tmp3[:, :],
                d["u_halo"].ap()[128 * hc:128 * (hc + 1), :], w=[b_t3])
            S_.op("dve", lambda e: e.tensor_reduce(
                stats[:, 0:1], tmp3[:, :], AXX, ALU.add),
                r=[b_t3], w=[b_stats])
            S_.op("act", lambda e: e.activation(
                sink, tmp3[:, :], AF.Square, accum_out=stats[:, 1:2]),
                r=[b_t3], w=sinkw + [b_stats])
            S_.op("act", lambda e: e.activation(
                stats[:, 2:3], stats[:, 0:1], AF.Copy, scale=1.0 / DM),
                r=[b_stats], w=[b_stats])
            S_.op("act", lambda e: e.activation(
                stats[:, 3:4], stats[:, 2:3], AF.Square),
                r=[b_stats], w=[b_stats])
            S_.op("dve", lambda e: e.scalar_tensor_tensor(
                stats[:, 4:5], stats[:, 1:2], 1.0 / DM, stats[:, 3:4],
                ALU.mult, ALU.subtract), r=[b_stats], w=[b_stats])
            S_.op("dve", lambda e: e.tensor_scalar(
                stats[:, 7:8], stats[:, 4:5], 1e-5, None, ALU.add),
                r=[b_stats], w=[b_stats])
            S_.op("act", lambda e: e.activation(
                stats[:, 5:6], stats[:, 7:8], AF.Sqrt),
                r=[b_stats], w=[b_stats])
            S_.op("dve", lambda e: e.reciprocal(stats[:, 6:7], stats[:, 5:6]),
                  r=[b_stats], w=[b_stats])
            for k in range(NKD):
                sc = hsc[k % 2]
                pk = 6 + (k % 2)
                S_.op("dve", lambda e, k=k, sc=sc: e.tensor_scalar(
                    sc, tmp3[:, 128 * k:128 * (k + 1)], stats[:, 2:3],
                    stats[:, 6:7], ALU.subtract, ALU.mult),
                    r=[b_t3, b_stats], w=[b_hsc])
                S_.op("pe", lambda e, sc=sc, pk=pk: e.transpose(
                    ps[pk][0:128, 0:128], sc, ident[:, :]),
                    r=[b_hsc, b_const, b_ps[pk]], w=[b_ps[pk]])
                dst = hxn_tile(k, lo, 128 * (hc + 1))
                srcc = ps[pk][0:128, (lo - 128 * hc):128]
                if k % 2 == 0:
                    S_.op("act", lambda e, dst=dst, srcc=srcc: e.copy(
                        dst, srcc), r=[b_ps[pk]], w=[b_hxn])
                else:
                    S_.op("dve", lambda e, dst=dst, srcc=srcc: e.tensor_copy(
                        dst, srcc), r=[b_ps[pk]], w=[b_hxn])
        for k in range(NKD):
            S_.op("pe", lambda e, k=k: e.matmul(
                ps[5][0:128, 0:NH], hxn_tile(k, 128, 256),
                wa[:, 32 * k:32 * (k + 1)],
                start=(k == 0), stop=(k == NKD - 1)),
                r=[b_hxn, b_wa],
                w=[b_ps[5]] if k in (0, NKD - 1) else ())
        S_.op("act", lambda e: e.activation(hspa, ps[5][0:128, 0:NH], AF.Exp),
              r=[b_ps[5], b_t3], w=[b_t3])
        S_.op("act", lambda e: e.activation(hspb, hspa, AF.Ln,
                                            bias=cbias[:, 1:2]),
              r=[b_t3, b_const], w=[b_t3])
        S_.op("pe", lambda e: e.matmul(
            ps[5][0:128, 0:NH], negtriu[:, :], hspb, start=True, stop=True),
            r=[b_const, b_t3, b_ps[5]], w=[b_ps[5]])
        S_.op("dve", lambda e: e.tensor_copy(hacum, ps[5][0:128, 0:NH]),
              r=[b_ps[5]], w=[b_hac])
        S_.op("pe", lambda e: e.matmul(
            ps[7][0:1, 0:NH], sel127[:, :], hacum, start=True, stop=True),
            r=[b_const, b_hac, b_ps7a], w=[b_ps7a])
        S_.op("dve", lambda e: e.tensor_copy(halast, ps[7][0:1, 0:NH]),
              r=[b_ps7a], w=[b_hal])
        S_.op("pe", lambda e: e.matmul(
            ps[7][0:128, 0:NH], ones[0:1, :], halast, start=True, stop=True),
            r=[b_const, b_hal, b_ps7a], w=[b_ps7a])
        S_.op("dve", lambda e: e.tensor_tensor(
            hds, ps[7][0:128, 0:NH], hacum, ALU.subtract),
            r=[b_ps7a, b_hac], w=[b_hds])
        S_.op("act", lambda e: e.activation(hds, hds, AF.Exp),
              r=[b_hds], w=[b_hds])

    def halo_proj(hp, which, bi):
        wv = wbuf.ap().rearrange("p (b k f) -> p b k f", b=2, k=NKD)
        gb = 16 * which + hp
        if which < 2:
            for k in range(NKD):
                S_.op("pe", lambda e, k=k: e.matmul(
                    ps[5][0:128, 0:131], wv[:, bi, k, 0:128],
                    hxn_tile(k, 125, 256),
                    start=(k == 0), stop=(k == NKD - 1)),
                    r=[b_hxn, b_wb[bi]],
                    w=[b_ps[5]] if k in (0, NKD - 1) else ())
            pbs = hpb[which]
            xo = hxb[which]
            S_.op("act", lambda e, pbs=pbs: e.copy(
                pbs[:, 0:131], ps[5][0:128, 0:131]),
                r=[b_ps[5]], w=[b_hpb[which]])
            S_.op("dve", lambda e, pbs=pbs, xo=xo, gb=gb: e.tensor_scalar(
                xo[:, 0:128], pbs[:, 0:128], convw[:, 4 * gb:4 * gb + 1],
                None, ALU.mult), r=[b_hpb[which], b_const], w=[b_hxb[which]])
            for wt in (1, 2, 3):
                S_.op("dve", lambda e, pbs=pbs, xo=xo, gb=gb, wt=wt:
                      e.scalar_tensor_tensor(
                          xo[:, 0:128], pbs[:, wt:wt + 128],
                          convw[:, 4 * gb + wt:4 * gb + wt + 1], xo[:, 0:128],
                          ALU.mult, ALU.add),
                      r=[b_hpb[which], b_const, b_hxb[which]],
                      w=[b_hxb[which]])
            S_.op("act", lambda e, xo=xo, gb=gb: e.activation(
                xo[:, 0:128], xo[:, 0:128], AF.Silu,
                bias=convb[:, gb:gb + 1]),
                r=[b_hxb[which], b_const], w=[b_hxb[which]])
            S_.op("dve", lambda e, pbs=pbs, gb=gb: e.tensor_copy(
                tails[:, 3 * gb:3 * gb + 3], pbs[:, 128:131]),
                r=[b_hpb[which]], w=[b_tails[gb]])
        else:
            for k in range(NKD):
                S_.op("pe", lambda e, k=k: e.matmul(
                    ps[5][0:128, 0:3], wv[:, bi, k, 0:128],
                    hxn_tile(k, 253, 256),
                    start=(k == 0), stop=(k == NKD - 1)),
                    r=[b_hxn, b_wb[bi]],
                    w=[b_ps[5]] if k in (0, NKD - 1) else ())
            S_.op("dve", lambda e, gb=gb: e.tensor_copy(
                tails[:, 3 * gb:3 * gb + 3], ps[5][0:128, 0:3]),
                r=[b_ps[5]], w=[b_tails[gb]])

    def halo_state(hp):
        h0, h1 = 2 * hp, 2 * hp + 1
        S_.op("pe", lambda e: e.transpose(
            ps[5][0:128, 0:128], hxb[0], ident[:, :]),
            r=[b_hxb[0], b_const, b_ps[5]], w=[b_ps[5]])
        S_.op("dve", lambda e: e.tensor_copy(hXT, ps[5][0:128, 0:128]),
              r=[b_ps[5]], w=[b_hssd])
        S_.op("pe", lambda e: e.transpose(
            ps[5][0:128, 0:128], hxb[1], ident[:, :]),
            r=[b_hxb[1], b_const, b_ps[5]], w=[b_ps[5]])
        S_.op("dve", lambda e: e.tensor_copy(hBT, ps[5][0:128, 0:128]),
              r=[b_ps[5]], w=[b_hssd])
        for hl, h in ((0, h0), (1, h1)):
            hh = 64 * hl
            S_.op("dve", lambda e, hh=hh, h=h: e.tensor_scalar(
                hBTS[:, hh:hh + 64], hBT[:, hh:hh + 64],
                hds[:, h:h + 1], None, ALU.mult),
                r=[b_hssd, b_hds], w=[b_hssd])
        S_.op("pe", lambda e: e.matmul(
            ps[5][0:128, 0:128], hBTS, hXT, start=True, stop=True),
            r=[b_hssd, b_ps[5]], w=[b_ps[5]])
        for hl, h in ((0, h0), (1, h1)):
            hh = 64 * hl
            scol = S[hh:hh + 64, HD * hp:HD * hp + 64]
            S_.op("dve", lambda e, hh=hh, scol=scol: e.tensor_scalar(
                scol, ps[5][hh:hh + 64, hh:hh + 64], flagv[hh:hh + 64, 0:1],
                None, ALU.mult), r=[b_ps[5], b_const], w=[b_S[h]])

    out_dmas = []

    def emit_ln_chunk(t, c, preloaded=False):
        S_.retag([b_ynf[c]], [b_xn[c]])
        ln_chunk(c,
                 d["u_own"].ap()[TT * t + 128 * c:TT * t + 128 * (c + 1), :],
                 lambda k, c=c: (
                     xnf[:, TT * k + 128 * c:TT * k + 128 * c + 128],
                     b_xn[c]), preloaded=preloaded)

    def emit_ln(t):
        for c in range(CPT):
            emit_ln_chunk(t, c)

    S_.retag([b_pb[0][i] for i in range(3)] + [b_xbc[0][i] for i in range(3)],
             b_lnus)
    S_.retag([b_sst, b_sst2], [b_lnx])
    emit_ln(0)
    for t in range(n_tiles):

        S_.retag([b_zf[c] for c in range(CPT)] + b_lnus
                 + [b_pb[s][i] for s in range(2) for i in range(3)]
                 + [b_xbc[s][i] for s in range(2) for i in range(3)]
                 + b_acum + b_acumf + b_alast
                 + b_ds + b_dcb + b_expa + [b_aw[0], b_aw[1]], [b_slB])
        S_.retag([b_sst, b_sst2, b_zb, b_rmw, b_lnx], [b_hst])
        S_.retag([b_slA2[0], b_slA2[1]], [b_slA])
        S_.retag([b_wb[0], b_wb[1]], [b_wb[0], b_wb[1]])
        slBv = slabB.ap().rearrange("p (k f) -> p k f", k=NKD)
        slAv = slabA.ap().rearrange("p (k f) -> p k f", k=NKD)

        def emit_hs_transpose(th, hs, hsbuf):
            for ic in range(4):
                S_.op("pe", lambda e, ic=ic, hs=hs: e.matmul(
                    ps[2][0:128, 128 * ic:128 * (ic + 1)],
                    hs[:, 128 * ic:128 * (ic + 1)], ident[:, :],
                    start=(ic == 0), stop=(ic == 3), is_transpose=True),
                    r=[hsbuf, b_const],
                    w=[b_ps[2]] if ic in (0, 3) else ())
            S_.op("act", lambda e, th=th: e.copy(
                hsx[:, 512 * th:512 * (th + 1)], ps[2][0:128, 0:512]),
                r=[b_ps[2]], w=[b_hst])

        for isl in range(16):
            pend_tr = []
            dma("sp", slAv[:, :, :],
                d["w_gate"].ap().rearrange("(k p) e -> p k e", p=128)
                [:, :, 512 * isl:512 * (isl + 1)], w=[b_slA])
            dma("sp", slBv[:, :, :],
                d["w_up"].ap().rearrange("(k p) e -> p k e", p=128)
                [:, :, 512 * isl:512 * (isl + 1)], w=[b_slB])
            for th in range(CPT):
                for k in range(NKD):
                    S_.op("pe", lambda e, k=k, th=th: e.matmul(
                        ps[0][0:128, 0:512],
                        xnf[:, TT * k + 128 * th:TT * k + 128 * (th + 1)],
                        slabA[:, 512 * k:512 * (k + 1)],
                        start=(k == 0), stop=(k == NKD - 1)),
                        r=[b_xn[th], b_slA],
                        w=[b_ps[0]] if k in (0, NKD - 1) else ())
                S_.op("act", lambda e: e.activation(
                    tmp1[:, 0:512], ps[0][0:128, 0:512], AF.Silu),
                    r=[b_ps[0]], w=[b_t1])
                for k in range(NKD):
                    S_.op("pe", lambda e, k=k, th=th: e.matmul(
                        ps[1][0:128, 0:512],
                        xnf[:, TT * k + 128 * th:TT * k + 128 * (th + 1)],
                        slabB[:, 512 * k:512 * (k + 1)],
                        start=(k == 0), stop=(k == NKD - 1)),
                        r=[b_xn[th], b_slB],
                        w=[b_ps[1]] if k in (0, NKD - 1) else ())
                S_.op("dve", lambda e: e.tensor_tensor(
                    tmp1[:, 512:1024], ps[1][0:128, 0:512], tmp1[:, 0:512],
                    ALU.mult), r=[b_ps[1], b_t1], w=[b_t1])
                hv = tmp1[:, 512:1024]
                habs = tmp1[:, 1536:2048]
                hs = (tmp1[:, 1024:1536] if th % 2 == 0
                      else tmp2[:, 1536:2048])
                S_.op("act", lambda e, hv=hv, habs=habs: e.activation(
                    habs, hv, AF.Abs), r=[b_t1], w=[b_t1])
                S_.op("dve", lambda e, habs=habs: e.tensor_tensor(
                    tmp2[:, 0:128], habs[:, 0::4], habs[:, 1::4], ALU.max),
                    r=[b_t1], w=[b_t2])
                S_.op("dve", lambda e, habs=habs: e.tensor_tensor(
                    tmp2[:, 128:256], habs[:, 0::4], habs[:, 1::4], ALU.min),
                    r=[b_t1], w=[b_t2])
                S_.op("dve", lambda e, habs=habs: e.tensor_tensor(
                    tmp2[:, 256:384], habs[:, 2::4], habs[:, 3::4], ALU.max),
                    r=[b_t1], w=[b_t2])
                S_.op("dve", lambda e, habs=habs: e.tensor_tensor(
                    tmp2[:, 384:512], habs[:, 2::4], habs[:, 3::4], ALU.min),
                    r=[b_t1], w=[b_t2])
                S_.op("dve", lambda e: e.tensor_tensor(
                    tmp2[:, 512:640], tmp2[:, 0:128], tmp2[:, 256:384],
                    ALU.min), r=[b_t2], w=[b_t2])
                S_.op("dve", lambda e: e.tensor_tensor(
                    tmp2[:, 640:768], tmp2[:, 128:256], tmp2[:, 384:512],
                    ALU.max), r=[b_t2], w=[b_t2])
                S_.op("dve", lambda e: e.tensor_tensor(
                    tmp2[:, 768:896], tmp2[:, 512:640], tmp2[:, 640:768],
                    ALU.max), r=[b_t2], w=[b_t2])
                for kk in range(4):
                    S_.op("dve", lambda e, kk=kk, habs=habs: e.tensor_tensor(
                        tmp2[:, 1024:1536][:, kk::4],
                        habs[:, kk::4], tmp2[:, 768:896], ALU.is_ge),
                        r=[b_t1, b_t2], w=[b_t2])
                hsbuf = b_hs[th % 2]
                S_.op("dve", lambda e, hv=hv, hs=hs: e.tensor_tensor(
                    hs, hv, tmp2[:, 1024:1536], ALU.mult),
                    r=[b_t1, b_t2], w=[hsbuf])
                pend_tr.append((th, hs, hsbuf))
                if th > 0:
                    emit_hs_transpose(*pend_tr.pop(0))
            for hd in range(4):
                wi = hd % 2
                wdv = wbuf.ap().rearrange("p (b k f) -> p b k f", b=2, k=4)
                dma("sp", wdv[:, wi, :, :],
                    d["w_down"].ap().rearrange("(k p) e -> p k e", p=128)
                    [:, 4 * isl:4 * isl + 4, 512 * hd:512 * (hd + 1)],
                    w=[b_wb[wi]])
                for th in range(CPT):
                    if hd == 0 and th == 1 and pend_tr:
                        emit_hs_transpose(*pend_tr.pop(0))
                    pbk = 3 + (th % 2)
                    for ic in range(4):
                        S_.op("pe", lambda e, ic=ic, th=th, wi=wi, pbk=pbk,
                              wdv=wdv: e.matmul(
                                  ps[pbk][0:128, 0:512],
                                  hsx[:, 512 * th + 128 * ic:
                                      512 * th + 128 * (ic + 1)],
                                  wdv[:, wi, ic, :],
                                  start=(ic == 0), stop=(ic == 3)),
                              r=[b_hst, b_wb[wi]],
                              w=[b_ps[pbk]] if ic in (0, 3) else ())
                    gsl = gacc[:, DM * th + 512 * hd:DM * th + 512 * (hd + 1)]
                    if isl == 0:
                        S_.op("dve", lambda e, gsl=gsl, pbk=pbk: e.tensor_copy(
                            gsl, ps[pbk][0:128, 0:512]),
                            r=[b_ps[pbk]], w=[b_g[th]])
                    else:
                        S_.op("dve", lambda e, gsl=gsl, pbk=pbk:
                              e.tensor_tensor(gsl, ps[pbk][0:128, 0:512],
                                              gsl, ALU.add),
                              r=[b_ps[pbk], b_g[th]], w=[b_g[th]])

        S_.retag([b_hs[0], b_hs[1]], [b_t1, b_t2])
        S_.retag([b_slB], [b_pb[s][i] for s in range(2) for i in range(3)]
                 + [b_xbc[s][i] for s in range(2) for i in range(3)]
                 + b_acum + b_acumf + b_alast + b_ds + b_dcb + b_expa
                 + [b_aw[0], b_aw[1]])
        S_.retag([b_hst], [b_sst, b_sst2])
        if t == 0:
            halo_ln_aprep()
            S_.retag([b_hsc], [b_sst])
        for c in range(CPT):
            a_prep(c, lambda k, c=c:
                   xnf[:, TT * k + 128 * c:TT * k + 128 * (c + 1)], [b_xn[c]])
        def mamba_inproj(hp):
            st = hp % 2
            for which, e0 in ((0, 128 * hp), (1, 2048 * which if False else
                              2048 + 128 * hp), (2, 4096 + 128 * hp)):
                bi = (3 * hp + which) % 2
                load_wblk(bi, e0)
                pbank = which % 2
                if t == 0:
                    halo_proj(hp, which, bi)
                inproj_block(bi, pbank, 512,
                             lambda k: xnf[:, TT * k:TT * k + 512],
                             list(b_xn))
                gb = 16 * which + hp
                pbs = slabB[:, PBO(st, which):PBO(st, which) + 516]
                S_.op("dve", lambda e, pbs=pbs, gb=gb: e.tensor_copy(
                    pbs[:, 0:3], tails[:, 3 * gb:3 * gb + 3]),
                    r=[b_tails[gb]], w=[b_pb[st][which]])
                S_.op("act", lambda e, pbs=pbs, pbank=pbank: e.copy(
                    pbs[:, 3:515], ps[pbank][0:128, 0:512]),
                    r=[b_ps[pbank]], w=[b_pb[st][which]])
                S_.op("dve", lambda e, pbs=pbs, gb=gb: e.tensor_copy(
                    tails[:, 3 * gb:3 * gb + 3], pbs[:, 512:515]),
                    r=[b_pb[st][which]], w=[b_tails[gb]])
                conv_chain(st, which, gb, 512, off_in=0, off_out=0)

        def ssd_block(hp, next_first=False):
            st = hp % 2
            if hp == 0:
                for f in pre_stages(0):
                    f()
            for c in range(CPT):
                if c == 0:
                    mops = [souter(hp, 0, 1), souter(hp, 1, 1),
                            x_stage(hp, 0, 1), x_stage(hp, 1, 1)]
                elif c < CPT - 1:
                    mops = [souter(hp, 0, c + 1), souter(hp, 1, c + 1)]
                elif next_first:
                    mops = pre_stages(hp + 1)
                else:
                    mops = ()
                ssd_chunk(hp, c, 128 * c, st=st, mset=c % 2, mops=mops)

        for hp in range(NHP):
            mamba_inproj(hp)
            if t == 0:
                halo_state(hp)
            if hp > 0:
                ssd_block(hp - 1, next_first=True)
        ssd_block(NHP - 1)

        S_.retag([b_pb[s][i] for s in range(2) for i in range(3)]
                 + [b_xbc[s][i] for s in range(2) for i in range(3)]
                 + b_acum + b_acumf + b_alast + b_ds + b_dcb + b_expa
                 + [b_aw[0], b_aw[1]],
                 [b_zf[0], b_zf[1], b_zf[2], b_zf[3]])
        S_.retag([b_sst, b_sst2, b_hsc, b_hxb[0], b_hxb[1], b_hssd,
                  b_hac, b_hds, b_hal], [b_zb])
        dma("sp", tmp3[0:1, 0:DM],
            d["z_bias"].ap().rearrange("(o n) -> o n", o=1), w=[b_t3])
        for j in range(4):
            pj = 6 + (j % 2)
            S_.op("pe", lambda e, j=j, pj=pj: e.matmul(
                ps[pj][0:128, 0:512], ones[0:1, :],
                tmp3[0:1, 512 * j:512 * (j + 1)], start=True, stop=True),
                r=[b_const, b_t3, b_ps[pj]], w=[b_ps[pj]])
            S_.op("dve", lambda e, j=j, pj=pj: e.tensor_copy(
                hsx[:, 512 * j:512 * (j + 1)], ps[pj][0:128, 0:512]),
                r=[b_ps[pj]], w=[b_zb])
        S_.retag([b_slA], [b_slA2[0], b_slA2[1]])
        for zs in range(4):
            for hh2 in range(2):
                dma("sp", slAv[:, 8 * hh2:8 * hh2 + 8, :],
                    d["w_int"].ap().rearrange("(k p) e -> p k e", p=128)
                    [:, 8 * hh2:8 * hh2 + 8,
                     6144 + 512 * zs:6144 + 512 * (zs + 1)], w=[b_slA2[hh2]])
            for c in range(CPT):
                pbk = c % 2
                for k in range(NKD):
                    S_.op("pe", lambda e, k=k, c=c, pbk=pbk: e.matmul(
                        ps[pbk][0:128, 0:512],
                        xnf[:, TT * k + 128 * c:TT * k + 128 * (c + 1)],
                        slabA[:, 512 * k:512 * (k + 1)],
                        start=(k == 0), stop=(k == NKD - 1)),
                        r=[b_xn[c], b_slA2[k // 8]],
                        w=[b_ps[pbk]] if k in (0, NKD - 1) else ())
                S_.op("dve", lambda e, zs=zs, c=c, pbk=pbk: e.tensor_tensor(
                    tmp3[:, 512 * (c % 4):512 * (c % 4) + 512],
                    ps[pbk][0:128, 0:512],
                    hsx[:, 512 * zs:512 * (zs + 1)], ALU.add),
                    r=[b_ps[pbk], b_zb], w=[b_t3])
                S_.op("act", lambda e, zs=zs, c=c: e.activation(
                    slabB[:, DM * c + 512 * zs:DM * c + 512 * (zs + 1)],
                    tmp3[:, 512 * (c % 4):512 * (c % 4) + 512], AF.Silu),
                    r=[b_t3], w=[b_zf[c]])

        S_.retag([b_zb], [b_rmw])
        dma("sp", tmp3[0:1, 0:DM],
            d["rms_w"].ap().rearrange("(o n) -> o n", o=1), w=[b_t3])
        for j in range(4):
            pj = 6 + (j % 2)
            S_.op("pe", lambda e, j=j, pj=pj: e.matmul(
                ps[pj][0:128, 0:512], ones[0:1, :],
                tmp3[0:1, 512 * j:512 * (j + 1)], start=True, stop=True),
                r=[b_const, b_t3, b_ps[pj]], w=[b_ps[pj]])
            S_.op("dve", lambda e, j=j, pj=pj: e.tensor_copy(
                hsx[:, 512 * j:512 * (j + 1)], ps[pj][0:128, 0:512]),
                r=[b_ps[pj]], w=[b_rmw])

        S_.retag(list(b_xn), list(b_ynf))

        def gating_tr(c):
            for k in range(NKD):
                pk = 6 + (k % 2)
                S_.op("pe", lambda e, k=k, pk=pk: e.transpose(
                    ps[pk][0:128, 0:128], tmp3[:, 128 * k:128 * (k + 1)],
                    ident[:, :]), r=[b_t3, b_const, b_ps[pk]], w=[b_ps[pk]])
                dst = xnf[:, TT * k + 128 * c:TT * k + 128 * (c + 1)]
                if k % 2 == 0:
                    S_.op("act", lambda e, dst=dst, pk=pk: e.copy(
                        dst, ps[pk][0:128, 0:128]), r=[b_ps[pk]], w=[b_ynf[c]])
                else:
                    S_.op("dve", lambda e, dst=dst, pk=pk: e.tensor_copy(
                        dst, ps[pk][0:128, 0:128]), r=[b_ps[pk]], w=[b_ynf[c]])

        def gating(c):
            S_.op("dve", lambda e, c=c: e.tensor_tensor(
                ytok[:, DM * c:DM * (c + 1)], slabB[:, DM * c:DM * (c + 1)],
                ytok[:, DM * c:DM * (c + 1)], ALU.mult),
                r=[b_zf[c], b_y[c]], w=[b_y[c]])
            S_.op("act", lambda e, c=c: e.activation(
                tmp3[:, :], ytok[:, DM * c:DM * (c + 1)], AF.Square,
                accum_out=stats[:, 1:2]), r=[b_y[c]], w=[b_t3, b_stats])
            S_.op("dve", lambda e: e.tensor_scalar(
                stats[:, 7:8], stats[:, 1:2], 1.0 / DM, 1e-5,
                ALU.mult, ALU.add), r=[b_stats], w=[b_stats])
            S_.op("act", lambda e: e.activation(
                stats[:, 5:6], stats[:, 7:8], AF.Sqrt),
                r=[b_stats], w=[b_stats])
            S_.op("dve", lambda e: e.reciprocal(stats[:, 6:7], stats[:, 5:6]),
                  r=[b_stats], w=[b_stats])
            S_.op("dve", lambda e, c=c: e.scalar_tensor_tensor(
                tmp3[:, :], ytok[:, DM * c:DM * (c + 1)], stats[:, 6:7],
                hsx[:, 0:DM], ALU.mult, ALU.mult),
                r=[b_y[c], b_stats, b_rmw], w=[b_t3])

        gating(0)
        gating_tr(0)
        S_.retag([b_y[c] for c in range(CPT)], [b_stk[c] for c in range(CPT)])
        for dsl in range(4):
            for hh2 in range(2):
                dma("sp", slAv[:, 8 * hh2:8 * hh2 + 8, :],
                    d["w_outt"].ap().rearrange("(k p) e -> p k e", p=128)
                    [:, 8 * hh2:8 * hh2 + 8, 512 * dsl:512 * (dsl + 1)],
                    w=[b_slA2[hh2]])
            for c in range(CPT):
                pbk = c % 2
                if dsl == 0 and c + 1 < CPT:
                    gating(c + 1)
                for k in range(NKD):
                    S_.op("pe", lambda e, k=k, c=c, pbk=pbk: e.matmul(
                        ps[pbk][0:128, 0:512],
                        xnf[:, TT * k + 128 * c:TT * k + 128 * (c + 1)],
                        slabA[:, 512 * k:512 * (k + 1)],
                        start=(k == 0), stop=(k == NKD - 1)),
                        r=[b_ynf[c], b_slA2[k // 8]],
                        w=[b_ps[pbk]] if k in (0, NKD - 1) else ())
                S_.op("act", lambda e, c=c, dsl=dsl, pbk=pbk: e.copy(
                    ytok[:, DM * c + 512 * dsl:DM * c + 512 * (dsl + 1)],
                    ps[pbk][0:128, 0:512]), r=[b_ps[pbk]], w=[b_stk[c]])
                if dsl == 0 and c + 1 < CPT:
                    gating_tr(c + 1)
                if dsl == 2 and t + 1 < n_tiles:
                    if c == 0:
                        S_.retag([b_zf[cc] for cc in range(CPT)], b_lnus)
                        S_.retag([b_rmw], [b_lnx])
                    if c >= 2:
                        cc = c - 2
                        ln_load(cc, d["u_own"].ap()
                                [TT * (t + 1) + 128 * cc:
                                 TT * (t + 1) + 128 * (cc + 1), :])
                if dsl == 3 and t + 1 < n_tiles:
                    emit_ln_chunk(t + 1, c, preloaded=True)
                    if c + 2 < CPT:
                        cc = c + 2
                        ln_load(cc, d["u_own"].ap()
                                [TT * (t + 1) + 128 * cc:
                                 TT * (t + 1) + 128 * (cc + 1), :])

        S_.retag([b_hxn, b_hpb[0], b_hpb[1], b_t1, b_t2], [b_t1, b_t2])
        for c in range(CPT):
            dma("act", tmp1[:, :],
                d["u_own"].ap()[TT * t + 128 * c:TT * t + 128 * (c + 1), :],
                w=[b_t1])
            S_.op("act", lambda e, c=c: e.activation(
                tmp3[:, :], gacc[:, DM * c:DM * (c + 1)], AF.Silu),
                r=[b_g[c]], w=[b_t3])
            S_.op("dve", lambda e, c=c: e.tensor_tensor(
                tmp2[:, :], tmp3[:, :], ytok[:, DM * c:DM * (c + 1)],
                ALU.mult), r=[b_t3, b_stk[c]], w=[b_t2])
            S_.op("dve", lambda e: e.tensor_tensor(
                tmp2[:, :], tmp1[:, :], tmp2[:, :], ALU.add),
                r=[b_t1, b_t2], w=[b_t2])
            od = dma("act",
                     d["out"].ap()[TT * t + 128 * c:TT * t + 128 * (c + 1), :],
                     tmp2[:, :], r=[b_t2])
            out_dmas.append(od)

    blk = es.enter_context(nc.Block())
    S_.finalize(blk, sem, qsem, out_dmas)
    es.close()
    return nc


_CACHE = {}


def _consts():
    c = {}
    c["ident"] = np.eye(128, dtype=np.float32)
    c["negtriu"] = -np.triu(np.ones((128, 128), dtype=np.float32))
    c["triladd"] = np.where(np.triu(np.ones((128, 128), dtype=bool)),
                            np.float32(0.0),
                            np.float32(-1e30)).astype(np.float32)
    s127 = np.zeros((128, 1), dtype=np.float32)
    s127[127, 0] = 1.0
    c["sel127"] = s127
    c["ones_row"] = np.ones((1, 128), dtype=np.float32)
    return c


def make_in_maps(u, w_int, w_outt, w_gate, w_up, w_down, conv_w, conv_b,
                 z_bias, rms_w, dvec, n_cores, tok):
    cns = _consts()
    in_maps = []
    for i in range(n_cores):
        m = {
            "u_own": np.ascontiguousarray(u[tok * i:tok * (i + 1)]),
            "u_halo": (np.zeros((256, DM), dtype=np.float32) if i == 0
                       else np.ascontiguousarray(u[tok * i - 256:tok * i])),
            "flag": (np.zeros((128, 1), dtype=np.float32) if i == 0
                     else np.ones((128, 1), dtype=np.float32)),
            "w_int": w_int, "w_outt": w_outt, "w_gate": w_gate,
            "w_up": w_up, "w_down": w_down,
            "conv_w": np.ascontiguousarray(
                conv_w.reshape(48, 128, 4).transpose(1, 0, 2).reshape(128, 192)),
            "conv_b": np.ascontiguousarray(conv_b.reshape(48, 128).T),
            "z_bias": z_bias, "rms_w": rms_w, "dvec": dvec,
        }
        m.update(cns)
        in_maps.append(m)
    return in_maps


def kernel(L0_attn_u, L0_attn_W_in, L0_attn_z_bias, L0_attn_conv_w,
           L0_attn_conv_b, L0_attn_D, L0_attn_rms_w, L0_attn_W_out,
           L0_ffn_w_gate, L0_ffn_w_up, L0_ffn_w_down):
    u = np.ascontiguousarray(np.asarray(L0_attn_u, dtype=np.float32)[0])
    w_int = np.ascontiguousarray(np.asarray(L0_attn_W_in, dtype=np.float32).T)
    w_outt = np.ascontiguousarray(np.asarray(L0_attn_W_out,
                                             dtype=np.float32).T)
    args = (u, w_int, w_outt,
            np.ascontiguousarray(np.asarray(L0_ffn_w_gate, dtype=np.float32)),
            np.ascontiguousarray(np.asarray(L0_ffn_w_up, dtype=np.float32)),
            np.ascontiguousarray(np.asarray(L0_ffn_w_down, dtype=np.float32)),
            np.ascontiguousarray(np.asarray(L0_attn_conv_w, dtype=np.float32)),
            np.ascontiguousarray(np.asarray(L0_attn_conv_b, dtype=np.float32)),
            np.ascontiguousarray(np.asarray(L0_attn_z_bias, dtype=np.float32)),
            np.ascontiguousarray(np.asarray(L0_attn_rms_w, dtype=np.float32)),
            np.ascontiguousarray(np.asarray(L0_attn_D, dtype=np.float32)))
    if "nc" not in _CACHE:
        _CACHE["nc"] = build_program(n_tiles=4)
    nc = _CACHE["nc"]
    TOK = 16384 // NCORES
    in_maps = make_in_maps(*args, NCORES, TOK)
    res = run_bass_kernel_spmd(nc, in_maps, core_ids=list(range(NCORES)))
    out = np.concatenate([np.asarray(r["out"]) for r in res.results], axis=0)
    return np.ascontiguousarray(out[None].astype(np.float32))


# version 19
# speedup vs baseline: 1.0085x; 1.0009x over previous
import numpy as np
from contextlib import ExitStack

import concourse.bass as bass
import concourse.mybir as mybir
from concourse.bass_utils import run_bass_kernel_spmd

F32 = mybir.dt.float32
AF = mybir.ActivationFunctionType
ALU = mybir.AluOpType
AXX = mybir.AxisListType.X

NCORES = 8
DM = 2048
NH = 32
HD = 64
NHP = NH // 2
CONVD = 6144
PROJ = 8224
FFI = 8192
CH = 128
TT = 512
CPT = TT // CH
NKD = DM // 128

ENGS = ("pe", "dve", "act", "pool", "sp")


class Buf:
    __slots__ = ("name", "w", "rs", "pending")

    def __init__(self, name):
        self.name = name
        self.w = None
        self.rs = []
        self.pending = set()


NRING = 8


class Op:
    __slots__ = ("eng", "fn", "deps", "dma", "need_inc", "cum", "qj", "sidx")

    def __init__(self, eng, fn, deps, dma):
        self.eng = eng
        self.fn = fn
        self.deps = deps
        self.dma = dma
        self.need_inc = False
        self.cum = 0
        self.qj = 0
        self.sidx = 0


class Sched:
    def __init__(self):
        self.ops = {e: [] for e in ENGS}
        self.all = []
        self.ring = {}
        self.qcnt = {}

    def op(self, eng, fn, r=(), w=(), deps=(), dma=False):
        ds = set()
        for b in r:
            if b.w is not None:
                ds.add(b.w)
            if b.pending:
                ds.update(b.pending)
                b.pending = set()
        for b in w:
            if b.w is not None:
                ds.add(b.w)
            ds.update(b.rs)
            if b.pending:
                ds.update(b.pending)
                b.pending = set()
        for dd in deps:
            if dd is not None:
                ds.add(dd)
        best = {}
        for dd in ds:
            key = (dd.eng, dd.qj, dd.dma)
            cur = best.get(key)
            if cur is None or dd.sidx > cur.sidx:
                best[key] = dd
        ds = set(best.values())
        o = Op(eng, fn, ds, dma)
        o.sidx = len(self.ops[eng])
        if dma:
            idx = self.qcnt.get(eng, 0)
            self.qcnt[eng] = idx + 1
            j = idx % NRING
            prev = self.ring.get((eng, j))
            if prev is not None:
                ds.add(prev)
            self.ring[(eng, j)] = o
            o.qj = j
            o.cum = 16 * (idx // NRING + 1)
            o.need_inc = True
        self.ops[eng].append(o)
        self.all.append(o)
        for b in r:
            b.rs.append(o)
        for b in w:
            b.w = o
            b.rs = []
        return o

    @staticmethod
    def retag(olds, news):
        ds = set()
        for b in olds:
            if b.w is not None:
                ds.add(b.w)
            ds.update(b.rs)
            ds.update(b.pending)
        for b in news:
            b.w = None
            b.rs = []
            b.pending = set(ds)

    def finalize(self, blk, sems, qsems, tail_waits):
        for o in self.all:
            for dd in o.deps:
                dd.need_inc = True
        for e in ENGS:
            c_cum = 0
            for o in self.ops[e]:
                if o.need_inc and not o.dma:
                    c_cum += 1
                    o.cum = c_cum
            assert c_cum < 60000, (e, c_cum)
        for (e, j), o in self.ring.items():
            assert o.cum < 60000

        def key_sem(dd):
            if dd.dma:
                return ("q", dd.eng, dd.qj), qsems[dd.eng][dd.qj]
            return dd.eng, sems[dd.eng]

        def emit_stream(e, eng):
            hw = {}
            for o in self.ops[e]:
                tgt = {}
                for dd in o.deps:
                    key, sem = key_sem(dd)
                    if tgt.get(key, (0, None))[0] < dd.cum:
                        tgt[key] = (dd.cum, sem)
                needed = [(v, sem) for key, (v, sem) in tgt.items()
                          if hw.get(key, 0) < v]
                for key, (v, sem) in tgt.items():
                    if hw.get(key, 0) < v:
                        hw[key] = v
                for v, sem in needed[:-1]:
                    eng.wait_ge(sem, v)
                inst = o.fn(eng)
                if needed:
                    inst._wait_ge(needed[-1][1], needed[-1][0])
                if o.need_inc:
                    if o.dma:
                        inst.then_inc(qsems[o.eng][o.qj], 16)
                    else:
                        inst.then_inc(sems[o.eng], 1)
            if e == "pool":
                for dd in tail_waits:
                    key, sem = key_sem(dd)
                    if hw.get(key, 0) < dd.cum:
                        hw[key] = dd.cum
                        eng.wait_ge(sem, dd.cum)

        blk.tensor(lambda eng: emit_stream("pe", eng))
        blk.vector(lambda eng: emit_stream("dve", eng))
        blk.scalar(lambda eng: emit_stream("act", eng))
        blk.gpsimd(lambda eng: emit_stream("pool", eng))
        blk.sync(lambda eng: emit_stream("sp", eng))


def build_program(n_tiles=4):
    TOK = n_tiles * TT
    nc = bass.Bass("TRN2", target_bir_lowering=False,
                   dynamic_dma_scratch_size=2048)

    d = {}
    for nm, shp in [("u_own", [TOK, DM]), ("u_halo", [256, DM]),
                    ("flag", [128, 1]), ("w_int", [DM, PROJ]),
                    ("w_outt", [DM, DM]), ("w_gate", [DM, FFI]),
                    ("w_up", [DM, FFI]), ("w_down", [FFI, DM]),
                    ("conv_w", [128, 192]), ("conv_b", [128, 48]),
                    ("z_bias", [DM]), ("rms_w", [DM]), ("dvec", [NH]),
                    ("ident", [128, 128]), ("negtriu", [128, 128]),
                    ("triladd", [128, 128]), ("sel127", [128, 1]),
                    ("ones_row", [1, 128])]:
        d[nm] = nc.dram_tensor(nm, shp, F32, kind="ExternalInput")
    d["out"] = nc.dram_tensor("out", [TOK, DM], F32, kind="ExternalOutput")

    es = ExitStack()
    sb = lambda name, shape: es.enter_context(nc.sbuf_tensor(name, shape, F32))

    xnf = sb("xnf", [128, NKD * TT])
    ytok = sb("ytok", [128, CPT * DM])
    gacc = sb("gacc", [128, CPT * DM])
    slabA = sb("slabA", [128, NKD * 512])
    slabB = sb("slabB", [128, NKD * 512])
    wbuf = sb("wbuf", [128, 2 * 2048])
    hsx = sb("hsx", [128, 2048])
    tmp1 = sb("tmp1", [128, DM])
    tmp2 = sb("tmp2", [128, DM])
    tmp3 = sb("tmp3", [128, DM])
    ident = sb("identc", [128, 128])
    negtriu = sb("negtriuc", [128, 128])
    triladd = sb("triladdc", [128, 128])
    sel127 = sb("sel127c", [128, 1])
    ones = sb("onesc", [1, 128])
    dbc = sb("dbcc", [128, NH])
    convw = sb("convwc", [128, 48 * 4])
    convb = sb("convbc", [128, 48])
    flagv = sb("flagc", [128, 1])
    wa = sb("wac", [128, NKD * NH])
    tails = sb("tailsc", [128, 48 * 3])
    S = sb("Sc", [128, NHP * HD])
    stats = sb("statsc", [128, 8])
    cbias = sb("cbiasc", [128, 2])
    acum = slabB[:, 6168:6424]
    acumf = slabB[0:32, 6424:6936]
    alast = slabB[0:1, 6936:7064]
    dsb = slabB[:, 7064:7192]
    dcb = slabB[:, 7192:7320]
    expa = slabB[:, 7320:7448]
    arow = slabB[0:1, 7448:7576]
    aw = (slabB[0:1, 7448:7704], slabB[0:1, 7704:7960])

    ps = [es.enter_context(nc.psum_tensor(f"ps{i}", [128, 512], F32))
          for i in range(8)]
    sem = {e: es.enter_context(nc.semaphore(f"sem_{e}")) for e in ENGS}
    qsem = {e: [es.enter_context(nc.semaphore(f"qsem_{e}{j}"))
                for j in range(8)]
            for e in ("sp", "act", "pool")}

    S_ = Sched()

    b_const = Buf("const")
    b_wa = Buf("wa")
    b_tails = [Buf(f"tl{i}") for i in range(48)]
    b_S = [Buf(f"S{h}") for h in range(NH)]
    b_acum = [Buf(f"ac{c}") for c in range(CPT)]
    b_acumf = [Buf(f"af{c}") for c in range(CPT)]
    b_alast = [Buf(f"al{c}") for c in range(CPT)]
    b_ds = [Buf(f"ds{c}") for c in range(CPT)]
    b_dcb = [Buf(f"dc{c}") for c in range(CPT)]
    b_expa = [Buf(f"ea{c}") for c in range(CPT)]
    b_arow = Buf("arow")
    b_aw = (Buf("aw0"), Buf("aw1"))
    b_ps7a = Buf("ps7a")
    b_ps7b = b_ps7a
    b_stats = Buf("stats")
    b_xn = [Buf(f"xn{c}") for c in range(CPT)]
    b_xnh = Buf("xnh")
    b_lnus = [Buf("lnu0"), Buf("lnu1")]
    b_lnx = Buf("lnx")
    b_ynf = [Buf(f"ynf{c}") for c in range(CPT)]
    b_y = [Buf(f"y{c}") for c in range(CPT)]
    b_stk = [Buf(f"stk{c}") for c in range(CPT)]
    b_g = [Buf(f"g{c}") for c in range(CPT)]
    b_slA = Buf("slA")
    b_slA2 = [Buf("slAh0"), Buf("slAh1")]
    b_slB = Buf("slB")
    b_pb = [[Buf(f"pb{s}{i}") for i in range(3)] for s in range(2)]
    b_xbc = [[Buf(f"xb{s}{i}") for i in range(3)] for s in range(2)]
    b_zf = [Buf(f"zf{c}") for c in range(CPT)]
    b_wb = [Buf("wb0"), Buf("wb1")]
    b_hst = Buf("hst")
    b_hs = [Buf("hs0"), Buf("hs1")]
    b_sst = Buf("sst")
    b_zb = Buf("zb")
    b_rmw = Buf("rmw")
    b_t1 = Buf("t1")
    b_t2 = Buf("t2")
    b_t3 = Buf("t3")
    b_ps = [Buf(f"p{i}") for i in range(8)]

    def dma(q, dst_ap, src_ap, r=(), w=(), deps=()):
        return S_.op(q, lambda e, a=dst_ap, b2=src_ap: e.dma_start(a, b2),
                     r=r, w=w, deps=deps, dma=True)

    XT2 = hsx[:, 0:128]
    BT2 = hsx[:, 128:256]
    MSK = hsx[:, 256:384]
    MTm = hsx[:, 384:512]
    BTS = hsx[:, 512:640]
    O1 = hsx[:, 640:704]
    E1 = hsx[:, 704:768]
    SPA = hsx[:, 768:800]
    SPB = hsx[:, 800:832]

    dma("sp", ident[:, :], d["ident"].ap(), w=[b_const])
    dma("sp", negtriu[:, :], d["negtriu"].ap(), w=[b_const])
    dma("sp", triladd[:, :], d["triladd"].ap(), w=[b_const])
    dma("sp", sel127[:, :], d["sel127"].ap(), w=[b_const])
    dma("sp", ones[:, :], d["ones_row"].ap(), w=[b_const])
    dma("sp", flagv[:, :], d["flag"].ap(), w=[b_const])
    dma("sp", convw[:, :], d["conv_w"].ap(), w=[b_const])
    dma("sp", convb[:, :], d["conv_b"].ap(), w=[b_const])
    dma("sp", wa.ap().rearrange("p (k f) -> p k f", k=NKD),
        d["w_int"].ap().rearrange("(k p) e -> p k e", p=128)[:, :, 8192:8224],
        w=[b_wa])
    dma("sp", tmp2[0:1, 0:NH],
        d["dvec"].ap().rearrange("(o n) -> o n", o=1), w=[b_t2])
    S_.op("dve", lambda e: e.memset(cbias[:, 0:1], 1e-5), w=[b_const])
    S_.op("dve", lambda e: e.memset(cbias[:, 1:2], 1.0), w=[b_const])
    S_.op("pe", lambda e: e.matmul(ps[2][0:128, 0:NH], ones[0:1, :],
                                   tmp2[0:1, 0:NH], start=True, stop=True),
          r=[b_const, b_t2], w=[b_ps[2]])
    S_.op("dve", lambda e: e.tensor_copy(dbc[:, :], ps[2][0:128, 0:NH]),
          r=[b_ps[2]], w=[b_const])

    lnus = (slabB[:, 0:2048], slabB[:, 2048:4096])
    lnx = hsx[:, 0:2048]

    def ln_load(c, u_src_ap):
        lnu = lnus[c % 2]
        lnub = b_lnus[c % 2]
        dma("act", lnu, u_src_ap, w=[lnub])

    def ln_chunk(c, u_src_ap, xn_out_fn, preloaded=False):
        lnu = lnus[c % 2]
        lnub = b_lnus[c % 2]
        if not preloaded:
            dma("act", lnu, u_src_ap, w=[lnub])
        S_.op("dve", lambda e, lnu=lnu: e.tensor_reduce(
            stats[:, 0:1], lnu, AXX, ALU.add), r=[lnub], w=[b_stats])
        S_.op("act", lambda e, lnu=lnu: e.activation(
            lnx, lnu, AF.Square, accum_out=stats[:, 1:2]),
            r=[lnub], w=[b_lnx, b_stats])
        S_.op("act", lambda e: e.activation(
            stats[:, 2:3], stats[:, 0:1], AF.Copy, scale=1.0 / DM),
            r=[b_stats], w=[b_stats])
        S_.op("act", lambda e: e.activation(
            stats[:, 3:4], stats[:, 2:3], AF.Square), r=[b_stats], w=[b_stats])
        S_.op("dve", lambda e: e.scalar_tensor_tensor(
            stats[:, 4:5], stats[:, 1:2], 1.0 / DM, stats[:, 3:4],
            ALU.mult, ALU.subtract), r=[b_stats], w=[b_stats])
        S_.op("dve", lambda e: e.tensor_scalar(
            stats[:, 7:8], stats[:, 4:5], 1e-5, None, ALU.add),
            r=[b_stats], w=[b_stats])
        S_.op("act", lambda e: e.activation(
            stats[:, 5:6], stats[:, 7:8], AF.Sqrt),
            r=[b_stats], w=[b_stats])
        S_.op("dve", lambda e: e.reciprocal(stats[:, 6:7], stats[:, 5:6]),
              r=[b_stats], w=[b_stats])
        for k in range(NKD):
            pk = 6 + (k % 2)
            sc = lnx[:, 128 * (k % 2):128 * (k % 2) + 128]
            S_.op("dve", lambda e, k=k, sc=sc, lnu=lnu, lnub=lnub:
                  e.tensor_scalar(
                      sc, lnu[:, 128 * k:128 * (k + 1)], stats[:, 2:3],
                      stats[:, 6:7], ALU.subtract, ALU.mult),
                  r=[lnub, b_stats], w=[b_lnx])
            S_.op("pe", lambda e, sc=sc, pk=pk: e.transpose(
                ps[pk][0:128, 0:128], sc, ident[:, :]),
                r=[b_lnx, b_const, b_ps[pk]], w=[b_ps[pk]])
            dst, wb = xn_out_fn(k)
            if k % 2 == 0:
                S_.op("act", lambda e, dst=dst, pk=pk: e.copy(
                    dst, ps[pk][0:128, 0:128]), r=[b_ps[pk]], w=[wb])
            else:
                S_.op("dve", lambda e, dst=dst, pk=pk: e.tensor_copy(
                    dst, ps[pk][0:128, 0:128]), r=[b_ps[pk]], w=[wb])

    def a_prep(c, xn_tile_fn, xn_deps):
        for k in range(NKD):
            S_.op("pe", lambda e, k=k: e.matmul(
                ps[5][0:128, 0:NH], xn_tile_fn(k), wa[:, 32 * k:32 * (k + 1)],
                start=(k == 0), stop=(k == NKD - 1)),
                r=list(xn_deps) + [b_wa],
                w=[b_ps[5]] if k in (0, NKD - 1) else ())
        S_.op("act", lambda e: e.activation(SPA, ps[5][0:128, 0:NH], AF.Exp),
              r=[b_ps[5]], w=[b_sst])
        S_.op("act", lambda e: e.activation(SPB, SPA, AF.Ln,
                                            bias=cbias[:, 1:2]),
              r=[b_sst, b_const], w=[b_sst])
        S_.op("pe", lambda e: e.matmul(
            ps[5][0:128, 0:NH], negtriu[:, :], SPB, start=True, stop=True),
            r=[b_const, b_sst, b_ps[5]], w=[b_ps[5]])
        S_.op("dve", lambda e, c=c: e.tensor_copy(
            acum[:, 64 * c + 32:64 * c + 64], ps[5][0:128, 0:NH]),
            r=[b_ps[5]], w=[b_acum[c]])
        S_.op("pe", lambda e, c=c: e.transpose(
            ps[6][0:64, 0:128], acum[:, 64 * c:64 * c + 64], ident[:, :]),
            r=[b_acum[c], b_const, b_ps[6]], w=[b_ps[6]])
        S_.op("dve", lambda e, c=c: e.tensor_copy(
            acumf[0:32, 128 * c:128 * (c + 1)], ps[6][32:64, 0:128]),
            r=[b_ps[6]], w=[b_acumf[c]])
        S_.op("pe", lambda e, c=c: e.matmul(
            ps[7][0:1, 0:NH], sel127[:, :], acum[:, 64 * c + 32:64 * c + 64],
            start=True, stop=True),
            r=[b_const, b_acum[c], b_ps7a], w=[b_ps7a])
        S_.op("dve", lambda e, c=c: e.tensor_copy(
            alast[0:1, 32 * c:32 * (c + 1)], ps[7][0:1, 0:NH]),
            r=[b_ps7a], w=[b_alast[c]])
        S_.op("pe", lambda e, c=c: e.matmul(
            ps[7][0:128, 0:NH], ones[0:1, :], alast[0:1, 32 * c:32 * (c + 1)],
            start=True, stop=True),
            r=[b_const, b_alast[c], b_ps7a], w=[b_ps7a])
        S_.op("dve", lambda e, c=c: e.tensor_tensor(
            dsb[:, 32 * c:32 * (c + 1)], ps[7][0:128, 0:NH],
            acum[:, 64 * c + 32:64 * c + 64], ALU.subtract),
            r=[b_ps7a, b_acum[c]], w=[b_ds[c]])
        S_.op("act", lambda e, c=c: e.activation(
            dsb[:, 32 * c:32 * (c + 1)], dsb[:, 32 * c:32 * (c + 1)], AF.Exp),
            r=[b_ds[c]], w=[b_ds[c]])
        S_.op("act", lambda e, c=c: e.activation(
            dcb[:, 32 * c:32 * (c + 1)], ps[7][0:128, 0:NH], AF.Exp),
            r=[b_ps7a], w=[b_dcb[c]])
        S_.op("act", lambda e, c=c: e.activation(
            expa[:, 32 * c:32 * (c + 1)], acum[:, 64 * c + 32:64 * c + 64],
            AF.Exp), r=[b_acum[c]], w=[b_expa[c]])

    wbv = wbuf.ap().rearrange("p (b q) -> p b q", b=2)

    def load_wblk(bi, e0, width=128):
        src = d["w_int"].ap().rearrange("(k p) e -> p k e", p=128)
        dst = wbuf.ap().rearrange("p (b k f) -> p b k f", b=2, k=NKD)
        return dma("sp", dst[:, bi, :, 0:width], src[:, :, e0:e0 + width],
                   w=[b_wb[bi]])

    def inproj_block(bi, pbank, toks_n, xn_tile_fn, xn_deps):
        wv = wbuf.ap().rearrange("p (b k f) -> p b k f", b=2, k=NKD)
        for k in range(NKD):
            S_.op("pe", lambda e, k=k: e.matmul(
                ps[pbank][0:128, 0:toks_n], wv[:, bi, k, 0:128],
                xn_tile_fn(k), start=(k == 0), stop=(k == NKD - 1)),
                r=list(xn_deps) + [b_wb[bi]],
                w=[b_ps[pbank]] if k in (0, NKD - 1) else ())

    def PBO(st, which):
        return 3084 * st + 516 * which

    def XBO(st, which):
        return 3084 * st + 1548 + 512 * which

    def conv_chain(st, which, gb, n_tok, off_in=0, off_out=0):
        pbs = slabB[:, PBO(st, which):PBO(st, which) + 516]
        xo = slabB[:, XBO(st, which):XBO(st, which) + 512]
        S_.op("dve", lambda e: e.tensor_scalar(
            xo[:, off_out:off_out + n_tok], pbs[:, off_in:off_in + n_tok],
            convw[:, 4 * gb:4 * gb + 1], None, ALU.mult),
            r=[b_pb[st][which], b_const], w=[b_xbc[st][which]])
        for wt in (1, 2, 3):
            S_.op("dve", lambda e, wt=wt: e.scalar_tensor_tensor(
                xo[:, off_out:off_out + n_tok],
                pbs[:, off_in + wt:off_in + wt + n_tok],
                convw[:, 4 * gb + wt:4 * gb + wt + 1],
                xo[:, off_out:off_out + n_tok], ALU.mult, ALU.add),
                r=[b_pb[st][which], b_const, b_xbc[st][which]],
                w=[b_xbc[st][which]])
        S_.op("act", lambda e: e.activation(
            xo[:, off_out:off_out + n_tok], xo[:, off_out:off_out + n_tok],
            AF.Silu, bias=convb[:, gb:gb + 1]),
            r=[b_xbc[st][which], b_const], w=[b_xbc[st][which]])

    def x_stage(hp, hl, cpair):
        h = 2 * hp + hl

        def f(h=h, hl=hl, cpair=cpair):
            S_.op("pe", lambda e: e.matmul(
                ps[2][0:1, 256:512], ident[0:32, h:h + 1],
                acumf[0:32, 256 * cpair:256 * cpair + 256],
                start=True, stop=True),
                r=[b_const, b_acumf[2 * cpair], b_acumf[2 * cpair + 1],
                   b_ps[2]], w=[b_ps[2]])
            S_.op("act", lambda e: e.activation(
                aw[hl], ps[2][0:1, 256:512], AF.Copy),
                r=[b_ps[2]], w=[b_aw[hl]])
        return f

    def souter(hp, hl, c):
        h = 2 * hp + hl
        msks = (MSK, MSK2) if c % 2 == 0 else (MSKB, MSK2B)
        msk = msks[hl]
        po = 128 * hl
        pbf = b_ps7a if hl == 0 else b_ps7b

        def f(h=h, hl=hl, c=c, msk=msk, po=po, pbf=pbf):
            S_.op("pe", lambda e: e.matmul(
                ps[7][0:128, po:po + 128], ones[0:1, :],
                aw[hl][0:1, 128 * (c % 2):128 * (c % 2) + 128],
                start=True, stop=True),
                r=[b_const, b_aw[hl], pbf], w=[pbf])
            S_.op("dve", lambda e: e.scalar_tensor_tensor(
                msk, ps[7][0:128, po:po + 128],
                acum[:, 64 * c + 32 + h:64 * c + 32 + h + 1],
                triladd[:, :], ALU.subtract, ALU.add),
                r=[pbf, b_acum[c], b_const], w=[b_sst])
            S_.op("act", lambda e: e.activation(msk, msk, AF.Exp),
                  r=[b_sst], w=[b_sst])
        return f

    def pre_stages(hp):
        return [x_stage(hp, 0, 0), x_stage(hp, 1, 0),
                souter(hp, 0, 0), souter(hp, 1, 0)]

    def ssd_chunk(hp, c, x_off, st=0, mset=0, mops=()):
        mops = list(mops)

        def mpop():
            if mops:
                mops.pop(0)()
        h0, h1 = 2 * hp, 2 * hp + 1
        msks = (MSK, MSK2) if mset == 0 else (MSKB, MSK2B)
        xa = slabB[:, XBO(st, 0) + x_off:XBO(st, 0) + x_off + 128]
        ba = slabB[:, XBO(st, 1) + x_off:XBO(st, 1) + x_off + 128]
        ca = slabB[:, XBO(st, 2) + x_off:XBO(st, 2) + x_off + 128]
        S_.op("pe", lambda e: e.transpose(ps[2][0:128, 0:128], xa, ident[:, :]),
              r=[b_xbc[st][0], b_const, b_ps[2]], w=[b_ps[2]])
        S_.op("act", lambda e: e.activation(XT2, ps[2][0:128, 0:128], AF.Copy),
              r=[b_ps[2]], w=[b_sst])
        S_.op("pe", lambda e: e.transpose(ps[6][0:128, 0:128], ba, ident[:, :]),
              r=[b_xbc[st][1], b_const, b_ps[6]], w=[b_ps[6]])
        S_.op("act", lambda e: e.activation(BT2, ps[6][0:128, 0:128], AF.Copy),
              r=[b_ps[6]], w=[b_sst])
        mpop()
        for hl, h in ((0, h0), (1, h1)):
            hh = 64 * hl
            scol = S[hh:hh + 64, HD * hp:HD * hp + 64]
            msk = msks[hl]
            S_.op("pe", lambda e, hh=hh: e.matmul(
                ps[3][0:128, 0:128], ba[hh:hh + 64, :], ca[hh:hh + 64, :],
                start=True, stop=True),
                r=[b_xbc[st][1], b_xbc[st][2], b_ps[3]], w=[b_ps[3]])
            mpop()
            S_.op("dve", lambda e, msk=msk: e.tensor_tensor(
                MTm, ps[3][0:128, 0:128], msk, ALU.mult),
                r=[b_ps[3], b_sst], w=[b_sst])
            S_.op("pe", lambda e, hh=hh, scol=scol, h=h: e.matmul(
                ps[4][0:128, 0:64], ca[hh:hh + 64, :], scol,
                start=True, stop=True),
                r=[b_xbc[st][2], b_S[h], b_ps[4]], w=[b_ps[4]])
            if hl == 0:
                mpop()
            S_.op("dve", lambda e, h=h, c=c: e.tensor_scalar(
                O1, ps[4][0:128, 0:64],
                expa[:, 32 * c + h:32 * c + h + 1], None, ALU.mult),
                r=[b_ps[4], b_expa[c]], w=[b_sst])
            S_.op("dve", lambda e, h=h, hh=hh: e.scalar_tensor_tensor(
                E1, XT2[:, hh:hh + 64], dbc[:, h:h + 1], O1,
                ALU.mult, ALU.add), r=[b_sst, b_const], w=[b_sst])
            S_.op("pe", lambda e, hh=hh: e.matmul(
                ps[5][0:128, 0:64], MTm, XT2[:, hh:hh + 64],
                start=True, stop=True), r=[b_sst, b_ps[5]], w=[b_ps[5]])
            yslice = ytok[:, DM * c + 64 * h:DM * c + 64 * h + 64]
            S_.op("dve", lambda e, yslice=yslice: e.tensor_tensor(
                yslice, ps[5][0:128, 0:64], E1, ALU.add),
                r=[b_ps[5], b_sst], w=[b_y[c]])
        while mops:
            mpop()
        for hl, h in ((0, h0), (1, h1)):
            hh = 64 * hl
            S_.op("dve", lambda e, hh=hh, h=h, c=c: e.tensor_scalar(
                BTS[:, hh:hh + 64], BT2[:, hh:hh + 64],
                dsb[:, 32 * c + h:32 * c + h + 1], None, ALU.mult),
                r=[b_sst, b_ds[c]], w=[b_sst])
        S_.op("pe", lambda e: e.matmul(
            ps[6][0:128, 0:128], BTS, XT2, start=True, stop=True),
            r=[b_sst, b_ps[6]], w=[b_ps[6]])
        for hl, h in ((0, h0), (1, h1)):
            hh = 64 * hl
            scol = S[hh:hh + 64, HD * hp:HD * hp + 64]
            pview = ps[6][hh:hh + 64, hh:hh + 64]
            S_.op("dve", lambda e, scol=scol, hh=hh, h=h, c=c:
                  e.tensor_scalar(
                      ssd_tmp[hh:hh + 64, 0:64], scol,
                      dcb[hh:hh + 64, 32 * c + h:32 * c + h + 1],
                      None, ALU.mult),
                  r=[b_S[h], b_dcb[c]], w=[b_sst2])
            S_.op("dve", lambda e, pview=pview, scol=scol, hh=hh:
                  e.tensor_tensor(scol, pview, ssd_tmp[hh:hh + 64, 0:64],
                                  ALU.add),
                  r=[b_ps[6], b_sst2], w=[b_S[h]])

    ssd_tmp = hsx[:, 832:896]
    MSK2 = hsx[:, 896:1024]
    MSKB = hsx[:, 1024:1152]
    MSK2B = hsx[:, 1152:1280]
    arow2 = slabB[0:1, 7576:7704]
    b_sst2 = Buf("sst2")
    b_arow2 = Buf("arow2")
    hpb = (tmp1[:, 1048:1568], tmp2[:, 1048:1568])
    hxb = (hsx[:, 1280:1408], hsx[:, 1408:1536])
    hsc = (hsx[:, 1024:1152], hsx[:, 1152:1280])
    hXT = hsx[:, 1536:1664]
    hBT = hsx[:, 1664:1792]
    hBTS = hsx[:, 1792:1920]
    hacum = hsx[:, 1920:1952]
    hds = hsx[:, 1952:1984]
    halast = hsx[0:1, 1984:2016]
    hspa = tmp3[:, 0:32]
    hspb = tmp3[:, 32:64]
    b_hxn = Buf("hxn")
    b_hsc = Buf("hsc")
    b_hpb = [Buf("hpb0"), Buf("hpb1")]
    b_hxb = [Buf("hxb0"), Buf("hxb1")]
    b_hssd = Buf("hssd")
    b_hac = Buf("hac")
    b_hds = Buf("hds")
    b_hal = Buf("hal")

    def hxn_tile(k, lo, hi):
        base = tmp2 if k < 8 else tmp1
        kk = k % 8
        return base[:, 131 * kk + (lo - 125):131 * kk + (hi - 125)]

    def halo_ln_aprep():
        sink = slabB[:, 0:2048]
        sinkw = [b_pb[0][0], b_pb[0][1], b_pb[0][2], b_xbc[0][0]]
        for hc in range(2):
            lo = 125 if hc == 0 else 128
            dma("act", tmp3[:, :],
                d["u_halo"].ap()[128 * hc:128 * (hc + 1), :], w=[b_t3])
            S_.op("dve", lambda e: e.tensor_reduce(
                stats[:, 0:1], tmp3[:, :], AXX, ALU.add),
                r=[b_t3], w=[b_stats])
            S_.op("act", lambda e: e.activation(
                sink, tmp3[:, :], AF.Square, accum_out=stats[:, 1:2]),
                r=[b_t3], w=sinkw + [b_stats])
            S_.op("act", lambda e: e.activation(
                stats[:, 2:3], stats[:, 0:1], AF.Copy, scale=1.0 / DM),
                r=[b_stats], w=[b_stats])
            S_.op("act", lambda e: e.activation(
                stats[:, 3:4], stats[:, 2:3], AF.Square),
                r=[b_stats], w=[b_stats])
            S_.op("dve", lambda e: e.scalar_tensor_tensor(
                stats[:, 4:5], stats[:, 1:2], 1.0 / DM, stats[:, 3:4],
                ALU.mult, ALU.subtract), r=[b_stats], w=[b_stats])
            S_.op("dve", lambda e: e.tensor_scalar(
                stats[:, 7:8], stats[:, 4:5], 1e-5, None, ALU.add),
                r=[b_stats], w=[b_stats])
            S_.op("act", lambda e: e.activation(
                stats[:, 5:6], stats[:, 7:8], AF.Sqrt),
                r=[b_stats], w=[b_stats])
            S_.op("dve", lambda e: e.reciprocal(stats[:, 6:7], stats[:, 5:6]),
                  r=[b_stats], w=[b_stats])
            for k in range(NKD):
                sc = hsc[k % 2]
                pk = 6 + (k % 2)
                S_.op("dve", lambda e, k=k, sc=sc: e.tensor_scalar(
                    sc, tmp3[:, 128 * k:128 * (k + 1)], stats[:, 2:3],
                    stats[:, 6:7], ALU.subtract, ALU.mult),
                    r=[b_t3, b_stats], w=[b_hsc])
                S_.op("pe", lambda e, sc=sc, pk=pk: e.transpose(
                    ps[pk][0:128, 0:128], sc, ident[:, :]),
                    r=[b_hsc, b_const, b_ps[pk]], w=[b_ps[pk]])
                dst = hxn_tile(k, lo, 128 * (hc + 1))
                srcc = ps[pk][0:128, (lo - 128 * hc):128]
                if k % 2 == 0:
                    S_.op("act", lambda e, dst=dst, srcc=srcc: e.copy(
                        dst, srcc), r=[b_ps[pk]], w=[b_hxn])
                else:
                    S_.op("dve", lambda e, dst=dst, srcc=srcc: e.tensor_copy(
                        dst, srcc), r=[b_ps[pk]], w=[b_hxn])
        for k in range(NKD):
            S_.op("pe", lambda e, k=k: e.matmul(
                ps[5][0:128, 0:NH], hxn_tile(k, 128, 256),
                wa[:, 32 * k:32 * (k + 1)],
                start=(k == 0), stop=(k == NKD - 1)),
                r=[b_hxn, b_wa],
                w=[b_ps[5]] if k in (0, NKD - 1) else ())
        S_.op("act", lambda e: e.activation(hspa, ps[5][0:128, 0:NH], AF.Exp),
              r=[b_ps[5], b_t3], w=[b_t3])
        S_.op("act", lambda e: e.activation(hspb, hspa, AF.Ln,
                                            bias=cbias[:, 1:2]),
              r=[b_t3, b_const], w=[b_t3])
        S_.op("pe", lambda e: e.matmul(
            ps[5][0:128, 0:NH], negtriu[:, :], hspb, start=True, stop=True),
            r=[b_const, b_t3, b_ps[5]], w=[b_ps[5]])
        S_.op("dve", lambda e: e.tensor_copy(hacum, ps[5][0:128, 0:NH]),
              r=[b_ps[5]], w=[b_hac])
        S_.op("pe", lambda e: e.matmul(
            ps[7][0:1, 0:NH], sel127[:, :], hacum, start=True, stop=True),
            r=[b_const, b_hac, b_ps7a], w=[b_ps7a])
        S_.op("dve", lambda e: e.tensor_copy(halast, ps[7][0:1, 0:NH]),
              r=[b_ps7a], w=[b_hal])
        S_.op("pe", lambda e: e.matmul(
            ps[7][0:128, 0:NH], ones[0:1, :], halast, start=True, stop=True),
            r=[b_const, b_hal, b_ps7a], w=[b_ps7a])
        S_.op("dve", lambda e: e.tensor_tensor(
            hds, ps[7][0:128, 0:NH], hacum, ALU.subtract),
            r=[b_ps7a, b_hac], w=[b_hds])
        S_.op("act", lambda e: e.activation(hds, hds, AF.Exp),
              r=[b_hds], w=[b_hds])

    def halo_proj(hp, which, bi):
        wv = wbuf.ap().rearrange("p (b k f) -> p b k f", b=2, k=NKD)
        gb = 16 * which + hp
        if which < 2:
            for k in range(NKD):
                S_.op("pe", lambda e, k=k: e.matmul(
                    ps[5][0:128, 0:131], wv[:, bi, k, 0:128],
                    hxn_tile(k, 125, 256),
                    start=(k == 0), stop=(k == NKD - 1)),
                    r=[b_hxn, b_wb[bi]],
                    w=[b_ps[5]] if k in (0, NKD - 1) else ())
            pbs = hpb[which]
            xo = hxb[which]
            S_.op("act", lambda e, pbs=pbs: e.copy(
                pbs[:, 0:131], ps[5][0:128, 0:131]),
                r=[b_ps[5]], w=[b_hpb[which]])
            S_.op("dve", lambda e, pbs=pbs, xo=xo, gb=gb: e.tensor_scalar(
                xo[:, 0:128], pbs[:, 0:128], convw[:, 4 * gb:4 * gb + 1],
                None, ALU.mult), r=[b_hpb[which], b_const], w=[b_hxb[which]])
            for wt in (1, 2, 3):
                S_.op("dve", lambda e, pbs=pbs, xo=xo, gb=gb, wt=wt:
                      e.scalar_tensor_tensor(
                          xo[:, 0:128], pbs[:, wt:wt + 128],
                          convw[:, 4 * gb + wt:4 * gb + wt + 1], xo[:, 0:128],
                          ALU.mult, ALU.add),
                      r=[b_hpb[which], b_const, b_hxb[which]],
                      w=[b_hxb[which]])
            S_.op("act", lambda e, xo=xo, gb=gb: e.activation(
                xo[:, 0:128], xo[:, 0:128], AF.Silu,
                bias=convb[:, gb:gb + 1]),
                r=[b_hxb[which], b_const], w=[b_hxb[which]])
            S_.op("dve", lambda e, pbs=pbs, gb=gb: e.tensor_copy(
                tails[:, 3 * gb:3 * gb + 3], pbs[:, 128:131]),
                r=[b_hpb[which]], w=[b_tails[gb]])
        else:
            for k in range(NKD):
                S_.op("pe", lambda e, k=k: e.matmul(
                    ps[5][0:128, 0:3], wv[:, bi, k, 0:128],
                    hxn_tile(k, 253, 256),
                    start=(k == 0), stop=(k == NKD - 1)),
                    r=[b_hxn, b_wb[bi]],
                    w=[b_ps[5]] if k in (0, NKD - 1) else ())
            S_.op("dve", lambda e, gb=gb: e.tensor_copy(
                tails[:, 3 * gb:3 * gb + 3], ps[5][0:128, 0:3]),
                r=[b_ps[5]], w=[b_tails[gb]])

    def halo_state(hp):
        h0, h1 = 2 * hp, 2 * hp + 1
        S_.op("pe", lambda e: e.transpose(
            ps[5][0:128, 0:128], hxb[0], ident[:, :]),
            r=[b_hxb[0], b_const, b_ps[5]], w=[b_ps[5]])
        S_.op("dve", lambda e: e.tensor_copy(hXT, ps[5][0:128, 0:128]),
              r=[b_ps[5]], w=[b_hssd])
        S_.op("pe", lambda e: e.transpose(
            ps[5][0:128, 0:128], hxb[1], ident[:, :]),
            r=[b_hxb[1], b_const, b_ps[5]], w=[b_ps[5]])
        S_.op("dve", lambda e: e.tensor_copy(hBT, ps[5][0:128, 0:128]),
              r=[b_ps[5]], w=[b_hssd])
        for hl, h in ((0, h0), (1, h1)):
            hh = 64 * hl
            S_.op("dve", lambda e, hh=hh, h=h: e.tensor_scalar(
                hBTS[:, hh:hh + 64], hBT[:, hh:hh + 64],
                hds[:, h:h + 1], None, ALU.mult),
                r=[b_hssd, b_hds], w=[b_hssd])
        S_.op("pe", lambda e: e.matmul(
            ps[5][0:128, 0:128], hBTS, hXT, start=True, stop=True),
            r=[b_hssd, b_ps[5]], w=[b_ps[5]])
        for hl, h in ((0, h0), (1, h1)):
            hh = 64 * hl
            scol = S[hh:hh + 64, HD * hp:HD * hp + 64]
            S_.op("dve", lambda e, hh=hh, scol=scol: e.tensor_scalar(
                scol, ps[5][hh:hh + 64, hh:hh + 64], flagv[hh:hh + 64, 0:1],
                None, ALU.mult), r=[b_ps[5], b_const], w=[b_S[h]])

    out_dmas = []

    def emit_ln_chunk(t, c, preloaded=False):
        S_.retag([b_ynf[c]], [b_xn[c]])
        ln_chunk(c,
                 d["u_own"].ap()[TT * t + 128 * c:TT * t + 128 * (c + 1), :],
                 lambda k, c=c: (
                     xnf[:, TT * k + 128 * c:TT * k + 128 * c + 128],
                     b_xn[c]), preloaded=preloaded)

    def emit_ln(t):
        for c in range(CPT):
            emit_ln_chunk(t, c)

    S_.retag([b_pb[0][i] for i in range(3)] + [b_xbc[0][i] for i in range(3)],
             b_lnus)
    S_.retag([b_sst, b_sst2], [b_lnx])
    emit_ln(0)
    for t in range(n_tiles):

        S_.retag([b_zf[c] for c in range(CPT)] + b_lnus
                 + [b_pb[s][i] for s in range(2) for i in range(3)]
                 + [b_xbc[s][i] for s in range(2) for i in range(3)]
                 + b_acum + b_acumf + b_alast
                 + b_ds + b_dcb + b_expa + [b_aw[0], b_aw[1]], [b_slB])
        S_.retag([b_sst, b_sst2, b_zb, b_rmw, b_lnx], [b_hst])
        S_.retag([b_slA2[0], b_slA2[1]], [b_slA])
        S_.retag([b_wb[0], b_wb[1]], [b_wb[0], b_wb[1]])
        slBv = slabB.ap().rearrange("p (k f) -> p k f", k=NKD)
        slAv = slabA.ap().rearrange("p (k f) -> p k f", k=NKD)

        def emit_hs_transpose(th, hs, hsbuf):
            for ic in range(4):
                S_.op("pe", lambda e, ic=ic, hs=hs: e.matmul(
                    ps[2][0:128, 128 * ic:128 * (ic + 1)],
                    hs[:, 128 * ic:128 * (ic + 1)], ident[:, :],
                    start=(ic == 0), stop=(ic == 3), is_transpose=True),
                    r=[hsbuf, b_const],
                    w=[b_ps[2]] if ic in (0, 3) else ())
            S_.op("act", lambda e, th=th: e.copy(
                hsx[:, 512 * th:512 * (th + 1)], ps[2][0:128, 0:512]),
                r=[b_ps[2]], w=[b_hst])

        for isl in range(16):
            pend_tr = []
            dma("sp", slAv[:, :, :],
                d["w_gate"].ap().rearrange("(k p) e -> p k e", p=128)
                [:, :, 512 * isl:512 * (isl + 1)], w=[b_slA])
            dma("sp", slBv[:, :, :],
                d["w_up"].ap().rearrange("(k p) e -> p k e", p=128)
                [:, :, 512 * isl:512 * (isl + 1)], w=[b_slB])
            for th in range(CPT):
                for k in range(NKD):
                    S_.op("pe", lambda e, k=k, th=th: e.matmul(
                        ps[0][0:128, 0:512],
                        xnf[:, TT * k + 128 * th:TT * k + 128 * (th + 1)],
                        slabA[:, 512 * k:512 * (k + 1)],
                        start=(k == 0), stop=(k == NKD - 1)),
                        r=[b_xn[th], b_slA],
                        w=[b_ps[0]] if k in (0, NKD - 1) else ())
                S_.op("act", lambda e: e.activation(
                    tmp1[:, 0:512], ps[0][0:128, 0:512], AF.Silu),
                    r=[b_ps[0]], w=[b_t1])
                for k in range(NKD):
                    S_.op("pe", lambda e, k=k, th=th: e.matmul(
                        ps[1][0:128, 0:512],
                        xnf[:, TT * k + 128 * th:TT * k + 128 * (th + 1)],
                        slabB[:, 512 * k:512 * (k + 1)],
                        start=(k == 0), stop=(k == NKD - 1)),
                        r=[b_xn[th], b_slB],
                        w=[b_ps[1]] if k in (0, NKD - 1) else ())
                S_.op("dve", lambda e: e.tensor_tensor(
                    tmp1[:, 512:1024], ps[1][0:128, 0:512], tmp1[:, 0:512],
                    ALU.mult), r=[b_ps[1], b_t1], w=[b_t1])
                hv = tmp1[:, 512:1024]
                habs = tmp1[:, 1536:2048]
                hs = (tmp1[:, 1024:1536] if th % 2 == 0
                      else tmp2[:, 1536:2048])
                S_.op("act", lambda e, hv=hv, habs=habs: e.activation(
                    habs, hv, AF.Abs), r=[b_t1], w=[b_t1])
                S_.op("dve", lambda e, habs=habs: e.tensor_tensor(
                    tmp2[:, 0:128], habs[:, 0::4], habs[:, 1::4], ALU.max),
                    r=[b_t1], w=[b_t2])
                S_.op("dve", lambda e, habs=habs: e.tensor_tensor(
                    tmp2[:, 128:256], habs[:, 0::4], habs[:, 1::4], ALU.min),
                    r=[b_t1], w=[b_t2])
                S_.op("dve", lambda e, habs=habs: e.tensor_tensor(
                    tmp2[:, 256:384], habs[:, 2::4], habs[:, 3::4], ALU.max),
                    r=[b_t1], w=[b_t2])
                S_.op("dve", lambda e, habs=habs: e.tensor_tensor(
                    tmp2[:, 384:512], habs[:, 2::4], habs[:, 3::4], ALU.min),
                    r=[b_t1], w=[b_t2])
                S_.op("dve", lambda e: e.tensor_tensor(
                    tmp2[:, 512:640], tmp2[:, 0:128], tmp2[:, 256:384],
                    ALU.min), r=[b_t2], w=[b_t2])
                S_.op("dve", lambda e: e.tensor_tensor(
                    tmp2[:, 640:768], tmp2[:, 128:256], tmp2[:, 384:512],
                    ALU.max), r=[b_t2], w=[b_t2])
                S_.op("dve", lambda e: e.tensor_tensor(
                    tmp2[:, 768:896], tmp2[:, 512:640], tmp2[:, 640:768],
                    ALU.max), r=[b_t2], w=[b_t2])
                for kk in range(4):
                    S_.op("dve", lambda e, kk=kk, habs=habs: e.tensor_tensor(
                        tmp2[:, 1024:1536][:, kk::4],
                        habs[:, kk::4], tmp2[:, 768:896], ALU.is_ge),
                        r=[b_t1, b_t2], w=[b_t2])
                hsbuf = b_hs[th % 2]
                S_.op("dve", lambda e, hv=hv, hs=hs: e.tensor_tensor(
                    hs, hv, tmp2[:, 1024:1536], ALU.mult),
                    r=[b_t1, b_t2], w=[hsbuf])
                pend_tr.append((th, hs, hsbuf))
                if th > 0:
                    emit_hs_transpose(*pend_tr.pop(0))
            for hd in range(4):
                wi = hd % 2
                wdv = wbuf.ap().rearrange("p (b k f) -> p b k f", b=2, k=4)
                dma("sp", wdv[:, wi, :, :],
                    d["w_down"].ap().rearrange("(k p) e -> p k e", p=128)
                    [:, 4 * isl:4 * isl + 4, 512 * hd:512 * (hd + 1)],
                    w=[b_wb[wi]])
                for th in range(CPT):
                    if hd == 0 and th == 1 and pend_tr:
                        emit_hs_transpose(*pend_tr.pop(0))
                    pbk = 3 + (th % 2)
                    for ic in range(4):
                        S_.op("pe", lambda e, ic=ic, th=th, wi=wi, pbk=pbk,
                              wdv=wdv: e.matmul(
                                  ps[pbk][0:128, 0:512],
                                  hsx[:, 512 * th + 128 * ic:
                                      512 * th + 128 * (ic + 1)],
                                  wdv[:, wi, ic, :],
                                  start=(ic == 0), stop=(ic == 3)),
                              r=[b_hst, b_wb[wi]],
                              w=[b_ps[pbk]] if ic in (0, 3) else ())
                    gsl = gacc[:, DM * th + 512 * hd:DM * th + 512 * (hd + 1)]
                    if isl == 0:
                        S_.op("dve", lambda e, gsl=gsl, pbk=pbk: e.tensor_copy(
                            gsl, ps[pbk][0:128, 0:512]),
                            r=[b_ps[pbk]], w=[b_g[th]])
                    else:
                        S_.op("dve", lambda e, gsl=gsl, pbk=pbk:
                              e.tensor_tensor(gsl, ps[pbk][0:128, 0:512],
                                              gsl, ALU.add),
                              r=[b_ps[pbk], b_g[th]], w=[b_g[th]])

        S_.retag([b_hs[0], b_hs[1]], [b_t1, b_t2])
        S_.retag([b_slB], [b_pb[s][i] for s in range(2) for i in range(3)]
                 + [b_xbc[s][i] for s in range(2) for i in range(3)]
                 + b_acum + b_acumf + b_alast + b_ds + b_dcb + b_expa
                 + [b_aw[0], b_aw[1]])
        S_.retag([b_hst], [b_sst, b_sst2])
        if t == 0:
            halo_ln_aprep()
            S_.retag([b_hsc], [b_sst])
        for c in range(CPT):
            a_prep(c, lambda k, c=c:
                   xnf[:, TT * k + 128 * c:TT * k + 128 * (c + 1)], [b_xn[c]])
        def mamba_inproj(hp):
            st = hp % 2
            for which, e0 in ((0, 128 * hp), (1, 2048 * which if False else
                              2048 + 128 * hp), (2, 4096 + 128 * hp)):
                bi = (3 * hp + which) % 2
                load_wblk(bi, e0)
                pbank = which % 2
                if t == 0:
                    halo_proj(hp, which, bi)
                inproj_block(bi, pbank, 512,
                             lambda k: xnf[:, TT * k:TT * k + 512],
                             list(b_xn))
                gb = 16 * which + hp
                pbs = slabB[:, PBO(st, which):PBO(st, which) + 516]
                S_.op("dve", lambda e, pbs=pbs, gb=gb: e.tensor_copy(
                    pbs[:, 0:3], tails[:, 3 * gb:3 * gb + 3]),
                    r=[b_tails[gb]], w=[b_pb[st][which]])
                S_.op("act", lambda e, pbs=pbs, pbank=pbank: e.copy(
                    pbs[:, 3:515], ps[pbank][0:128, 0:512]),
                    r=[b_ps[pbank]], w=[b_pb[st][which]])
                S_.op("dve", lambda e, pbs=pbs, gb=gb: e.tensor_copy(
                    tails[:, 3 * gb:3 * gb + 3], pbs[:, 512:515]),
                    r=[b_pb[st][which]], w=[b_tails[gb]])
                conv_chain(st, which, gb, 512, off_in=0, off_out=0)

        def ssd_block(hp, next_first=False):
            st = hp % 2
            if hp == 0:
                for f in pre_stages(0):
                    f()
            for c in range(CPT):
                if c == 0:
                    mops = [souter(hp, 0, 1), souter(hp, 1, 1),
                            x_stage(hp, 0, 1), x_stage(hp, 1, 1)]
                elif c == 1:
                    mops = [souter(hp, 0, 2), souter(hp, 1, 2)]
                elif c == 2:
                    mops = [souter(hp, 0, 3), souter(hp, 1, 3)]
                    if next_first:
                        mops += [x_stage(hp + 1, 0, 0),
                                 x_stage(hp + 1, 1, 0)]
                elif next_first:
                    mops = [souter(hp + 1, 0, 0), souter(hp + 1, 1, 0)]
                else:
                    mops = ()
                ssd_chunk(hp, c, 128 * c, st=st, mset=c % 2, mops=mops)

        for hp in range(NHP):
            mamba_inproj(hp)
            if t == 0:
                halo_state(hp)
            if hp > 0:
                ssd_block(hp - 1, next_first=True)
        ssd_block(NHP - 1)

        S_.retag([b_pb[s][i] for s in range(2) for i in range(3)]
                 + [b_xbc[s][i] for s in range(2) for i in range(3)]
                 + b_acum + b_acumf + b_alast + b_ds + b_dcb + b_expa
                 + [b_aw[0], b_aw[1]],
                 [b_zf[0], b_zf[1], b_zf[2], b_zf[3]])
        S_.retag([b_sst, b_sst2, b_hsc, b_hxb[0], b_hxb[1], b_hssd,
                  b_hac, b_hds, b_hal], [b_zb])
        dma("sp", tmp3[0:1, 0:DM],
            d["z_bias"].ap().rearrange("(o n) -> o n", o=1), w=[b_t3])
        for j in range(4):
            pj = 6 + (j % 2)
            S_.op("pe", lambda e, j=j, pj=pj: e.matmul(
                ps[pj][0:128, 0:512], ones[0:1, :],
                tmp3[0:1, 512 * j:512 * (j + 1)], start=True, stop=True),
                r=[b_const, b_t3, b_ps[pj]], w=[b_ps[pj]])
            S_.op("dve", lambda e, j=j, pj=pj: e.tensor_copy(
                hsx[:, 512 * j:512 * (j + 1)], ps[pj][0:128, 0:512]),
                r=[b_ps[pj]], w=[b_zb])
        S_.retag([b_slA], [b_slA2[0], b_slA2[1]])
        for zs in range(4):
            for hh2 in range(2):
                dma("sp", slAv[:, 8 * hh2:8 * hh2 + 8, :],
                    d["w_int"].ap().rearrange("(k p) e -> p k e", p=128)
                    [:, 8 * hh2:8 * hh2 + 8,
                     6144 + 512 * zs:6144 + 512 * (zs + 1)], w=[b_slA2[hh2]])
            for c in range(CPT):
                pbk = c % 2
                for k in range(NKD):
                    S_.op("pe", lambda e, k=k, c=c, pbk=pbk: e.matmul(
                        ps[pbk][0:128, 0:512],
                        xnf[:, TT * k + 128 * c:TT * k + 128 * (c + 1)],
                        slabA[:, 512 * k:512 * (k + 1)],
                        start=(k == 0), stop=(k == NKD - 1)),
                        r=[b_xn[c], b_slA2[k // 8]],
                        w=[b_ps[pbk]] if k in (0, NKD - 1) else ())
                S_.op("dve", lambda e, zs=zs, c=c, pbk=pbk: e.tensor_tensor(
                    tmp3[:, 512 * (c % 4):512 * (c % 4) + 512],
                    ps[pbk][0:128, 0:512],
                    hsx[:, 512 * zs:512 * (zs + 1)], ALU.add),
                    r=[b_ps[pbk], b_zb], w=[b_t3])
                S_.op("act", lambda e, zs=zs, c=c: e.activation(
                    slabB[:, DM * c + 512 * zs:DM * c + 512 * (zs + 1)],
                    tmp3[:, 512 * (c % 4):512 * (c % 4) + 512], AF.Silu),
                    r=[b_t3], w=[b_zf[c]])

        S_.retag([b_zb], [b_rmw])
        dma("sp", tmp3[0:1, 0:DM],
            d["rms_w"].ap().rearrange("(o n) -> o n", o=1), w=[b_t3])
        for j in range(4):
            pj = 6 + (j % 2)
            S_.op("pe", lambda e, j=j, pj=pj: e.matmul(
                ps[pj][0:128, 0:512], ones[0:1, :],
                tmp3[0:1, 512 * j:512 * (j + 1)], start=True, stop=True),
                r=[b_const, b_t3, b_ps[pj]], w=[b_ps[pj]])
            S_.op("dve", lambda e, j=j, pj=pj: e.tensor_copy(
                hsx[:, 512 * j:512 * (j + 1)], ps[pj][0:128, 0:512]),
                r=[b_ps[pj]], w=[b_rmw])

        S_.retag(list(b_xn), list(b_ynf))

        def gating_tr(c):
            for k in range(NKD):
                pk = 6 + (k % 2)
                S_.op("pe", lambda e, k=k, pk=pk: e.transpose(
                    ps[pk][0:128, 0:128], tmp3[:, 128 * k:128 * (k + 1)],
                    ident[:, :]), r=[b_t3, b_const, b_ps[pk]], w=[b_ps[pk]])
                dst = xnf[:, TT * k + 128 * c:TT * k + 128 * (c + 1)]
                if k % 2 == 0:
                    S_.op("act", lambda e, dst=dst, pk=pk: e.copy(
                        dst, ps[pk][0:128, 0:128]), r=[b_ps[pk]], w=[b_ynf[c]])
                else:
                    S_.op("dve", lambda e, dst=dst, pk=pk: e.tensor_copy(
                        dst, ps[pk][0:128, 0:128]), r=[b_ps[pk]], w=[b_ynf[c]])

        def gating(c):
            S_.op("dve", lambda e, c=c: e.tensor_tensor(
                ytok[:, DM * c:DM * (c + 1)], slabB[:, DM * c:DM * (c + 1)],
                ytok[:, DM * c:DM * (c + 1)], ALU.mult),
                r=[b_zf[c], b_y[c]], w=[b_y[c]])
            S_.op("act", lambda e, c=c: e.activation(
                tmp3[:, :], ytok[:, DM * c:DM * (c + 1)], AF.Square,
                accum_out=stats[:, 1:2]), r=[b_y[c]], w=[b_t3, b_stats])
            S_.op("dve", lambda e: e.tensor_scalar(
                stats[:, 7:8], stats[:, 1:2], 1.0 / DM, 1e-5,
                ALU.mult, ALU.add), r=[b_stats], w=[b_stats])
            S_.op("act", lambda e: e.activation(
                stats[:, 5:6], stats[:, 7:8], AF.Sqrt),
                r=[b_stats], w=[b_stats])
            S_.op("dve", lambda e: e.reciprocal(stats[:, 6:7], stats[:, 5:6]),
                  r=[b_stats], w=[b_stats])
            S_.op("dve", lambda e, c=c: e.scalar_tensor_tensor(
                tmp3[:, :], ytok[:, DM * c:DM * (c + 1)], stats[:, 6:7],
                hsx[:, 0:DM], ALU.mult, ALU.mult),
                r=[b_y[c], b_stats, b_rmw], w=[b_t3])

        gating(0)
        gating_tr(0)
        S_.retag([b_y[c] for c in range(CPT)], [b_stk[c] for c in range(CPT)])
        for dsl in range(4):
            for hh2 in range(2):
                dma("sp", slAv[:, 8 * hh2:8 * hh2 + 8, :],
                    d["w_outt"].ap().rearrange("(k p) e -> p k e", p=128)
                    [:, 8 * hh2:8 * hh2 + 8, 512 * dsl:512 * (dsl + 1)],
                    w=[b_slA2[hh2]])
            for c in range(CPT):
                pbk = c % 2
                if dsl == 0 and c + 1 < CPT:
                    gating(c + 1)
                for k in range(NKD):
                    S_.op("pe", lambda e, k=k, c=c, pbk=pbk: e.matmul(
                        ps[pbk][0:128, 0:512],
                        xnf[:, TT * k + 128 * c:TT * k + 128 * (c + 1)],
                        slabA[:, 512 * k:512 * (k + 1)],
                        start=(k == 0), stop=(k == NKD - 1)),
                        r=[b_ynf[c], b_slA2[k // 8]],
                        w=[b_ps[pbk]] if k in (0, NKD - 1) else ())
                S_.op("act", lambda e, c=c, dsl=dsl, pbk=pbk: e.copy(
                    ytok[:, DM * c + 512 * dsl:DM * c + 512 * (dsl + 1)],
                    ps[pbk][0:128, 0:512]), r=[b_ps[pbk]], w=[b_stk[c]])
                if dsl == 0 and c + 1 < CPT:
                    gating_tr(c + 1)
                if dsl == 2 and t + 1 < n_tiles:
                    if c == 0:
                        S_.retag([b_zf[cc] for cc in range(CPT)], b_lnus)
                        S_.retag([b_rmw], [b_lnx])
                    if c >= 2:
                        cc = c - 2
                        ln_load(cc, d["u_own"].ap()
                                [TT * (t + 1) + 128 * cc:
                                 TT * (t + 1) + 128 * (cc + 1), :])
                if dsl == 3 and t + 1 < n_tiles:
                    emit_ln_chunk(t + 1, c, preloaded=True)
                    if c + 2 < CPT:
                        cc = c + 2
                        ln_load(cc, d["u_own"].ap()
                                [TT * (t + 1) + 128 * cc:
                                 TT * (t + 1) + 128 * (cc + 1), :])

        S_.retag([b_hxn, b_hpb[0], b_hpb[1], b_t1, b_t2], [b_t1, b_t2])
        for c in range(CPT):
            dma("act", tmp1[:, :],
                d["u_own"].ap()[TT * t + 128 * c:TT * t + 128 * (c + 1), :],
                w=[b_t1])
            S_.op("act", lambda e, c=c: e.activation(
                tmp3[:, :], gacc[:, DM * c:DM * (c + 1)], AF.Silu),
                r=[b_g[c]], w=[b_t3])
            S_.op("dve", lambda e, c=c: e.tensor_tensor(
                tmp2[:, :], tmp3[:, :], ytok[:, DM * c:DM * (c + 1)],
                ALU.mult), r=[b_t3, b_stk[c]], w=[b_t2])
            S_.op("dve", lambda e: e.tensor_tensor(
                tmp2[:, :], tmp1[:, :], tmp2[:, :], ALU.add),
                r=[b_t1, b_t2], w=[b_t2])
            od = dma("act",
                     d["out"].ap()[TT * t + 128 * c:TT * t + 128 * (c + 1), :],
                     tmp2[:, :], r=[b_t2])
            out_dmas.append(od)

    blk = es.enter_context(nc.Block())
    S_.finalize(blk, sem, qsem, out_dmas)
    es.close()
    return nc


_CACHE = {}


def _consts():
    c = {}
    c["ident"] = np.eye(128, dtype=np.float32)
    c["negtriu"] = -np.triu(np.ones((128, 128), dtype=np.float32))
    c["triladd"] = np.where(np.triu(np.ones((128, 128), dtype=bool)),
                            np.float32(0.0),
                            np.float32(-1e30)).astype(np.float32)
    s127 = np.zeros((128, 1), dtype=np.float32)
    s127[127, 0] = 1.0
    c["sel127"] = s127
    c["ones_row"] = np.ones((1, 128), dtype=np.float32)
    return c


def make_in_maps(u, w_int, w_outt, w_gate, w_up, w_down, conv_w, conv_b,
                 z_bias, rms_w, dvec, n_cores, tok):
    cns = _consts()
    in_maps = []
    for i in range(n_cores):
        m = {
            "u_own": np.ascontiguousarray(u[tok * i:tok * (i + 1)]),
            "u_halo": (np.zeros((256, DM), dtype=np.float32) if i == 0
                       else np.ascontiguousarray(u[tok * i - 256:tok * i])),
            "flag": (np.zeros((128, 1), dtype=np.float32) if i == 0
                     else np.ones((128, 1), dtype=np.float32)),
            "w_int": w_int, "w_outt": w_outt, "w_gate": w_gate,
            "w_up": w_up, "w_down": w_down,
            "conv_w": np.ascontiguousarray(
                conv_w.reshape(48, 128, 4).transpose(1, 0, 2).reshape(128, 192)),
            "conv_b": np.ascontiguousarray(conv_b.reshape(48, 128).T),
            "z_bias": z_bias, "rms_w": rms_w, "dvec": dvec,
        }
        m.update(cns)
        in_maps.append(m)
    return in_maps


def kernel(L0_attn_u, L0_attn_W_in, L0_attn_z_bias, L0_attn_conv_w,
           L0_attn_conv_b, L0_attn_D, L0_attn_rms_w, L0_attn_W_out,
           L0_ffn_w_gate, L0_ffn_w_up, L0_ffn_w_down):
    u = np.ascontiguousarray(np.asarray(L0_attn_u, dtype=np.float32)[0])
    w_int = np.ascontiguousarray(np.asarray(L0_attn_W_in, dtype=np.float32).T)
    w_outt = np.ascontiguousarray(np.asarray(L0_attn_W_out,
                                             dtype=np.float32).T)
    args = (u, w_int, w_outt,
            np.ascontiguousarray(np.asarray(L0_ffn_w_gate, dtype=np.float32)),
            np.ascontiguousarray(np.asarray(L0_ffn_w_up, dtype=np.float32)),
            np.ascontiguousarray(np.asarray(L0_ffn_w_down, dtype=np.float32)),
            np.ascontiguousarray(np.asarray(L0_attn_conv_w, dtype=np.float32)),
            np.ascontiguousarray(np.asarray(L0_attn_conv_b, dtype=np.float32)),
            np.ascontiguousarray(np.asarray(L0_attn_z_bias, dtype=np.float32)),
            np.ascontiguousarray(np.asarray(L0_attn_rms_w, dtype=np.float32)),
            np.ascontiguousarray(np.asarray(L0_attn_D, dtype=np.float32)))
    if "nc" not in _CACHE:
        _CACHE["nc"] = build_program(n_tiles=4)
    nc = _CACHE["nc"]
    TOK = 16384 // NCORES
    in_maps = make_in_maps(*args, NCORES, TOK)
    res = run_bass_kernel_spmd(nc, in_maps, core_ids=list(range(NCORES)))
    out = np.concatenate([np.asarray(r["out"]) for r in res.results], axis=0)
    return np.ascontiguousarray(out[None].astype(np.float32))


# version 20
# speedup vs baseline: 1.1171x; 1.1077x over previous
import numpy as np
from contextlib import ExitStack

import concourse.bass as bass
import concourse.mybir as mybir
from concourse.bass_utils import run_bass_kernel_spmd

F32 = mybir.dt.float32
AF = mybir.ActivationFunctionType
ALU = mybir.AluOpType
AXX = mybir.AxisListType.X

NCORES = 8
DM = 2048
NH = 32
HD = 64
NHP = NH // 2
CONVD = 6144
PROJ = 8224
FFI = 8192
CH = 128
TT = 512
CPT = TT // CH
NKD = DM // 128

ENGS = ("pe", "dve", "act", "pool", "sp")


class Buf:
    __slots__ = ("name", "w", "rs", "pending")

    def __init__(self, name):
        self.name = name
        self.w = None
        self.rs = []
        self.pending = set()


NRING = 8


class Op:
    __slots__ = ("eng", "fn", "deps", "dma", "need_inc", "cum", "qj", "sidx")

    def __init__(self, eng, fn, deps, dma):
        self.eng = eng
        self.fn = fn
        self.deps = deps
        self.dma = dma
        self.need_inc = False
        self.cum = 0
        self.qj = 0
        self.sidx = 0


class Sched:
    def __init__(self):
        self.ops = {e: [] for e in ENGS}
        self.all = []
        self.ring = {}
        self.qcnt = {}

    def op(self, eng, fn, r=(), w=(), deps=(), dma=False):
        ds = set()
        for b in r:
            if b.w is not None:
                ds.add(b.w)
            if b.pending:
                ds.update(b.pending)
                b.pending = set()
        for b in w:
            if b.w is not None:
                ds.add(b.w)
            ds.update(b.rs)
            if b.pending:
                ds.update(b.pending)
                b.pending = set()
        for dd in deps:
            if dd is not None:
                ds.add(dd)
        best = {}
        for dd in ds:
            key = (dd.eng, dd.qj, dd.dma)
            cur = best.get(key)
            if cur is None or dd.sidx > cur.sidx:
                best[key] = dd
        ds = set(best.values())
        o = Op(eng, fn, ds, dma)
        o.sidx = len(self.ops[eng])
        if dma:
            idx = self.qcnt.get(eng, 0)
            self.qcnt[eng] = idx + 1
            j = idx % NRING
            prev = self.ring.get((eng, j))
            if prev is not None:
                ds.add(prev)
            self.ring[(eng, j)] = o
            o.qj = j
            o.cum = 16 * (idx // NRING + 1)
            o.need_inc = True
        self.ops[eng].append(o)
        self.all.append(o)
        for b in r:
            b.rs.append(o)
        for b in w:
            b.w = o
            b.rs = []
        return o

    @staticmethod
    def retag(olds, news):
        ds = set()
        for b in olds:
            if b.w is not None:
                ds.add(b.w)
            ds.update(b.rs)
            ds.update(b.pending)
        for b in news:
            b.w = None
            b.rs = []
            b.pending = set(ds)

    def finalize(self, blk, sems, qsems, tail_waits):
        for o in self.all:
            for dd in o.deps:
                dd.need_inc = True
        for e in ENGS:
            c_cum = 0
            for o in self.ops[e]:
                if o.need_inc and not o.dma:
                    c_cum += 1
                    o.cum = c_cum
            assert c_cum < 60000, (e, c_cum)
        for (e, j), o in self.ring.items():
            assert o.cum < 60000

        def key_sem(dd):
            if dd.dma:
                return ("q", dd.eng, dd.qj), qsems[dd.eng][dd.qj]
            return dd.eng, sems[dd.eng]

        def emit_stream(e, eng):
            hw = {}
            for o in self.ops[e]:
                tgt = {}
                for dd in o.deps:
                    key, sem = key_sem(dd)
                    if tgt.get(key, (0, None))[0] < dd.cum:
                        tgt[key] = (dd.cum, sem)
                needed = [(v, sem) for key, (v, sem) in tgt.items()
                          if hw.get(key, 0) < v]
                for key, (v, sem) in tgt.items():
                    if hw.get(key, 0) < v:
                        hw[key] = v
                for v, sem in needed[:-1]:
                    eng.wait_ge(sem, v)
                inst = o.fn(eng)
                if needed:
                    inst._wait_ge(needed[-1][1], needed[-1][0])
                if o.need_inc:
                    if o.dma:
                        inst.then_inc(qsems[o.eng][o.qj], 16)
                    else:
                        inst.then_inc(sems[o.eng], 1)
            if e == "pool":
                for dd in tail_waits:
                    key, sem = key_sem(dd)
                    if hw.get(key, 0) < dd.cum:
                        hw[key] = dd.cum
                        eng.wait_ge(sem, dd.cum)

        blk.tensor(lambda eng: emit_stream("pe", eng))
        blk.vector(lambda eng: emit_stream("dve", eng))
        blk.scalar(lambda eng: emit_stream("act", eng))
        blk.gpsimd(lambda eng: emit_stream("pool", eng))
        blk.sync(lambda eng: emit_stream("sp", eng))


def build_program(n_tiles=4):
    TOK = n_tiles * TT
    nc = bass.Bass("TRN2", target_bir_lowering=False,
                   dynamic_dma_scratch_size=2048)

    d = {}
    for nm, shp in [("u_own", [TOK, DM]), ("u_halo", [256, DM]),
                    ("flag", [128, 1]), ("w_int", [DM, PROJ]),
                    ("w_outt", [DM, DM]), ("w_gate", [DM, FFI]),
                    ("w_up", [DM, FFI]), ("w_down", [FFI, DM]),
                    ("conv_w", [128, 192]), ("conv_b", [128, 48]),
                    ("z_bias", [DM]), ("rms_w", [DM]), ("dvec", [NH]),
                    ("ident", [128, 128]), ("negtriu", [128, 128]),
                    ("triladd", [128, 128]), ("sel127", [128, 1]),
                    ("ones_row", [1, 128])]:
        d[nm] = nc.dram_tensor(nm, shp, F32, kind="ExternalInput")
    d["out"] = nc.dram_tensor("out", [TOK, DM], F32, kind="ExternalOutput")

    es = ExitStack()
    sb = lambda name, shape: es.enter_context(nc.sbuf_tensor(name, shape, F32))

    xnf = sb("xnf", [128, NKD * TT])
    ytok = sb("ytok", [128, CPT * DM])
    gacc = sb("gacc", [128, CPT * DM])
    slabA = sb("slabA", [128, NKD * 512])
    slabB = sb("slabB", [128, NKD * 512])
    wbuf = sb("wbuf", [128, 2 * 2048])
    hsx = sb("hsx", [128, 2048])
    tmp1 = sb("tmp1", [128, DM])
    tmp2 = sb("tmp2", [128, DM])
    tmp3 = sb("tmp3", [128, DM])
    ident = sb("identc", [128, 128])
    negtriu = sb("negtriuc", [128, 128])
    triladd = sb("triladdc", [128, 128])
    sel127 = sb("sel127c", [128, 1])
    ones = sb("onesc", [1, 128])
    dbc = sb("dbcc", [128, NH])
    convw = sb("convwc", [128, 48 * 4])
    convb = sb("convbc", [128, 48])
    flagv = sb("flagc", [128, 1])
    wa = sb("wac", [128, NKD * NH])
    tails = sb("tailsc", [128, 48 * 3])
    S = sb("Sc", [128, NHP * HD])
    stats = sb("statsc", [128, 8])
    cbias = sb("cbiasc", [128, 2])
    acum = slabB[:, 6168:6424]
    acumf = slabB[0:32, 6424:6936]
    alast = slabB[0:1, 6936:7064]
    dsb = slabB[:, 7064:7192]
    dcb = slabB[:, 7192:7320]
    expa = slabB[:, 7320:7448]
    arow = slabB[0:1, 7448:7576]
    aw = (slabB[0:1, 7448:7704], slabB[0:1, 7704:7960])

    ps = [es.enter_context(nc.psum_tensor(f"ps{i}", [128, 512], F32))
          for i in range(8)]
    sem = {e: es.enter_context(nc.semaphore(f"sem_{e}")) for e in ENGS}
    qsem = {e: [es.enter_context(nc.semaphore(f"qsem_{e}{j}"))
                for j in range(8)]
            for e in ("sp", "act", "pool")}

    S_ = Sched()

    b_const = Buf("const")
    b_wa = Buf("wa")
    b_tails = [Buf(f"tl{i}") for i in range(48)]
    b_S = [Buf(f"S{h}") for h in range(NH)]
    b_acum = [Buf(f"ac{c}") for c in range(CPT)]
    b_acumf = [Buf(f"af{c}") for c in range(CPT)]
    b_alast = [Buf(f"al{c}") for c in range(CPT)]
    b_ds = [Buf(f"ds{c}") for c in range(CPT)]
    b_dcb = [Buf(f"dc{c}") for c in range(CPT)]
    b_expa = [Buf(f"ea{c}") for c in range(CPT)]
    b_arow = Buf("arow")
    b_aw = (Buf("aw0"), Buf("aw1"))
    b_ps7a = Buf("ps7a")
    b_ps7b = b_ps7a
    b_stats = Buf("stats")
    b_xn = [Buf(f"xn{c}") for c in range(CPT)]
    b_xnh = Buf("xnh")
    b_lnus = [Buf("lnu0"), Buf("lnu1")]
    b_lnx = Buf("lnx")
    b_ynf = [Buf(f"ynf{c}") for c in range(CPT)]
    b_y = [Buf(f"y{c}") for c in range(CPT)]
    b_stk = [Buf(f"stk{c}") for c in range(CPT)]
    b_g = [Buf(f"g{c}") for c in range(CPT)]
    b_slA = Buf("slA")
    b_slA2 = [Buf("slAh0"), Buf("slAh1")]
    b_slB = Buf("slB")
    b_pb = [[Buf(f"pb{s}{i}") for i in range(3)] for s in range(2)]
    b_xbc = [[Buf(f"xb{s}{i}") for i in range(3)] for s in range(2)]
    b_zf = [Buf(f"zf{c}") for c in range(CPT)]
    b_wb = [Buf("wb0"), Buf("wb1")]
    b_hst = Buf("hst")
    b_hs = [Buf("hs0"), Buf("hs1")]
    b_sst = Buf("sst")
    b_zb = Buf("zb")
    b_rmw = Buf("rmw")
    b_t1 = Buf("t1")
    b_t2 = Buf("t2")
    b_t3 = Buf("t3")
    b_ps = [Buf(f"p{i}") for i in range(8)]

    def dma(q, dst_ap, src_ap, r=(), w=(), deps=()):
        return S_.op(q, lambda e, a=dst_ap, b2=src_ap: e.dma_start(a, b2),
                     r=r, w=w, deps=deps, dma=True)

    XT2 = hsx[:, 0:128]
    BT2 = hsx[:, 128:256]
    MSK = hsx[:, 256:384]
    MTm = hsx[:, 384:512]
    BTS = hsx[:, 512:640]
    O1 = hsx[:, 640:704]
    E1 = hsx[:, 704:768]
    SPA = hsx[:, 768:800]
    SPB = hsx[:, 800:832]

    dma("sp", ident[:, :], d["ident"].ap(), w=[b_const])
    dma("sp", negtriu[:, :], d["negtriu"].ap(), w=[b_const])
    dma("sp", triladd[:, :], d["triladd"].ap(), w=[b_const])
    dma("sp", sel127[:, :], d["sel127"].ap(), w=[b_const])
    dma("sp", ones[:, :], d["ones_row"].ap(), w=[b_const])
    dma("sp", flagv[:, :], d["flag"].ap(), w=[b_const])
    dma("sp", convw[:, :], d["conv_w"].ap(), w=[b_const])
    dma("sp", convb[:, :], d["conv_b"].ap(), w=[b_const])
    dma("sp", wa.ap().rearrange("p (k f) -> p k f", k=NKD),
        d["w_int"].ap().rearrange("(k p) e -> p k e", p=128)[:, :, 8192:8224],
        w=[b_wa])
    dma("sp", tmp2[0:1, 0:NH],
        d["dvec"].ap().rearrange("(o n) -> o n", o=1), w=[b_t2])
    S_.op("dve", lambda e: e.memset(cbias[:, 0:1], 1e-5), w=[b_const])
    S_.op("dve", lambda e: e.memset(cbias[:, 1:2], 1.0), w=[b_const])
    S_.op("pe", lambda e: e.matmul(ps[2][0:128, 0:NH], ones[0:1, :],
                                   tmp2[0:1, 0:NH], start=True, stop=True),
          r=[b_const, b_t2], w=[b_ps[2]])
    S_.op("dve", lambda e: e.tensor_copy(dbc[:, :], ps[2][0:128, 0:NH]),
          r=[b_ps[2]], w=[b_const])

    lnus = (slabB[:, 0:2048], slabB[:, 2048:4096])
    lnx = hsx[:, 0:2048]

    def ln_load(c, u_src_ap):
        lnu = lnus[c % 2]
        lnub = b_lnus[c % 2]
        dma("act", lnu, u_src_ap, w=[lnub])

    def ln_chunk(c, u_src_ap, xn_out_fn, preloaded=False):
        lnu = lnus[c % 2]
        lnub = b_lnus[c % 2]
        if not preloaded:
            dma("act", lnu, u_src_ap, w=[lnub])
        S_.op("dve", lambda e, lnu=lnu: e.tensor_reduce(
            stats[:, 0:1], lnu, AXX, ALU.add), r=[lnub], w=[b_stats])
        S_.op("act", lambda e, lnu=lnu: e.activation(
            lnx, lnu, AF.Square, accum_out=stats[:, 1:2]),
            r=[lnub], w=[b_lnx, b_stats])
        S_.op("act", lambda e: e.activation(
            stats[:, 2:3], stats[:, 0:1], AF.Copy, scale=1.0 / DM),
            r=[b_stats], w=[b_stats])
        S_.op("act", lambda e: e.activation(
            stats[:, 3:4], stats[:, 2:3], AF.Square), r=[b_stats], w=[b_stats])
        S_.op("dve", lambda e: e.scalar_tensor_tensor(
            stats[:, 4:5], stats[:, 1:2], 1.0 / DM, stats[:, 3:4],
            ALU.mult, ALU.subtract), r=[b_stats], w=[b_stats])
        S_.op("dve", lambda e: e.tensor_scalar(
            stats[:, 7:8], stats[:, 4:5], 1e-5, None, ALU.add),
            r=[b_stats], w=[b_stats])
        S_.op("act", lambda e: e.activation(
            stats[:, 5:6], stats[:, 7:8], AF.Sqrt),
            r=[b_stats], w=[b_stats])
        S_.op("dve", lambda e: e.reciprocal(stats[:, 6:7], stats[:, 5:6]),
              r=[b_stats], w=[b_stats])
        for k in range(NKD):
            pk = 6 + (k % 2)
            sc = lnx[:, 128 * (k % 2):128 * (k % 2) + 128]
            S_.op("dve", lambda e, k=k, sc=sc, lnu=lnu, lnub=lnub:
                  e.tensor_scalar(
                      sc, lnu[:, 128 * k:128 * (k + 1)], stats[:, 2:3],
                      stats[:, 6:7], ALU.subtract, ALU.mult),
                  r=[lnub, b_stats], w=[b_lnx])
            S_.op("pe", lambda e, sc=sc, pk=pk: e.transpose(
                ps[pk][0:128, 0:128], sc, ident[:, :]),
                r=[b_lnx, b_const, b_ps[pk]], w=[b_ps[pk]])
            dst, wb = xn_out_fn(k)
            if k % 2 == 0:
                S_.op("act", lambda e, dst=dst, pk=pk: e.copy(
                    dst, ps[pk][0:128, 0:128]), r=[b_ps[pk]], w=[wb])
            else:
                S_.op("dve", lambda e, dst=dst, pk=pk: e.tensor_copy(
                    dst, ps[pk][0:128, 0:128]), r=[b_ps[pk]], w=[wb])

    def a_prep(c, xn_tile_fn, xn_deps):
        for k in range(NKD):
            S_.op("pe", lambda e, k=k: e.matmul(
                ps[5][0:128, 0:NH], xn_tile_fn(k), wa[:, 32 * k:32 * (k + 1)],
                start=(k == 0), stop=(k == NKD - 1)),
                r=list(xn_deps) + [b_wa],
                w=[b_ps[5]] if k in (0, NKD - 1) else ())
        S_.op("act", lambda e: e.activation(SPA, ps[5][0:128, 0:NH], AF.Exp),
              r=[b_ps[5]], w=[b_sst])
        S_.op("act", lambda e: e.activation(SPB, SPA, AF.Ln,
                                            bias=cbias[:, 1:2]),
              r=[b_sst, b_const], w=[b_sst])
        S_.op("pe", lambda e: e.matmul(
            ps[5][0:128, 0:NH], negtriu[:, :], SPB, start=True, stop=True),
            r=[b_const, b_sst, b_ps[5]], w=[b_ps[5]])
        S_.op("dve", lambda e, c=c: e.tensor_copy(
            acum[:, 64 * c + 32:64 * c + 64], ps[5][0:128, 0:NH]),
            r=[b_ps[5]], w=[b_acum[c]])
        S_.op("pe", lambda e, c=c: e.transpose(
            ps[6][0:64, 0:128], acum[:, 64 * c:64 * c + 64], ident[:, :]),
            r=[b_acum[c], b_const, b_ps[6]], w=[b_ps[6]])
        S_.op("dve", lambda e, c=c: e.tensor_copy(
            acumf[0:32, 128 * c:128 * (c + 1)], ps[6][32:64, 0:128]),
            r=[b_ps[6]], w=[b_acumf[c]])
        S_.op("pe", lambda e, c=c: e.matmul(
            ps[7][0:1, 0:NH], sel127[:, :], acum[:, 64 * c + 32:64 * c + 64],
            start=True, stop=True),
            r=[b_const, b_acum[c], b_ps7a], w=[b_ps7a])
        S_.op("dve", lambda e, c=c: e.tensor_copy(
            alast[0:1, 32 * c:32 * (c + 1)], ps[7][0:1, 0:NH]),
            r=[b_ps7a], w=[b_alast[c]])
        S_.op("pe", lambda e, c=c: e.matmul(
            ps[7][0:128, 0:NH], ones[0:1, :], alast[0:1, 32 * c:32 * (c + 1)],
            start=True, stop=True),
            r=[b_const, b_alast[c], b_ps7a], w=[b_ps7a])
        S_.op("dve", lambda e, c=c: e.tensor_tensor(
            dsb[:, 32 * c:32 * (c + 1)], ps[7][0:128, 0:NH],
            acum[:, 64 * c + 32:64 * c + 64], ALU.subtract),
            r=[b_ps7a, b_acum[c]], w=[b_ds[c]])
        S_.op("act", lambda e, c=c: e.activation(
            dsb[:, 32 * c:32 * (c + 1)], dsb[:, 32 * c:32 * (c + 1)], AF.Exp),
            r=[b_ds[c]], w=[b_ds[c]])
        S_.op("act", lambda e, c=c: e.activation(
            dcb[:, 32 * c:32 * (c + 1)], ps[7][0:128, 0:NH], AF.Exp),
            r=[b_ps7a], w=[b_dcb[c]])
        S_.op("act", lambda e, c=c: e.activation(
            expa[:, 32 * c:32 * (c + 1)], acum[:, 64 * c + 32:64 * c + 64],
            AF.Exp), r=[b_acum[c]], w=[b_expa[c]])

    wbv = wbuf.ap().rearrange("p (b q) -> p b q", b=2)

    def load_wblk(bi, e0, width=128):
        src = d["w_int"].ap().rearrange("(k p) e -> p k e", p=128)
        dst = wbuf.ap().rearrange("p (b k f) -> p b k f", b=2, k=NKD)
        return dma("sp", dst[:, bi, :, 0:width], src[:, :, e0:e0 + width],
                   w=[b_wb[bi]])

    def inproj_block(bi, pbank, toks_n, xn_tile_fn, xn_deps):
        wv = wbuf.ap().rearrange("p (b k f) -> p b k f", b=2, k=NKD)
        for k in range(NKD):
            S_.op("pe", lambda e, k=k: e.matmul(
                ps[pbank][0:128, 0:toks_n], wv[:, bi, k, 0:128],
                xn_tile_fn(k), start=(k == 0), stop=(k == NKD - 1)),
                r=list(xn_deps) + [b_wb[bi]],
                w=[b_ps[pbank]] if k in (0, NKD - 1) else ())

    def PBO(st, which):
        return 3084 * st + 516 * which

    def XBO(st, which):
        return 3084 * st + 1548 + 512 * which

    def conv_chain(st, which, gb, n_tok, off_in=0, off_out=0):
        pbs = slabB[:, PBO(st, which):PBO(st, which) + 516]
        xo = slabB[:, XBO(st, which):XBO(st, which) + 512]
        S_.op("dve", lambda e: e.tensor_scalar(
            xo[:, off_out:off_out + n_tok], pbs[:, off_in:off_in + n_tok],
            convw[:, 4 * gb:4 * gb + 1], None, ALU.mult),
            r=[b_pb[st][which], b_const], w=[b_xbc[st][which]])
        for wt in (1, 2, 3):
            S_.op("dve", lambda e, wt=wt: e.scalar_tensor_tensor(
                xo[:, off_out:off_out + n_tok],
                pbs[:, off_in + wt:off_in + wt + n_tok],
                convw[:, 4 * gb + wt:4 * gb + wt + 1],
                xo[:, off_out:off_out + n_tok], ALU.mult, ALU.add),
                r=[b_pb[st][which], b_const, b_xbc[st][which]],
                w=[b_xbc[st][which]])
        S_.op("act", lambda e: e.activation(
            xo[:, off_out:off_out + n_tok], xo[:, off_out:off_out + n_tok],
            AF.Silu, bias=convb[:, gb:gb + 1]),
            r=[b_xbc[st][which], b_const], w=[b_xbc[st][which]])

    def x_stage(hp, hl, cpair):
        h = 2 * hp + hl

        def f(h=h, hl=hl, cpair=cpair):
            S_.op("pe", lambda e: e.matmul(
                ps[2][0:1, 256:512], ident[0:32, h:h + 1],
                acumf[0:32, 256 * cpair:256 * cpair + 256],
                start=True, stop=True),
                r=[b_const, b_acumf[2 * cpair], b_acumf[2 * cpair + 1],
                   b_ps[2]], w=[b_ps[2]])
            S_.op("act", lambda e: e.activation(
                aw[hl], ps[2][0:1, 256:512], AF.Copy),
                r=[b_ps[2]], w=[b_aw[hl]])
        return f

    def souter(hp, hl, c):
        h = 2 * hp + hl
        msks = (MSK, MSK2) if c % 2 == 0 else (MSKB, MSK2B)
        msk = msks[hl]
        po = 128 * hl
        pbf = b_ps7a if hl == 0 else b_ps7b

        def f(h=h, hl=hl, c=c, msk=msk, po=po, pbf=pbf):
            S_.op("pe", lambda e: e.matmul(
                ps[7][0:128, po:po + 128], ones[0:1, :],
                aw[hl][0:1, 128 * (c % 2):128 * (c % 2) + 128],
                start=True, stop=True),
                r=[b_const, b_aw[hl], pbf], w=[pbf])
            S_.op("dve", lambda e: e.scalar_tensor_tensor(
                msk, ps[7][0:128, po:po + 128],
                acum[:, 64 * c + 32 + h:64 * c + 32 + h + 1],
                triladd[:, :], ALU.subtract, ALU.add),
                r=[pbf, b_acum[c], b_const], w=[b_sst])
            S_.op("act", lambda e: e.activation(msk, msk, AF.Exp),
                  r=[b_sst], w=[b_sst])
        return f

    def pre_stages(hp):
        return [x_stage(hp, 0, 0), x_stage(hp, 1, 0),
                souter(hp, 0, 0), souter(hp, 1, 0)]

    def ssd_chunk(hp, c, x_off, st=0, mset=0, mops=()):
        mops = list(mops)

        def mpop():
            if mops:
                mops.pop(0)()
        h0, h1 = 2 * hp, 2 * hp + 1
        msks = (MSK, MSK2) if mset == 0 else (MSKB, MSK2B)
        xa = slabB[:, XBO(st, 0) + x_off:XBO(st, 0) + x_off + 128]
        ba = slabB[:, XBO(st, 1) + x_off:XBO(st, 1) + x_off + 128]
        ca = slabB[:, XBO(st, 2) + x_off:XBO(st, 2) + x_off + 128]
        S_.op("pe", lambda e: e.transpose(ps[2][0:128, 0:128], xa, ident[:, :]),
              r=[b_xbc[st][0], b_const, b_ps[2]], w=[b_ps[2]])
        S_.op("dve", lambda e: e.tensor_copy(XT2, ps[2][0:128, 0:128]),
              r=[b_ps[2]], w=[b_sst])
        S_.op("pe", lambda e: e.transpose(ps[6][0:128, 0:128], ba, ident[:, :]),
              r=[b_xbc[st][1], b_const, b_ps[6]], w=[b_ps[6]])
        S_.op("act", lambda e: e.activation(BT2, ps[6][0:128, 0:128], AF.Copy),
              r=[b_ps[6]], w=[b_sst])
        mpop()
        for hl, h in ((0, h0), (1, h1)):
            hh = 64 * hl
            scol = S[hh:hh + 64, HD * hp:HD * hp + 64]
            msk = msks[hl]
            S_.op("pe", lambda e, hh=hh: e.matmul(
                ps[3][0:128, 0:128], ba[hh:hh + 64, :], ca[hh:hh + 64, :],
                start=True, stop=True),
                r=[b_xbc[st][1], b_xbc[st][2], b_ps[3]], w=[b_ps[3]])
            mpop()
            S_.op("dve", lambda e, msk=msk: e.tensor_tensor(
                MTm, ps[3][0:128, 0:128], msk, ALU.mult),
                r=[b_ps[3], b_sst], w=[b_sst])
            S_.op("pe", lambda e, hh=hh, scol=scol, h=h: e.matmul(
                ps[4][0:128, 0:64], ca[hh:hh + 64, :], scol,
                start=True, stop=True),
                r=[b_xbc[st][2], b_S[h], b_ps[4]], w=[b_ps[4]])
            if hl == 0:
                mpop()
            S_.op("act", lambda e, h=h, c=c: e.activation(
                O1, ps[4][0:128, 0:64], AF.Copy,
                scale=expa[:, 32 * c + h:32 * c + h + 1]),
                r=[b_ps[4], b_expa[c]], w=[b_sst])
            S_.op("dve", lambda e, h=h, hh=hh: e.scalar_tensor_tensor(
                E1, XT2[:, hh:hh + 64], dbc[:, h:h + 1], O1,
                ALU.mult, ALU.add), r=[b_sst, b_const], w=[b_sst])
            S_.op("pe", lambda e, hh=hh: e.matmul(
                ps[5][0:128, 0:64], MTm, XT2[:, hh:hh + 64],
                start=True, stop=True), r=[b_sst, b_ps[5]], w=[b_ps[5]])
            yslice = ytok[:, DM * c + 64 * h:DM * c + 64 * h + 64]
            S_.op("dve", lambda e, yslice=yslice: e.tensor_tensor(
                yslice, ps[5][0:128, 0:64], E1, ALU.add),
                r=[b_ps[5], b_sst], w=[b_y[c]])
        while mops:
            mpop()
        for hl, h in ((0, h0), (1, h1)):
            hh = 64 * hl
            S_.op("act", lambda e, hh=hh, h=h, c=c: e.activation(
                BTS[:, hh:hh + 64], BT2[:, hh:hh + 64], AF.Copy,
                scale=dsb[:, 32 * c + h:32 * c + h + 1]),
                r=[b_sst, b_ds[c]], w=[b_sst])
        S_.op("pe", lambda e: e.matmul(
            ps[6][0:128, 0:128], BTS, XT2, start=True, stop=True),
            r=[b_sst, b_ps[6]], w=[b_ps[6]])
        for hl, h in ((0, h0), (1, h1)):
            hh = 64 * hl
            scol = S[hh:hh + 64, HD * hp:HD * hp + 64]
            pview = ps[6][hh:hh + 64, hh:hh + 64]
            S_.op("dve", lambda e, pview=pview, scol=scol, hh=hh, h=h, c=c:
                  e.scalar_tensor_tensor(
                      scol, scol,
                      dcb[hh:hh + 64, 32 * c + h:32 * c + h + 1],
                      pview, ALU.mult, ALU.add),
                  r=[b_S[h], b_dcb[c], b_ps[6]], w=[b_S[h]])

    ssd_tmp = hsx[:, 832:896]
    MSK2 = hsx[:, 896:1024]
    MSKB = hsx[:, 1024:1152]
    MSK2B = hsx[:, 1152:1280]
    arow2 = slabB[0:1, 7576:7704]
    b_sst2 = Buf("sst2")
    b_arow2 = Buf("arow2")
    hpb = (tmp1[:, 1048:1568], tmp2[:, 1048:1568])
    hxb = (hsx[:, 1280:1408], hsx[:, 1408:1536])
    hsc = (hsx[:, 1024:1152], hsx[:, 1152:1280])
    hXT = hsx[:, 1536:1664]
    hBT = hsx[:, 1664:1792]
    hBTS = hsx[:, 1792:1920]
    hacum = hsx[:, 1920:1952]
    hds = hsx[:, 1952:1984]
    halast = hsx[0:1, 1984:2016]
    hspa = tmp3[:, 0:32]
    hspb = tmp3[:, 32:64]
    b_hxn = Buf("hxn")
    b_hsc = Buf("hsc")
    b_hpb = [Buf("hpb0"), Buf("hpb1")]
    b_hxb = [Buf("hxb0"), Buf("hxb1")]
    b_hssd = Buf("hssd")
    b_hac = Buf("hac")
    b_hds = Buf("hds")
    b_hal = Buf("hal")

    def hxn_tile(k, lo, hi):
        base = tmp2 if k < 8 else tmp1
        kk = k % 8
        return base[:, 131 * kk + (lo - 125):131 * kk + (hi - 125)]

    def halo_ln_aprep():
        sink = slabB[:, 0:2048]
        sinkw = [b_pb[0][0], b_pb[0][1], b_pb[0][2], b_xbc[0][0]]
        for hc in range(2):
            lo = 125 if hc == 0 else 128
            dma("act", tmp3[:, :],
                d["u_halo"].ap()[128 * hc:128 * (hc + 1), :], w=[b_t3])
            S_.op("dve", lambda e: e.tensor_reduce(
                stats[:, 0:1], tmp3[:, :], AXX, ALU.add),
                r=[b_t3], w=[b_stats])
            S_.op("act", lambda e: e.activation(
                sink, tmp3[:, :], AF.Square, accum_out=stats[:, 1:2]),
                r=[b_t3], w=sinkw + [b_stats])
            S_.op("act", lambda e: e.activation(
                stats[:, 2:3], stats[:, 0:1], AF.Copy, scale=1.0 / DM),
                r=[b_stats], w=[b_stats])
            S_.op("act", lambda e: e.activation(
                stats[:, 3:4], stats[:, 2:3], AF.Square),
                r=[b_stats], w=[b_stats])
            S_.op("dve", lambda e: e.scalar_tensor_tensor(
                stats[:, 4:5], stats[:, 1:2], 1.0 / DM, stats[:, 3:4],
                ALU.mult, ALU.subtract), r=[b_stats], w=[b_stats])
            S_.op("dve", lambda e: e.tensor_scalar(
                stats[:, 7:8], stats[:, 4:5], 1e-5, None, ALU.add),
                r=[b_stats], w=[b_stats])
            S_.op("act", lambda e: e.activation(
                stats[:, 5:6], stats[:, 7:8], AF.Sqrt),
                r=[b_stats], w=[b_stats])
            S_.op("dve", lambda e: e.reciprocal(stats[:, 6:7], stats[:, 5:6]),
                  r=[b_stats], w=[b_stats])
            for k in range(NKD):
                sc = hsc[k % 2]
                pk = 6 + (k % 2)
                S_.op("dve", lambda e, k=k, sc=sc: e.tensor_scalar(
                    sc, tmp3[:, 128 * k:128 * (k + 1)], stats[:, 2:3],
                    stats[:, 6:7], ALU.subtract, ALU.mult),
                    r=[b_t3, b_stats], w=[b_hsc])
                S_.op("pe", lambda e, sc=sc, pk=pk: e.transpose(
                    ps[pk][0:128, 0:128], sc, ident[:, :]),
                    r=[b_hsc, b_const, b_ps[pk]], w=[b_ps[pk]])
                dst = hxn_tile(k, lo, 128 * (hc + 1))
                srcc = ps[pk][0:128, (lo - 128 * hc):128]
                if k % 2 == 0:
                    S_.op("act", lambda e, dst=dst, srcc=srcc: e.copy(
                        dst, srcc), r=[b_ps[pk]], w=[b_hxn])
                else:
                    S_.op("dve", lambda e, dst=dst, srcc=srcc: e.tensor_copy(
                        dst, srcc), r=[b_ps[pk]], w=[b_hxn])
        for k in range(NKD):
            S_.op("pe", lambda e, k=k: e.matmul(
                ps[5][0:128, 0:NH], hxn_tile(k, 128, 256),
                wa[:, 32 * k:32 * (k + 1)],
                start=(k == 0), stop=(k == NKD - 1)),
                r=[b_hxn, b_wa],
                w=[b_ps[5]] if k in (0, NKD - 1) else ())
        S_.op("act", lambda e: e.activation(hspa, ps[5][0:128, 0:NH], AF.Exp),
              r=[b_ps[5], b_t3], w=[b_t3])
        S_.op("act", lambda e: e.activation(hspb, hspa, AF.Ln,
                                            bias=cbias[:, 1:2]),
              r=[b_t3, b_const], w=[b_t3])
        S_.op("pe", lambda e: e.matmul(
            ps[5][0:128, 0:NH], negtriu[:, :], hspb, start=True, stop=True),
            r=[b_const, b_t3, b_ps[5]], w=[b_ps[5]])
        S_.op("dve", lambda e: e.tensor_copy(hacum, ps[5][0:128, 0:NH]),
              r=[b_ps[5]], w=[b_hac])
        S_.op("pe", lambda e: e.matmul(
            ps[7][0:1, 0:NH], sel127[:, :], hacum, start=True, stop=True),
            r=[b_const, b_hac, b_ps7a], w=[b_ps7a])
        S_.op("dve", lambda e: e.tensor_copy(halast, ps[7][0:1, 0:NH]),
              r=[b_ps7a], w=[b_hal])
        S_.op("pe", lambda e: e.matmul(
            ps[7][0:128, 0:NH], ones[0:1, :], halast, start=True, stop=True),
            r=[b_const, b_hal, b_ps7a], w=[b_ps7a])
        S_.op("dve", lambda e: e.tensor_tensor(
            hds, ps[7][0:128, 0:NH], hacum, ALU.subtract),
            r=[b_ps7a, b_hac], w=[b_hds])
        S_.op("act", lambda e: e.activation(hds, hds, AF.Exp),
              r=[b_hds], w=[b_hds])

    def halo_proj(hp, which, bi):
        wv = wbuf.ap().rearrange("p (b k f) -> p b k f", b=2, k=NKD)
        gb = 16 * which + hp
        if which < 2:
            for k in range(NKD):
                S_.op("pe", lambda e, k=k: e.matmul(
                    ps[5][0:128, 0:131], wv[:, bi, k, 0:128],
                    hxn_tile(k, 125, 256),
                    start=(k == 0), stop=(k == NKD - 1)),
                    r=[b_hxn, b_wb[bi]],
                    w=[b_ps[5]] if k in (0, NKD - 1) else ())
            pbs = hpb[which]
            xo = hxb[which]
            S_.op("act", lambda e, pbs=pbs: e.copy(
                pbs[:, 0:131], ps[5][0:128, 0:131]),
                r=[b_ps[5]], w=[b_hpb[which]])
            S_.op("dve", lambda e, pbs=pbs, xo=xo, gb=gb: e.tensor_scalar(
                xo[:, 0:128], pbs[:, 0:128], convw[:, 4 * gb:4 * gb + 1],
                None, ALU.mult), r=[b_hpb[which], b_const], w=[b_hxb[which]])
            for wt in (1, 2, 3):
                S_.op("dve", lambda e, pbs=pbs, xo=xo, gb=gb, wt=wt:
                      e.scalar_tensor_tensor(
                          xo[:, 0:128], pbs[:, wt:wt + 128],
                          convw[:, 4 * gb + wt:4 * gb + wt + 1], xo[:, 0:128],
                          ALU.mult, ALU.add),
                      r=[b_hpb[which], b_const, b_hxb[which]],
                      w=[b_hxb[which]])
            S_.op("act", lambda e, xo=xo, gb=gb: e.activation(
                xo[:, 0:128], xo[:, 0:128], AF.Silu,
                bias=convb[:, gb:gb + 1]),
                r=[b_hxb[which], b_const], w=[b_hxb[which]])
            S_.op("dve", lambda e, pbs=pbs, gb=gb: e.tensor_copy(
                tails[:, 3 * gb:3 * gb + 3], pbs[:, 128:131]),
                r=[b_hpb[which]], w=[b_tails[gb]])
        else:
            for k in range(NKD):
                S_.op("pe", lambda e, k=k: e.matmul(
                    ps[5][0:128, 0:3], wv[:, bi, k, 0:128],
                    hxn_tile(k, 253, 256),
                    start=(k == 0), stop=(k == NKD - 1)),
                    r=[b_hxn, b_wb[bi]],
                    w=[b_ps[5]] if k in (0, NKD - 1) else ())
            S_.op("dve", lambda e, gb=gb: e.tensor_copy(
                tails[:, 3 * gb:3 * gb + 3], ps[5][0:128, 0:3]),
                r=[b_ps[5]], w=[b_tails[gb]])

    def halo_state(hp):
        h0, h1 = 2 * hp, 2 * hp + 1
        S_.op("pe", lambda e: e.transpose(
            ps[5][0:128, 0:128], hxb[0], ident[:, :]),
            r=[b_hxb[0], b_const, b_ps[5]], w=[b_ps[5]])
        S_.op("dve", lambda e: e.tensor_copy(hXT, ps[5][0:128, 0:128]),
              r=[b_ps[5]], w=[b_hssd])
        S_.op("pe", lambda e: e.transpose(
            ps[5][0:128, 0:128], hxb[1], ident[:, :]),
            r=[b_hxb[1], b_const, b_ps[5]], w=[b_ps[5]])
        S_.op("dve", lambda e: e.tensor_copy(hBT, ps[5][0:128, 0:128]),
              r=[b_ps[5]], w=[b_hssd])
        for hl, h in ((0, h0), (1, h1)):
            hh = 64 * hl
            S_.op("dve", lambda e, hh=hh, h=h: e.tensor_scalar(
                hBTS[:, hh:hh + 64], hBT[:, hh:hh + 64],
                hds[:, h:h + 1], None, ALU.mult),
                r=[b_hssd, b_hds], w=[b_hssd])
        S_.op("pe", lambda e: e.matmul(
            ps[5][0:128, 0:128], hBTS, hXT, start=True, stop=True),
            r=[b_hssd, b_ps[5]], w=[b_ps[5]])
        for hl, h in ((0, h0), (1, h1)):
            hh = 64 * hl
            scol = S[hh:hh + 64, HD * hp:HD * hp + 64]
            S_.op("dve", lambda e, hh=hh, scol=scol: e.tensor_scalar(
                scol, ps[5][hh:hh + 64, hh:hh + 64], flagv[hh:hh + 64, 0:1],
                None, ALU.mult), r=[b_ps[5], b_const], w=[b_S[h]])

    out_dmas = []

    def emit_ln_chunk(t, c, preloaded=False):
        S_.retag([b_ynf[c]], [b_xn[c]])
        ln_chunk(c,
                 d["u_own"].ap()[TT * t + 128 * c:TT * t + 128 * (c + 1), :],
                 lambda k, c=c: (
                     xnf[:, TT * k + 128 * c:TT * k + 128 * c + 128],
                     b_xn[c]), preloaded=preloaded)

    def emit_ln(t):
        for c in range(CPT):
            emit_ln_chunk(t, c)

    S_.retag([b_pb[0][i] for i in range(3)] + [b_xbc[0][i] for i in range(3)],
             b_lnus)
    S_.retag([b_sst, b_sst2], [b_lnx])
    emit_ln(0)
    for t in range(n_tiles):

        S_.retag([b_zf[c] for c in range(CPT)] + b_lnus
                 + [b_pb[s][i] for s in range(2) for i in range(3)]
                 + [b_xbc[s][i] for s in range(2) for i in range(3)]
                 + b_acum + b_acumf + b_alast
                 + b_ds + b_dcb + b_expa + [b_aw[0], b_aw[1]], [b_slB])
        S_.retag([b_sst, b_sst2, b_zb, b_rmw, b_lnx], [b_hst])
        S_.retag([b_slA2[0], b_slA2[1]], [b_slA])
        S_.retag([b_wb[0], b_wb[1]], [b_wb[0], b_wb[1]])
        slBv = slabB.ap().rearrange("p (k f) -> p k f", k=NKD)
        slAv = slabA.ap().rearrange("p (k f) -> p k f", k=NKD)

        def emit_hs_transpose(th, hs, hsbuf):
            for ic in range(4):
                S_.op("pe", lambda e, ic=ic, hs=hs: e.matmul(
                    ps[2][0:128, 128 * ic:128 * (ic + 1)],
                    hs[:, 128 * ic:128 * (ic + 1)], ident[:, :],
                    start=(ic == 0), stop=(ic == 3), is_transpose=True),
                    r=[hsbuf, b_const],
                    w=[b_ps[2]] if ic in (0, 3) else ())
            S_.op("act", lambda e, th=th: e.copy(
                hsx[:, 512 * th:512 * (th + 1)], ps[2][0:128, 0:512]),
                r=[b_ps[2]], w=[b_hst])

        for isl in range(16):
            pend_tr = []
            dma("sp", slAv[:, :, :],
                d["w_gate"].ap().rearrange("(k p) e -> p k e", p=128)
                [:, :, 512 * isl:512 * (isl + 1)], w=[b_slA])
            dma("sp", slBv[:, :, :],
                d["w_up"].ap().rearrange("(k p) e -> p k e", p=128)
                [:, :, 512 * isl:512 * (isl + 1)], w=[b_slB])
            for th in range(CPT):
                for k in range(NKD):
                    S_.op("pe", lambda e, k=k, th=th: e.matmul(
                        ps[0][0:128, 0:512],
                        xnf[:, TT * k + 128 * th:TT * k + 128 * (th + 1)],
                        slabA[:, 512 * k:512 * (k + 1)],
                        start=(k == 0), stop=(k == NKD - 1)),
                        r=[b_xn[th], b_slA],
                        w=[b_ps[0]] if k in (0, NKD - 1) else ())
                S_.op("act", lambda e: e.activation(
                    tmp1[:, 0:512], ps[0][0:128, 0:512], AF.Silu),
                    r=[b_ps[0]], w=[b_t1])
                for k in range(NKD):
                    S_.op("pe", lambda e, k=k, th=th: e.matmul(
                        ps[1][0:128, 0:512],
                        xnf[:, TT * k + 128 * th:TT * k + 128 * (th + 1)],
                        slabB[:, 512 * k:512 * (k + 1)],
                        start=(k == 0), stop=(k == NKD - 1)),
                        r=[b_xn[th], b_slB],
                        w=[b_ps[1]] if k in (0, NKD - 1) else ())
                S_.op("dve", lambda e: e.tensor_tensor(
                    tmp1[:, 512:1024], ps[1][0:128, 0:512], tmp1[:, 0:512],
                    ALU.mult), r=[b_ps[1], b_t1], w=[b_t1])
                hv = tmp1[:, 512:1024]
                habs = tmp1[:, 1536:2048]
                hs = (tmp1[:, 1024:1536] if th % 2 == 0
                      else tmp2[:, 1536:2048])
                S_.op("act", lambda e, hv=hv, habs=habs: e.activation(
                    habs, hv, AF.Abs), r=[b_t1], w=[b_t1])
                S_.op("dve", lambda e, habs=habs: e.tensor_tensor(
                    tmp2[:, 0:128], habs[:, 0::4], habs[:, 1::4], ALU.max),
                    r=[b_t1], w=[b_t2])
                S_.op("dve", lambda e, habs=habs: e.tensor_tensor(
                    tmp2[:, 128:256], habs[:, 0::4], habs[:, 1::4], ALU.min),
                    r=[b_t1], w=[b_t2])
                S_.op("dve", lambda e, habs=habs: e.tensor_tensor(
                    tmp2[:, 256:384], habs[:, 2::4], habs[:, 3::4], ALU.max),
                    r=[b_t1], w=[b_t2])
                S_.op("dve", lambda e, habs=habs: e.tensor_tensor(
                    tmp2[:, 384:512], habs[:, 2::4], habs[:, 3::4], ALU.min),
                    r=[b_t1], w=[b_t2])
                S_.op("dve", lambda e: e.tensor_tensor(
                    tmp2[:, 512:640], tmp2[:, 0:128], tmp2[:, 256:384],
                    ALU.min), r=[b_t2], w=[b_t2])
                S_.op("dve", lambda e: e.tensor_tensor(
                    tmp2[:, 640:768], tmp2[:, 128:256], tmp2[:, 384:512],
                    ALU.max), r=[b_t2], w=[b_t2])
                S_.op("dve", lambda e: e.tensor_tensor(
                    tmp2[:, 768:896], tmp2[:, 512:640], tmp2[:, 640:768],
                    ALU.max), r=[b_t2], w=[b_t2])
                for kk in range(4):
                    S_.op("dve", lambda e, kk=kk, habs=habs: e.tensor_tensor(
                        tmp2[:, 1024:1536][:, kk::4],
                        habs[:, kk::4], tmp2[:, 768:896], ALU.is_ge),
                        r=[b_t1, b_t2], w=[b_t2])
                hsbuf = b_hs[th % 2]
                S_.op("dve", lambda e, hv=hv, hs=hs: e.tensor_tensor(
                    hs, hv, tmp2[:, 1024:1536], ALU.mult),
                    r=[b_t1, b_t2], w=[hsbuf])
                pend_tr.append((th, hs, hsbuf))
                if th > 0:
                    emit_hs_transpose(*pend_tr.pop(0))
            for hd in range(4):
                wi = hd % 2
                wdv = wbuf.ap().rearrange("p (b k f) -> p b k f", b=2, k=4)
                dma("sp", wdv[:, wi, :, :],
                    d["w_down"].ap().rearrange("(k p) e -> p k e", p=128)
                    [:, 4 * isl:4 * isl + 4, 512 * hd:512 * (hd + 1)],
                    w=[b_wb[wi]])
                for th in range(CPT):
                    if hd == 0 and th == 1 and pend_tr:
                        emit_hs_transpose(*pend_tr.pop(0))
                    pbk = 3 + (th % 2)
                    for ic in range(4):
                        S_.op("pe", lambda e, ic=ic, th=th, wi=wi, pbk=pbk,
                              wdv=wdv: e.matmul(
                                  ps[pbk][0:128, 0:512],
                                  hsx[:, 512 * th + 128 * ic:
                                      512 * th + 128 * (ic + 1)],
                                  wdv[:, wi, ic, :],
                                  start=(ic == 0), stop=(ic == 3)),
                              r=[b_hst, b_wb[wi]],
                              w=[b_ps[pbk]] if ic in (0, 3) else ())
                    gsl = gacc[:, DM * th + 512 * hd:DM * th + 512 * (hd + 1)]
                    if isl == 0:
                        S_.op("dve", lambda e, gsl=gsl, pbk=pbk: e.tensor_copy(
                            gsl, ps[pbk][0:128, 0:512]),
                            r=[b_ps[pbk]], w=[b_g[th]])
                    else:
                        S_.op("dve", lambda e, gsl=gsl, pbk=pbk:
                              e.tensor_tensor(gsl, ps[pbk][0:128, 0:512],
                                              gsl, ALU.add),
                              r=[b_ps[pbk], b_g[th]], w=[b_g[th]])

        S_.retag([b_hs[0], b_hs[1]], [b_t1, b_t2])
        S_.retag([b_slB], [b_pb[s][i] for s in range(2) for i in range(3)]
                 + [b_xbc[s][i] for s in range(2) for i in range(3)]
                 + b_acum + b_acumf + b_alast + b_ds + b_dcb + b_expa
                 + [b_aw[0], b_aw[1]])
        S_.retag([b_hst], [b_sst, b_sst2])
        if t == 0:
            halo_ln_aprep()
            S_.retag([b_hsc], [b_sst])
        for c in range(CPT):
            a_prep(c, lambda k, c=c:
                   xnf[:, TT * k + 128 * c:TT * k + 128 * (c + 1)], [b_xn[c]])
        def mamba_inproj(hp):
            st = hp % 2
            for which, e0 in ((0, 128 * hp), (1, 2048 * which if False else
                              2048 + 128 * hp), (2, 4096 + 128 * hp)):
                bi = (3 * hp + which) % 2
                load_wblk(bi, e0)
                pbank = which % 2
                if t == 0:
                    halo_proj(hp, which, bi)
                inproj_block(bi, pbank, 512,
                             lambda k: xnf[:, TT * k:TT * k + 512],
                             list(b_xn))
                gb = 16 * which + hp
                pbs = slabB[:, PBO(st, which):PBO(st, which) + 516]
                S_.op("dve", lambda e, pbs=pbs, gb=gb: e.tensor_copy(
                    pbs[:, 0:3], tails[:, 3 * gb:3 * gb + 3]),
                    r=[b_tails[gb]], w=[b_pb[st][which]])
                S_.op("act", lambda e, pbs=pbs, pbank=pbank: e.copy(
                    pbs[:, 3:515], ps[pbank][0:128, 0:512]),
                    r=[b_ps[pbank]], w=[b_pb[st][which]])
                S_.op("dve", lambda e, pbs=pbs, gb=gb: e.tensor_copy(
                    tails[:, 3 * gb:3 * gb + 3], pbs[:, 512:515]),
                    r=[b_pb[st][which]], w=[b_tails[gb]])
                conv_chain(st, which, gb, 512, off_in=0, off_out=0)

        def ssd_block(hp, next_first=False):
            st = hp % 2
            if hp == 0:
                for f in pre_stages(0):
                    f()
            for c in range(CPT):
                if c == 0:
                    mops = [souter(hp, 0, 1), souter(hp, 1, 1),
                            x_stage(hp, 0, 1), x_stage(hp, 1, 1)]
                elif c == 1:
                    mops = [souter(hp, 0, 2), souter(hp, 1, 2)]
                elif c == 2:
                    mops = [souter(hp, 0, 3), souter(hp, 1, 3)]
                    if next_first:
                        mops += [x_stage(hp + 1, 0, 0),
                                 x_stage(hp + 1, 1, 0)]
                elif next_first:
                    mops = [souter(hp + 1, 0, 0), souter(hp + 1, 1, 0)]
                else:
                    mops = ()
                ssd_chunk(hp, c, 128 * c, st=st, mset=c % 2, mops=mops)

        for hp in range(NHP):
            mamba_inproj(hp)
            if t == 0:
                halo_state(hp)
            if hp > 0:
                ssd_block(hp - 1, next_first=True)
        ssd_block(NHP - 1)

        S_.retag([b_pb[s][i] for s in range(2) for i in range(3)]
                 + [b_xbc[s][i] for s in range(2) for i in range(3)]
                 + b_acum + b_acumf + b_alast + b_ds + b_dcb + b_expa
                 + [b_aw[0], b_aw[1]],
                 [b_zf[0], b_zf[1], b_zf[2], b_zf[3]])
        S_.retag([b_sst, b_sst2, b_hsc, b_hxb[0], b_hxb[1], b_hssd,
                  b_hac, b_hds, b_hal], [b_zb])
        dma("sp", tmp3[0:1, 0:DM],
            d["z_bias"].ap().rearrange("(o n) -> o n", o=1), w=[b_t3])
        for j in range(4):
            pj = 6 + (j % 2)
            S_.op("pe", lambda e, j=j, pj=pj: e.matmul(
                ps[pj][0:128, 0:512], ones[0:1, :],
                tmp3[0:1, 512 * j:512 * (j + 1)], start=True, stop=True),
                r=[b_const, b_t3, b_ps[pj]], w=[b_ps[pj]])
            S_.op("dve", lambda e, j=j, pj=pj: e.tensor_copy(
                hsx[:, 512 * j:512 * (j + 1)], ps[pj][0:128, 0:512]),
                r=[b_ps[pj]], w=[b_zb])
        S_.retag([b_slA], [b_slA2[0], b_slA2[1]])
        for zs in range(4):
            for hh2 in range(2):
                dma("sp", slAv[:, 8 * hh2:8 * hh2 + 8, :],
                    d["w_int"].ap().rearrange("(k p) e -> p k e", p=128)
                    [:, 8 * hh2:8 * hh2 + 8,
                     6144 + 512 * zs:6144 + 512 * (zs + 1)], w=[b_slA2[hh2]])
            for c in range(CPT):
                pbk = c % 2
                for k in range(NKD):
                    S_.op("pe", lambda e, k=k, c=c, pbk=pbk: e.matmul(
                        ps[pbk][0:128, 0:512],
                        xnf[:, TT * k + 128 * c:TT * k + 128 * (c + 1)],
                        slabA[:, 512 * k:512 * (k + 1)],
                        start=(k == 0), stop=(k == NKD - 1)),
                        r=[b_xn[c], b_slA2[k // 8]],
                        w=[b_ps[pbk]] if k in (0, NKD - 1) else ())
                S_.op("dve", lambda e, zs=zs, c=c, pbk=pbk: e.tensor_tensor(
                    tmp3[:, 512 * (c % 4):512 * (c % 4) + 512],
                    ps[pbk][0:128, 0:512],
                    hsx[:, 512 * zs:512 * (zs + 1)], ALU.add),
                    r=[b_ps[pbk], b_zb], w=[b_t3])
                S_.op("act", lambda e, zs=zs, c=c: e.activation(
                    slabB[:, DM * c + 512 * zs:DM * c + 512 * (zs + 1)],
                    tmp3[:, 512 * (c % 4):512 * (c % 4) + 512], AF.Silu),
                    r=[b_t3], w=[b_zf[c]])

        S_.retag([b_zb], [b_rmw])
        dma("sp", tmp3[0:1, 0:DM],
            d["rms_w"].ap().rearrange("(o n) -> o n", o=1), w=[b_t3])
        for j in range(4):
            pj = 6 + (j % 2)
            S_.op("pe", lambda e, j=j, pj=pj: e.matmul(
                ps[pj][0:128, 0:512], ones[0:1, :],
                tmp3[0:1, 512 * j:512 * (j + 1)], start=True, stop=True),
                r=[b_const, b_t3, b_ps[pj]], w=[b_ps[pj]])
            S_.op("dve", lambda e, j=j, pj=pj: e.tensor_copy(
                hsx[:, 512 * j:512 * (j + 1)], ps[pj][0:128, 0:512]),
                r=[b_ps[pj]], w=[b_rmw])

        S_.retag(list(b_xn), list(b_ynf))

        def gating_tr(c):
            for k in range(NKD):
                pk = 6 + (k % 2)
                S_.op("pe", lambda e, k=k, pk=pk: e.transpose(
                    ps[pk][0:128, 0:128], tmp3[:, 128 * k:128 * (k + 1)],
                    ident[:, :]), r=[b_t3, b_const, b_ps[pk]], w=[b_ps[pk]])
                dst = xnf[:, TT * k + 128 * c:TT * k + 128 * (c + 1)]
                if k % 2 == 0:
                    S_.op("act", lambda e, dst=dst, pk=pk: e.copy(
                        dst, ps[pk][0:128, 0:128]), r=[b_ps[pk]], w=[b_ynf[c]])
                else:
                    S_.op("dve", lambda e, dst=dst, pk=pk: e.tensor_copy(
                        dst, ps[pk][0:128, 0:128]), r=[b_ps[pk]], w=[b_ynf[c]])

        def gating(c):
            S_.op("dve", lambda e, c=c: e.tensor_tensor(
                ytok[:, DM * c:DM * (c + 1)], slabB[:, DM * c:DM * (c + 1)],
                ytok[:, DM * c:DM * (c + 1)], ALU.mult),
                r=[b_zf[c], b_y[c]], w=[b_y[c]])
            S_.op("act", lambda e, c=c: e.activation(
                tmp3[:, :], ytok[:, DM * c:DM * (c + 1)], AF.Square,
                accum_out=stats[:, 1:2]), r=[b_y[c]], w=[b_t3, b_stats])
            S_.op("dve", lambda e: e.tensor_scalar(
                stats[:, 7:8], stats[:, 1:2], 1.0 / DM, 1e-5,
                ALU.mult, ALU.add), r=[b_stats], w=[b_stats])
            S_.op("act", lambda e: e.activation(
                stats[:, 5:6], stats[:, 7:8], AF.Sqrt),
                r=[b_stats], w=[b_stats])
            S_.op("dve", lambda e: e.reciprocal(stats[:, 6:7], stats[:, 5:6]),
                  r=[b_stats], w=[b_stats])
            S_.op("dve", lambda e, c=c: e.scalar_tensor_tensor(
                tmp3[:, :], ytok[:, DM * c:DM * (c + 1)], stats[:, 6:7],
                hsx[:, 0:DM], ALU.mult, ALU.mult),
                r=[b_y[c], b_stats, b_rmw], w=[b_t3])

        gating(0)
        gating_tr(0)
        S_.retag([b_y[c] for c in range(CPT)], [b_stk[c] for c in range(CPT)])
        for dsl in range(4):
            for hh2 in range(2):
                dma("sp", slAv[:, 8 * hh2:8 * hh2 + 8, :],
                    d["w_outt"].ap().rearrange("(k p) e -> p k e", p=128)
                    [:, 8 * hh2:8 * hh2 + 8, 512 * dsl:512 * (dsl + 1)],
                    w=[b_slA2[hh2]])
            for c in range(CPT):
                pbk = c % 2
                if dsl == 0 and c + 1 < CPT:
                    gating(c + 1)
                for k in range(NKD):
                    S_.op("pe", lambda e, k=k, c=c, pbk=pbk: e.matmul(
                        ps[pbk][0:128, 0:512],
                        xnf[:, TT * k + 128 * c:TT * k + 128 * (c + 1)],
                        slabA[:, 512 * k:512 * (k + 1)],
                        start=(k == 0), stop=(k == NKD - 1)),
                        r=[b_ynf[c], b_slA2[k // 8]],
                        w=[b_ps[pbk]] if k in (0, NKD - 1) else ())
                S_.op("act", lambda e, c=c, dsl=dsl, pbk=pbk: e.copy(
                    ytok[:, DM * c + 512 * dsl:DM * c + 512 * (dsl + 1)],
                    ps[pbk][0:128, 0:512]), r=[b_ps[pbk]], w=[b_stk[c]])
                if dsl == 0 and c + 1 < CPT:
                    gating_tr(c + 1)
                if dsl == 2 and t + 1 < n_tiles:
                    if c == 0:
                        S_.retag([b_zf[cc] for cc in range(CPT)], b_lnus)
                        S_.retag([b_rmw], [b_lnx])
                    if c >= 2:
                        cc = c - 2
                        ln_load(cc, d["u_own"].ap()
                                [TT * (t + 1) + 128 * cc:
                                 TT * (t + 1) + 128 * (cc + 1), :])
                if dsl == 3 and t + 1 < n_tiles:
                    emit_ln_chunk(t + 1, c, preloaded=True)
                    if c + 2 < CPT:
                        cc = c + 2
                        ln_load(cc, d["u_own"].ap()
                                [TT * (t + 1) + 128 * cc:
                                 TT * (t + 1) + 128 * (cc + 1), :])

        S_.retag([b_hxn, b_hpb[0], b_hpb[1], b_t1, b_t2], [b_t1, b_t2])
        for c in range(CPT):
            dma("act", tmp1[:, :],
                d["u_own"].ap()[TT * t + 128 * c:TT * t + 128 * (c + 1), :],
                w=[b_t1])
            S_.op("act", lambda e, c=c: e.activation(
                tmp3[:, :], gacc[:, DM * c:DM * (c + 1)], AF.Silu),
                r=[b_g[c]], w=[b_t3])
            S_.op("dve", lambda e, c=c: e.tensor_tensor(
                tmp2[:, :], tmp3[:, :], ytok[:, DM * c:DM * (c + 1)],
                ALU.mult), r=[b_t3, b_stk[c]], w=[b_t2])
            S_.op("dve", lambda e: e.tensor_tensor(
                tmp2[:, :], tmp1[:, :], tmp2[:, :], ALU.add),
                r=[b_t1, b_t2], w=[b_t2])
            od = dma("act",
                     d["out"].ap()[TT * t + 128 * c:TT * t + 128 * (c + 1), :],
                     tmp2[:, :], r=[b_t2])
            out_dmas.append(od)

    blk = es.enter_context(nc.Block())
    S_.finalize(blk, sem, qsem, out_dmas)
    es.close()
    return nc


_CACHE = {}


def _consts():
    c = {}
    c["ident"] = np.eye(128, dtype=np.float32)
    c["negtriu"] = -np.triu(np.ones((128, 128), dtype=np.float32))
    c["triladd"] = np.where(np.triu(np.ones((128, 128), dtype=bool)),
                            np.float32(0.0),
                            np.float32(-1e30)).astype(np.float32)
    s127 = np.zeros((128, 1), dtype=np.float32)
    s127[127, 0] = 1.0
    c["sel127"] = s127
    c["ones_row"] = np.ones((1, 128), dtype=np.float32)
    return c


def make_in_maps(u, w_int, w_outt, w_gate, w_up, w_down, conv_w, conv_b,
                 z_bias, rms_w, dvec, n_cores, tok):
    cns = _consts()
    in_maps = []
    for i in range(n_cores):
        m = {
            "u_own": np.ascontiguousarray(u[tok * i:tok * (i + 1)]),
            "u_halo": (np.zeros((256, DM), dtype=np.float32) if i == 0
                       else np.ascontiguousarray(u[tok * i - 256:tok * i])),
            "flag": (np.zeros((128, 1), dtype=np.float32) if i == 0
                     else np.ones((128, 1), dtype=np.float32)),
            "w_int": w_int, "w_outt": w_outt, "w_gate": w_gate,
            "w_up": w_up, "w_down": w_down,
            "conv_w": np.ascontiguousarray(
                conv_w.reshape(48, 128, 4).transpose(1, 0, 2).reshape(128, 192)),
            "conv_b": np.ascontiguousarray(conv_b.reshape(48, 128).T),
            "z_bias": z_bias, "rms_w": rms_w, "dvec": dvec,
        }
        m.update(cns)
        in_maps.append(m)
    return in_maps


def kernel(L0_attn_u, L0_attn_W_in, L0_attn_z_bias, L0_attn_conv_w,
           L0_attn_conv_b, L0_attn_D, L0_attn_rms_w, L0_attn_W_out,
           L0_ffn_w_gate, L0_ffn_w_up, L0_ffn_w_down):
    u = np.ascontiguousarray(np.asarray(L0_attn_u, dtype=np.float32)[0])
    w_int = np.ascontiguousarray(np.asarray(L0_attn_W_in, dtype=np.float32).T)
    w_outt = np.ascontiguousarray(np.asarray(L0_attn_W_out,
                                             dtype=np.float32).T)
    args = (u, w_int, w_outt,
            np.ascontiguousarray(np.asarray(L0_ffn_w_gate, dtype=np.float32)),
            np.ascontiguousarray(np.asarray(L0_ffn_w_up, dtype=np.float32)),
            np.ascontiguousarray(np.asarray(L0_ffn_w_down, dtype=np.float32)),
            np.ascontiguousarray(np.asarray(L0_attn_conv_w, dtype=np.float32)),
            np.ascontiguousarray(np.asarray(L0_attn_conv_b, dtype=np.float32)),
            np.ascontiguousarray(np.asarray(L0_attn_z_bias, dtype=np.float32)),
            np.ascontiguousarray(np.asarray(L0_attn_rms_w, dtype=np.float32)),
            np.ascontiguousarray(np.asarray(L0_attn_D, dtype=np.float32)))
    if "nc" not in _CACHE:
        _CACHE["nc"] = build_program(n_tiles=4)
    nc = _CACHE["nc"]
    TOK = 16384 // NCORES
    in_maps = make_in_maps(*args, NCORES, TOK)
    res = run_bass_kernel_spmd(nc, in_maps, core_ids=list(range(NCORES)))
    out = np.concatenate([np.asarray(r["out"]) for r in res.results], axis=0)
    return np.ascontiguousarray(out[None].astype(np.float32))
